# version 07
# baseline (speedup 1.0000x reference)
import numpy as np
import concourse.bass as bass
import concourse.mybir as mybir
from contextlib import ExitStack

F32 = mybir.dt.float32
BF16 = mybir.dt.bfloat16
AF = mybir.ActivationFunctionType
ALU = mybir.AluOpType

T = 8192
D = 1024
NCH = 8
FT = 512
NT = T // FT
NQ = 512
NP = T // NQ
KV = 128
SEG = 16
EPS = 1e-5
ISQ = 1.0 / 32.0


def tiles_in_panel(p):
    return 4 * p + 4


def tiles_before_panel(p):
    return 2 * p * p + 2 * p


def segs_in_panel(p):
    return (tiles_in_panel(p) + SEG - 1) // SEG


def segs_before_panel(p):
    return sum(segs_in_panel(q) for q in range(p))


TOTAL_KV = tiles_before_panel(NP)


LASTINC = {}
KEEPALIVE = []


class Sem:
    def __init__(self, nc, E, name):
        self.h = E(nc.semaphore(name))
        self.name = name
        self.count = 0
        self.hw = {}

    def inc(self, inst, n=1):
        inst.then_inc(self.h, n)
        self.count += n
        KEEPALIVE.append(inst)
        LASTINC[id(inst)] = (self, self.count)

    def wait(self, eng, ename, val):
        if val <= 0:
            return
        if self.hw.get(ename, 0) >= val:
            return
        self.hw[ename] = val
        eng.wait_ge(self.h, val)


class PhaseSem:

    def __init__(self, s):
        self.s = s
        self.base = s.count
        self.name = s.name

    def inc(self, inst, n=1):
        self.s.inc(inst, n)

    @property
    def count(self):
        return self.s.count - self.base

    def wait(self, eng, ename, val):
        if val <= 0:
            return
        self.s.wait(eng, ename, self.base + val)


class SemPool:
    def __init__(self, nc, E):
        self.nc = nc
        self.E = E
        self.pool = []
        self.idx = 0

    def phase_reset(self):
        self.idx = 0

    def get(self, name=None):
        if self.idx == len(self.pool):
            self.pool.append(Sem(self.nc, self.E, f"pool{self.idx}"))
        s = self.pool[self.idx]
        self.idx += 1
        return PhaseSem(s)


class Chain:

    def __init__(self, sem, name):
        self.sem = sem
        self.name = name
        self.last = None
        self.pending = None

    def before(self, eng):
        if self.pending is not None:
            inst = self.pending
            li = LASTINC.get(id(inst))
            if li is None:
                self.sem.inc(inst, 1)
                li = (self.sem, self.sem.count)
            self.last = li
            self.pending = None
        if self.last is not None:
            sem, val = self.last
            sem.wait(eng, self.name, val)

    def after(self, inst):
        self.pending = inst

    def flush(self, eng):
        self.before(eng)


class SemGroup:

    def __init__(self, nc, E, name, n, per=16):
        self.sems = [Sem(nc, E, f"{name}{k}") for k in range(n)]
        self.n = n
        self.per = per

    def inc(self, inst, idx):
        self.sems[idx % self.n].inc(inst, self.per)

    def wait_done(self, eng, en, idx):
        if idx < 0:
            return
        self.sems[idx % self.n].wait(eng, en, self.per * (idx // self.n + 1))

    def wait_all(self, eng, en, total):
        for idx in range(max(0, total - self.n), total):
            self.wait_done(eng, en, idx)


COMPUTE_OPS = {"tensor_tensor", "tensor_scalar", "tensor_copy", "tensor_reduce",
               "memset", "reciprocal", "activation", "copy", "add", "mul", "iota",
               "reduce_sum", "reduce_max", "select", "bn_stats"}


class CE:

    def __init__(self, eng, chain):
        self._e = eng
        self._c = chain

    def __getattr__(self, a):
        f = getattr(self._e, a)
        if a in COMPUTE_OPS:
            def g(*args, **kw):
                self._c.before(self._e)
                inst = f(*args, **kw)
                KEEPALIVE.append(inst)
                self._c.after(inst)
                return inst
            return g
        return f



def prep_common(I):
    C = {}
    C["w0rT"] = np.ascontiguousarray(I["L0_ffn_W_r"].T)
    C["w0kT"] = np.ascontiguousarray(I["L0_ffn_W_k"].T)
    C["w0vT"] = np.ascontiguousarray(I["L0_ffn_W_v"].T)
    C["wq0"] = np.ascontiguousarray(I["L0_attn_Wq"])
    C["wk0"] = np.ascontiguousarray(I["L0_attn_Wk"])
    C["wv0T"] = np.ascontiguousarray(I["L0_attn_Wv"].T)
    C["mur"] = np.ascontiguousarray(np.asarray(I["L0_ffn_mu_r"]).reshape(NCH, 128).T)
    C["muk"] = np.ascontiguousarray(np.asarray(I["L0_ffn_mu_k"]).reshape(NCH, 128).T)
    C["w1qT"] = np.ascontiguousarray(I["L1_attn_Wq"].T)
    C["w1kT"] = np.ascontiguousarray(I["L1_attn_Wk"].T)
    C["w1vT"] = np.ascontiguousarray(I["L1_attn_Wv"].T)
    C["b1q"] = np.ascontiguousarray(np.asarray(I["L1_attn_bq"]).reshape(NCH, 128).T)
    C["b1k"] = np.ascontiguousarray(np.asarray(I["L1_attn_bk"]).reshape(NCH, 128).T)
    C["b1vrow"] = np.ascontiguousarray(np.asarray(I["L1_attn_bv"]).reshape(1, D))
    C["w1in"] = np.ascontiguousarray(I["L1_ffn_Win"])
    C["b1in"] = np.ascontiguousarray(np.asarray(I["L1_ffn_bin_"]).reshape(64, 1))
    W1 = np.asarray(I["L1_ffn_W1"])
    bd1 = np.zeros((64, 1024), np.float32)
    for m in range(64):
        bd1[m, 16 * m:16 * m + 16] = W1[m]
    C["bd1"] = bd1
    C["b1f"] = np.ascontiguousarray(
        np.asarray(I["L1_ffn_b1"]).reshape(64 * 16).reshape(NCH, 128).T)
    W2 = np.asarray(I["L1_ffn_W2"])
    bd2 = np.zeros((1024, 64), np.float32)
    for m in range(64):
        bd2[16 * m:16 * m + 16, m] = W2[m]
    C["bd2"] = bd2
    C["b1b2"] = np.ascontiguousarray(np.asarray(I["L1_ffn_b2"]).reshape(64, 1))
    C["w1out"] = np.ascontiguousarray(I["L1_ffn_Wout"])
    C["b1out"] = np.ascontiguousarray(np.asarray(I["L1_ffn_bout"]).reshape(NCH, 128).T)
    hselred = np.zeros((128, NCH * 16), np.float32)
    for c in range(NCH):
        for p in range(128):
            hselred[p, c * 16 + 2 * c + p // 64] = 1.0
    C["hselred"] = hselred
    hselbc = np.zeros((16, NCH * 128), np.float32)
    for c in range(NCH):
        for p in range(128):
            hselbc[2 * c + p // 64, c * 128 + p] = 1.0
    C["hselbc"] = hselbc
    tk = np.arange(128)[:, None]
    tq = np.arange(512)[None, :]
    masks = np.concatenate([(tq >= tk + 128 * j).astype(np.float32) for j in range(4)],
                           axis=1)
    C["masks"] = np.ascontiguousarray(masks)
    return {k: np.asarray(v, np.float32) for k, v in C.items()}


def prep_percore(I):
    x = np.asarray(I["L0_attn_x"])
    return [np.ascontiguousarray(x[c].T) for c in range(8)]


PHASE_SETS = {"A1": ("A1",), "A2": ("A2",), "B": ("B",), "FUSED": ("A1", "A2", "B")}

DRAM_SPECS = {
    "xT": ((D, T), F32, None, ("A1", "A2")),
    "w0rT": ((D, D), F32, None, ("A1",)),
    "w0kT": ((D, D), F32, None, ("A1",)),
    "w0vT": ((D, D), F32, None, ("A1",)),
    "wq0": ((64, 64), F32, None, ("A1",)),
    "wk0": ((64, 64), F32, None, ("A1",)),
    "wv0T": ((64, 64), F32, None, ("A1",)),
    "mur": ((128, NCH), F32, None, ("A1",)),
    "muk": ((128, NCH), F32, None, ("A1",)),
    "hselred": ((128, NCH * 16), F32, None, ("A1",)),
    "g0T": ((D, T), BF16, "A1", ("A2",)),
    "s0T": ((16, T), F32, "A1", ("A2",)),
    "vsumH": ((128, NCH), F32, "A1", ("A2",)),
    "vsumT": ((1, NCH * 128), F32, "A1", ("A2",)),
    "hselbc": ((16, NCH * 128), F32, None, ("A2",)),
    "w1qT": ((D, D), F32, None, ("A2",)),
    "w1kT": ((D, D), F32, None, ("A2",)),
    "w1vT": ((D, D), F32, None, ("A2",)),
    "b1q": ((128, NCH), F32, None, ("A2",)),
    "b1k": ((128, NCH), F32, None, ("A2",)),
    "b1vrow": ((1, D), F32, None, ("A2",)),
    "w1in": ((D, 64), F32, None, ("A2",)),
    "b1in": ((64, 1), F32, None, ("A2",)),
    "x1T": ((D, T), F32, "A2", ("B",)),
    "qT": ((D, T), BF16, "A2", ("B",)),
    "kT": ((D, T), BF16, "A2", ("B",)),
    "v1": ((T, D), BF16, "A2", ("B",)),
    "gffT": ((D, T), BF16, "A2", ("B",)),
    "masks": ((128, 4 * 512), F32, None, ("B",)),
    "bd1": ((64, 1024), F32, None, ("A2",)),
    "b1f": ((128, NCH), F32, None, ("A2",)),
    "bd2": ((1024, 64), F32, None, ("A2",)),
    "b1b2": ((64, 1), F32, None, ("A2",)),
    "w1out": ((64, 1024), F32, None, ("A2",)),
    "b1out": ((128, NCH), F32, None, ("A2",)),
    "yT": ((D, T), F32, "B", ()),
}


def chunked_load(eng, dst_sb, dst_off, src_dram, col0, ncol, nch=NCH):
    src = src_dram.ap()[:, col0:col0 + ncol].rearrange("(c p) n -> p c n", p=128)
    dst = dst_sb[:, dst_off:dst_off + nch * ncol].rearrange("p (c n) -> p c n", c=nch)
    return eng.dma_start(out=dst, in_=src)


def chunked_store(eng, dst_dram, col0, ncol, src_sb, src_off, nch=NCH):
    dst = dst_dram.ap()[:, col0:col0 + ncol].rearrange("(c p) n -> p c n", p=128)
    src = src_sb[:, src_off:src_off + nch * ncol].rearrange("p (c n) -> p c n", c=nch)
    return eng.dma_start(out=dst, in_=src)


def make_program(mode):
    phases = PHASE_SETS[mode]
    nc = bass.Bass("TRN2", target_bir_lowering=False)
    dram = {}
    input_names = []
    for name, (shape, dt, prod, cons) in DRAM_SPECS.items():
        used_c = any(p in phases for p in cons)
        prod_here = prod in phases
        if prod is None:
            if used_c:
                dram[name] = nc.dram_tensor(name, list(shape), dt, kind="ExternalInput")
                input_names.append(name)
        elif name == "yT":
            if prod_here:
                dram[name] = nc.dram_tensor(name, list(shape), dt, kind="ExternalOutput")
        elif prod_here and used_c:
            dram[name] = nc.dram_tensor(name, list(shape), dt)
        elif prod_here:
            dram[name] = nc.dram_tensor(name, list(shape), dt, kind="ExternalOutput")
        elif used_c:
            dram[name] = nc.dram_tensor(name, list(shape), dt, kind="ExternalInput")
            input_names.append(name)
    if "A2" in phases:
        dram["cc_in"] = nc.dram_tensor("cc_in", [16, T], F32)
        dram["cc_out"] = nc.dram_tensor("cc_out", [16, T], F32)
        dram["probsH"] = nc.dram_tensor("probsH", [16, T], BF16)

    with ExitStack() as octx:
        E = octx.enter_context
        bar = Sem(nc, E, "bar")
        pool = SemPool(nc, E)
        state = {"gen": 0}

        def barrier_end(chains=None):
            with nc.Block() as blk:
                for nm, engf in (("sync", blk.sync), ("tensor", blk.tensor),
                                 ("scalar", blk.scalar), ("vector", blk.vector),
                                 ("gpsimd", blk.gpsimd)):
                    @engf
                    def _(eng, nm=nm):
                        if chains and nm in chains and chains[nm].last is not None:
                            sem, val = chains[nm].last
                            sem.wait(eng, nm, val)
                        bar.inc(eng.nop(), 1)
            state["gen"] += 5

        def bwait(eng, ename):
            bar.wait(eng, ename, state["gen"])

        for ph in phases:
            pool.phase_reset()
            if ph == "A1":
                chains = build_A1(nc, E, dram, bwait, pool)
            elif ph == "A2":
                chains = build_A2(nc, E, dram, bwait, barrier_end, pool)
            else:
                chains = build_B(nc, E, dram, bwait, pool)
            barrier_end(chains)
    return nc, input_names



def build_A1(nc, E, dram, bwait, pool):
    xT, g0T, s0T, vsumH = dram["xT"], dram["g0T"], dram["s0T"], dram["vsumH"]

    ctx = ExitStack()
    A = ctx.enter_context
    with ctx:
        xbuf = A(nc.sbuf_tensor("a1_xbuf", [128, 2 * NCH * FT], F32))
        xbf = A(nc.sbuf_tensor("a1_xbf", [128, NCH * FT], BF16))
        sqbf = A(nc.sbuf_tensor("a1_sqbf", [128, NCH * FT], BF16))
        xn = A(nc.sbuf_tensor("a1_xn", [128, 2 * NCH * 513], BF16))
        xr = A(nc.sbuf_tensor("a1_xr", [128, 2 * NCH * FT], BF16))
        xk = A(nc.sbuf_tensor("a1_xk", [128, 2 * NCH * FT], BF16))
        dscr = A(nc.sbuf_tensor("a1_d", [128, FT], BF16))
        escr = A(nc.sbuf_tensor("a1_e", [128, FT], BF16))
        tmpscr = A(nc.sbuf_tensor("a1_tmp", [128, FT], F32))
        r_sb = A(nc.sbuf_tensor("a1_r", [128, NCH * FT], BF16))
        relu_sb = A(nc.sbuf_tensor("a1_relu", [128, 2 * FT], BF16))
        kact = A(nc.sbuf_tensor("a1_kact", [128, NCH * FT], BF16))
        kvprod = A(nc.sbuf_tensor("a1_kvp", [128, 2 * FT], BF16))
        g0sb = A(nc.sbuf_tensor("a1_g0", [128, 2 * NCH * FT], BF16))
        qksb = A(nc.sbuf_tensor("a1_qk", [128, 2 * FT], BF16))
        vp = A(nc.sbuf_tensor("a1_vp", [128, NCH], F32))
        vsum = A(nc.sbuf_tensor("a1_vsum", [128, NCH], F32))
        s0row = A(nc.sbuf_tensor("a1_s0row", [16, 2 * FT], F32))
        mrow = A(nc.sbuf_tensor("a1_mrow", [1, FT], F32))
        exrow = A(nc.sbuf_tensor("a1_exrow", [1, FT], F32))
        vrow = A(nc.sbuf_tensor("a1_vrow", [1, FT], F32))
        rrow = A(nc.sbuf_tensor("a1_rrow", [1, FT], F32))
        srow = A(nc.sbuf_tensor("a1_srow", [1, FT], F32))
        bmrow = A(nc.sbuf_tensor("a1_bmrow", [128, 2 * FT], BF16))
        brrow = A(nc.sbuf_tensor("a1_brrow", [128, 2 * FT], BF16))
        ones128 = A(nc.sbuf_tensor("a1_ones128", [128, 1], BF16))
        onesrow = A(nc.sbuf_tensor("a1_onesrow", [1, 128], F32))
        w0r = A(nc.sbuf_tensor("a1_w0r", [128, NCH * D], BF16))
        w0k = A(nc.sbuf_tensor("a1_w0k", [128, NCH * D], BF16))
        w0v = A(nc.sbuf_tensor("a1_w0v", [128, NCH * D], BF16))
        wq0b = A(nc.sbuf_tensor("a1_wq0b", [64, 64], BF16))
        wk0b = A(nc.sbuf_tensor("a1_wk0b", [64, 64], BF16))
        wv0b = A(nc.sbuf_tensor("a1_wv0b", [64, 64], BF16))
        mscr = A(nc.sbuf_tensor("a1_mscr", [64, 64], BF16))
        Mbd = A(nc.sbuf_tensor("a1_Mbd", [128, 128], BF16))
        Vbd = A(nc.sbuf_tensor("a1_Vbd", [128, 128], BF16))
        murs = A(nc.sbuf_tensor("a1_murs", [128, NCH], F32))
        muks = A(nc.sbuf_tensor("a1_muks", [128, NCH], F32))
        hred = A(nc.sbuf_tensor("a1_hred", [128, NCH * 16], BF16))

        psLN1 = A(nc.psum_tensor("a1_psLN1", [128, 512], F32))
        psLN2 = A(nc.psum_tensor("a1_psLN2", [128, 512], F32))
        psF = [A(nc.psum_tensor(f"a1_psF{k}", [128, 512], F32)) for k in range(2)]
        psQ = [A(nc.psum_tensor(f"a1_psQ{k}", [128, 512], F32)) for k in range(2)]
        psS0 = A(nc.psum_tensor("a1_psS0", [16, 512], F32))
        psM = A(nc.psum_tensor("a1_psM", [64, 64], F32))

        s_ld = SemGroup(nc, E, "a1_ld", 2)
        s_wld = Sem(nc, E, "a1_wld")
        s_init = pool.get()
        s_mmM = pool.get()
        s_evM = pool.get()
        s_cast = pool.get()
        s_mmstat = pool.get()
        s_rowd = pool.get()
        s_sqr = pool.get()
        s_rowg = pool.get()
        s_bc = pool.get()
        s_bcevS = pool.get()
        s_xnLN = pool.get()
        s_mix = pool.get()
        s_mmR = pool.get()
        s_sig = pool.get()
        s_mmK = pool.get()
        s_rel = pool.get()
        s_kact = pool.get()
        s_mmV = pool.get()
        s_kvev = pool.get()
        s_silu = pool.get()
        s_rk = pool.get()
        s_mmU = pool.get()
        s_qk = pool.get()
        s_mmV0 = pool.get()
        s_mms0 = pool.get()
        s_vev = pool.get()
        s_vsadd = pool.get()
        s_xnfree = pool.get()
        s_s0ev = pool.get()
        s_g0st = SemGroup(nc, E, "a1_g0st", 2)
        s_s0st = SemGroup(nc, E, "a1_s0st", 2)
        s_fin = Sem(nc, E, "a1_fin")
        chains = {"vector": Chain(pool.get(), "a1_chv"), "scalar": Chain(pool.get(), "a1_chs"),
                  "gpsimd": Chain(pool.get(), "a1_chg")}

        def xb(i, c):
            o = ((i % 2) * NCH + c) * FT
            return xbuf[:, o:o + FT]

        def xns(i, c, a, b):
            o = ((i % 2) * NCH + c) * 513
            return xn[:, o + a:o + b]

        def xrs(i, c):
            o = ((i % 2) * NCH + c) * FT
            return xr[:, o:o + FT]

        def xks(i, c):
            o = ((i % 2) * NCH + c) * FT
            return xk[:, o:o + FT]

        def wsl(w, c, o):
            return w[:, c * D + o * 128:c * D + o * 128 + 128]

        def row(buf, i):
            if buf.shape[1] == FT:
                return buf[:, 0:FT]
            o = (i % 2) * FT
            return buf[:, o:o + FT]

        with nc.Block() as blk:
            @blk.gpsimd
            def _(eng):
                en = "gpsimd"
                eng = CE(eng, chains[en])
                bwait(eng, en)
                for nm, dst in (("wq0", wq0b), ("wk0", wk0b), ("wv0T", wv0b)):
                    s_wld.inc(eng.dma_start(out=dst[:, :], in_=dram[nm].ap()), 16)
                for nm, dst in (("mur", murs), ("muk", muks)):
                    s_wld.inc(eng.dma_start(out=dst[:, :], in_=dram[nm].ap()), 16)
                s_wld.inc(eng.dma_start(out=hred[:, :], in_=dram["hselred"].ap()), 16)
                for nm, dst in (("w0rT", w0r), ("w0kT", w0k), ("w0vT", w0v)):
                    src = dram[nm].ap().rearrange("(c p) n -> p c n", p=128)
                    d2 = dst[:, :].rearrange("p (c n) -> p c n", c=NCH)
                    s_wld.inc(eng.dma_start(out=d2, in_=src), 16)
                chains[en].flush(eng)

            @blk.vector
            def _(eng):
                en = "vector"
                eng = CE(eng, chains[en])
                bwait(eng, en)
                s_init.inc(eng.memset(Mbd[:, :], 0.0))
                s_init.inc(eng.memset(Vbd[:, :], 0.0))
                s_init.inc(eng.memset(ones128[:, :], 1.0))
                s_init.inc(eng.memset(onesrow[:, :], 1.0))
                s_init.inc(eng.memset(vp[:, :], 0.0))
                s_init.inc(eng.memset(vsum[:, :], 0.0))
                for c in range(NCH):
                    s_init.inc(eng.memset(xn[:, c * 513:c * 513 + 1], 0.0))
                s_evM.wait(eng, en, 1)
                s_init.inc(eng.tensor_copy(Mbd[0:64, 0:64], mscr[:, :]))
                s_init.inc(eng.tensor_copy(Mbd[64:128, 64:128], mscr[:, :]))
                s_wld.wait(eng, en, 144)
                s_init.inc(eng.tensor_copy(Vbd[0:64, 0:64], wv0b[:, :]))
                s_init.inc(eng.tensor_copy(Vbd[64:128, 64:128], wv0b[:, :]))

                def v_rows(i):
                    s_mmstat.wait(eng, en, 8 * (i + 1))
                    s_bc.wait(eng, en, i)
                    eng.tensor_scalar(row(mrow, i), psLN1[0:1, :], 1.0 / D, None, ALU.mult)
                    eng.tensor_scalar(row(exrow, i), psLN2[0:1, :], 1.0 / D, None, ALU.mult)
                    eng.tensor_tensor(row(vrow, i), row(mrow, i), row(mrow, i), ALU.mult)
                    eng.tensor_tensor(row(vrow, i), row(exrow, i), row(vrow, i), ALU.subtract)
                    inst = eng.tensor_scalar(row(vrow, i), row(vrow, i), EPS, None, ALU.add)
                    s_rowd.inc(inst)
                    s_sqr.wait(eng, en, i + 1)
                    inst = eng.reciprocal(row(rrow, i), srow[:, :])
                    s_rowg.inc(inst)

                def v_xn(i):
                    s_bcevS.wait(eng, en, i + 1)
                    s_xnfree.wait(eng, en, i - 1)
                    sl = (i % 2) * FT
                    for c in range(NCH):
                        eng.tensor_tensor(tmpscr[:, :], xb(i, c), bmrow[:, sl:sl + FT], ALU.subtract)
                        inst = eng.tensor_tensor(xns(i, c, 1, 513), tmpscr[:, :],
                                                 brrow[:, sl:sl + FT], ALU.mult)
                    s_xnLN.inc(inst)
                    if i + 1 < NT:
                        for c in range(NCH):
                            eng.tensor_copy(xns(i + 1, c, 0, 1), xns(i, c, 512, 513))

                def v_mix(i):
                    s_rk.wait(eng, en, i - 1)
                    for c in range(NCH):
                        eng.tensor_tensor(dscr[:, :], xns(i, c, 1, 513), xns(i, c, 0, 512), ALU.subtract)
                        eng.tensor_scalar(escr[:, :], dscr[:, :], murs[:, c:c + 1], None, ALU.mult)
                        eng.tensor_tensor(xrs(i, c), xns(i, c, 0, 512), escr[:, :], ALU.add)
                        eng.tensor_scalar(escr[:, :], dscr[:, :], muks[:, c:c + 1], None, ALU.mult)
                        inst = eng.tensor_tensor(xks(i, c), xns(i, c, 0, 512), escr[:, :], ALU.add)
                    s_mix.inc(inst)

                def v_vsum(i):
                    s_vev.wait(eng, en, 8 * (i + 1))
                    inst = eng.tensor_tensor(vsum[:, :], vsum[:, :], vp[:, :], ALU.add)
                    s_vsadd.inc(inst)

                v_rows(0)
                v_xn(0)
                v_mix(0)
                for i in range(NT):
                    if i + 1 < NT:
                        v_rows(i + 1)
                        v_xn(i + 1)
                    if i >= 1:
                        v_vsum(i - 1)
                    if i + 1 < NT:
                        v_mix(i + 1)
                    for o in range(NCH):
                        s_rel.wait(eng, en, 8 * i + o + 1)
                        inst = eng.tensor_tensor(kact[:, o * FT:(o + 1) * FT], psF[o % 2][:, :],
                                                 relu_sb[:, (o % 2) * FT:(o % 2 + 1) * FT], ALU.mult)
                        s_kact.inc(inst)
                    for o in range(NCH):
                        s_mmV.wait(eng, en, 8 * i + o + 1)
                        s_silu.wait(eng, en, 8 * i + o - 1)
                        inst = eng.tensor_tensor(kvprod[:, (o % 2) * FT:(o % 2 + 1) * FT],
                                                 psF[o % 2][:, :], r_sb[:, o * FT:(o + 1) * FT], ALU.mult)
                        s_kvev.inc(inst)
                    for c in range(NCH):
                        s_mmU.wait(eng, en, 8 * i + c + 1)
                        s_mms0.wait(eng, en, 8 * i + c - 1)
                        inst = eng.tensor_tensor(qksb[:, (c % 2) * FT:(c % 2 + 1) * FT],
                                                 xns(i, c, 1, 513), psQ[c % 2][:, :], ALU.mult)
                        s_qk.inc(inst)
                v_vsum(NT - 1)
                chains[en].flush(eng)

            @blk.scalar
            def _(eng):
                en = "scalar"
                eng = CE(eng, chains[en])
                bwait(eng, en)
                s_mmM.wait(eng, en, 1)
                s_evM.inc(eng.copy(mscr[:, :], psM[:, :]))

                def s_casts(j):
                    s_ld.wait_done(eng, en, j)
                    for c in range(NCH):
                        s_mmstat.wait(eng, en, 8 * j + c - 7)
                        sl = c * FT
                        s_cast.inc(eng.copy(xbf[:, sl:sl + FT], xb(j, c)))
                        s_cast.inc(eng.activation(out=sqbf[:, sl:sl + FT], in_=xb(j, c),
                                                  func=AF.Square))

                def s_sqrt(j):
                    s_rowd.wait(eng, en, j + 1)
                    s_sqr.inc(eng.activation(out=srow[:, :], in_=row(vrow, j), func=AF.Sqrt))

                def s_bcev(j):
                    s_bc.wait(eng, en, j + 1)
                    sl = (j % 2) * FT
                    eng.copy(bmrow[:, sl:sl + FT], psLN1[:, :])
                    inst = eng.copy(brrow[:, sl:sl + FT], psLN2[:, :])
                    s_bcevS.inc(inst)

                s_casts(0)
                s_sqrt(0)
                s_bcev(0)
                if NT > 1:
                    s_casts(1)
                for i in range(NT):
                    for o in range(NCH):
                        s_mmR.wait(eng, en, 8 * i + o + 1)
                        s_kvev.wait(eng, en, 8 * (i - 1) + o + 1)
                        inst = eng.activation(out=r_sb[:, o * FT:(o + 1) * FT], in_=psF[o % 2][:, :],
                                              func=AF.Sigmoid)
                        s_sig.inc(inst)
                    if i + 2 < NT:
                        s_casts(i + 2)
                    if i + 1 < NT:
                        s_sqrt(i + 1)
                        s_bcev(i + 1)
                    for o in range(NCH):
                        s_mmK.wait(eng, en, 8 * i + o + 1)
                        s_kact.wait(eng, en, 8 * i + o - 1)
                        inst = eng.activation(out=relu_sb[:, (o % 2) * FT:(o % 2 + 1) * FT],
                                              in_=psF[o % 2][:, :], func=AF.Relu)
                        s_rel.inc(inst)
                    for o in range(NCH):
                        s_kvev.wait(eng, en, 8 * i + o + 1)
                        s_g0st.wait_done(eng, en, i - 2)
                        dst = g0sb[:, ((i % 2) * NCH + o) * FT:((i % 2) * NCH + o + 1) * FT]
                        inst = eng.activation(out=dst, in_=kvprod[:, (o % 2) * FT:(o % 2 + 1) * FT],
                                              func=AF.Silu)
                        s_silu.inc(inst)
                    for c in range(NCH):
                        s_mmV0.wait(eng, en, 8 * i + c + 1)
                        s_vsadd.wait(eng, en, i)
                        inst = eng.activation(out=relu_sb[:, 0:FT], in_=psQ[c % 2][:, :],
                                              func=AF.Copy, accum_out=vp[:, c:c + 1])
                        s_vev.inc(inst)
                    s_mms0.wait(eng, en, 8 * (i + 1))
                    s_s0st.wait_done(eng, en, i - 2)
                    inst = eng.copy(row(s0row, i), psS0[:, :])
                    s_s0ev.inc(inst)
                chains[en].flush(eng)

            @blk.tensor
            def _(eng):
                en = "tensor"
                bwait(eng, en)
                s_wld.wait(eng, en, 144)
                inst = eng.matmul(psM[:, :], wk0b[:, :], wq0b[:, :], start=True, stop=True)
                s_mmM.inc(inst)
                s_init.wait(eng, en, 18)

                def t_stats(j):
                    s_bcevS.wait(eng, en, j)
                    for c in range(NCH):
                        s_cast.wait(eng, en, 16 * j + 2 * (c + 1))
                        sl = c * FT
                        eng.matmul(psLN1[0:1, :], ones128[:, :], xbf[:, sl:sl + FT],
                                   start=(c == 0), stop=(c == NCH - 1))
                        inst = eng.matmul(psLN2[0:1, :], ones128[:, :], sqbf[:, sl:sl + FT],
                                          start=(c == 0), stop=(c == NCH - 1))
                        s_mmstat.inc(inst)

                def t_bcast(j):
                    s_rowg.wait(eng, en, j + 1)
                    eng.matmul(psLN1[:, :], onesrow[:, :], row(mrow, j), start=True, stop=True)
                    inst = eng.matmul(psLN2[:, :], onesrow[:, :], row(rrow, j), start=True, stop=True)
                    s_bc.inc(inst)

                t_stats(0)
                t_bcast(0)
                for i in range(NT):
                    if i + 1 < NT:
                        t_stats(i + 1)
                    s_mix.wait(eng, en, i + 1)
                    for o in range(NCH):
                        if o < 2:
                            s_kvev.wait(eng, en, 8 * (i - 1) + o + 7)
                        else:
                            s_sig.wait(eng, en, 8 * i + o - 1)
                        for c in range(NCH):
                            inst = eng.matmul(psF[o % 2][:, :], wsl(w0r, c, o), xrs(i, c),
                                              start=(c == 0), stop=(c == NCH - 1))
                        s_mmR.inc(inst)
                    if i + 1 < NT:
                        t_bcast(i + 1)
                    for o in range(NCH):
                        if o == 0:
                            s_sig.wait(eng, en, 8 * i + 7)
                        elif o == 1:
                            s_sig.wait(eng, en, 8 * i + 8)
                        else:
                            s_kact.wait(eng, en, 8 * i + o - 1)
                        for c in range(NCH):
                            inst = eng.matmul(psF[o % 2][:, :], wsl(w0k, c, o), xks(i, c),
                                              start=(c == 0), stop=(c == NCH - 1))
                        s_mmK.inc(inst)
                        if o == NCH - 1:
                            s_rk.inc(eng.nop())
                    s_kact.wait(eng, en, 8 * (i + 1))
                    for o in range(NCH):
                        if o >= 2:
                            s_kvev.wait(eng, en, 8 * i + o - 1)
                        for c in range(NCH):
                            inst = eng.matmul(psF[o % 2][:, :], wsl(w0v, c, o),
                                              kact[:, c * FT:(c + 1) * FT],
                                              start=(c == 0), stop=(c == NCH - 1))
                        s_mmV.inc(inst)
                    def emit_s0red(c):
                        s_qk.wait(eng, en, 8 * i + c + 1)
                        if c == 0:
                            s_s0ev.wait(eng, en, i)
                        inst = eng.matmul(psS0[:, :], hred[:, c * 16:(c + 1) * 16],
                                          qksb[:, (c % 2) * FT:(c % 2 + 1) * FT],
                                          start=(c == 0), stop=(c == NCH - 1))
                        s_mms0.inc(inst)

                    for c in range(NCH):
                        if c < 2:
                            s_vev.wait(eng, en, 8 * i + c - 1)
                        else:
                            s_qk.wait(eng, en, 8 * i + c - 1)
                        inst = eng.matmul(psQ[c % 2][:, :], Mbd[:, :], xns(i, c, 1, 513),
                                          start=True, stop=True)
                        s_mmU.inc(inst)
                        if c >= 2:
                            emit_s0red(c - 2)
                    emit_s0red(NCH - 2)
                    emit_s0red(NCH - 1)
                    for c in range(NCH):
                        if c < 2:
                            s_qk.wait(eng, en, 8 * i + c + 7)
                        else:
                            s_vev.wait(eng, en, 8 * i + c - 1)
                        inst = eng.matmul(psQ[c % 2][:, :], Vbd[:, :], xns(i, c, 1, 513),
                                          start=True, stop=True)
                        s_mmV0.inc(inst)
                        if c == NCH - 1:
                            s_xnfree.inc(eng.nop())

            @blk.sync
            def _(eng):
                en = "sync"
                bwait(eng, en)
                for i in range(min(2, NT)):
                    s_ld.inc(chunked_load(eng, xbuf, (i % 2) * NCH * FT, xT, i * FT, FT), i)
                for i in range(NT):
                    if i + 2 < NT:
                        s_xnLN.wait(eng, en, i + 1)
                        s_ld.inc(chunked_load(eng, xbuf, (i % 2) * NCH * FT, xT, (i + 2) * FT, FT), i + 2)
                    s_silu.wait(eng, en, 8 * (i + 1))
                    s_g0st.inc(chunked_store(eng, g0T, i * FT, FT, g0sb, (i % 2) * NCH * FT), i)
                    s_s0ev.wait(eng, en, i + 1)
                    s_s0st.inc(eng.dma_start(out=s0T.ap()[:, i * FT:(i + 1) * FT],
                                             in_=row(s0row, i)), i)
                s_vsadd.wait(eng, en, NT)
                s_fin.inc(eng.dma_start(out=vsumH.ap(), in_=vsum[:, :]), 16)
                s_fin.inc(eng.dma_start(
                    out=dram["vsumT"].ap().rearrange("o (c p) -> (o p) c", p=128),
                    in_=vsum[:, :], allow_slow_non_contiguous=True), 16)
                s_fin.wait(eng, en, 32)
                s_g0st.wait_all(eng, en, NT)
                s_s0st.wait_all(eng, en, NT)

        assert s_mix.count == NT and s_mmstat.count == 8 * NT
        assert s_sig.count == 8 * NT and s_kact.count == 8 * NT
        return chains



def build_A2(nc, E, dram, bwait, barrier_end, pool):
    xT, g0T, s0T, vsumH = dram["xT"], dram["g0T"], dram["s0T"], dram["vsumH"]
    x1T, qT, kT, v1, gffT = dram["x1T"], dram["qT"], dram["kT"], dram["v1"], dram["gffT"]
    cc_in, cc_out = dram["cc_in"], dram["cc_out"]

    pctx = ExitStack()
    P = pctx.enter_context
    with pctx:
        vsumt = P(nc.sbuf_tensor("a2_vsumt", [1, NCH * 128], BF16))
        s_prob = pool.get()
        chains = {"vector": Chain(pool.get(), "a2_chv"), "scalar": Chain(pool.get(), "a2_chs"),
                  "gpsimd": Chain(pool.get(), "a2_chg")}

        cctx = ExitStack()
        C = cctx.enter_context
        with cctx:
            bufP = C(nc.sbuf_tensor("a2_bufP", [16, T], F32))
            bufQ = C(nc.sbuf_tensor("a2_bufQ", [16, T], F32))
            bufR = C(nc.sbuf_tensor("a2_bufR", [16, T], F32))
            probsb = C(nc.sbuf_tensor("a2_probsb", [16, T], BF16))
            s_cl = Sem(nc, E, "a2_cl")
            s_ex = pool.get()
            s_cc = Sem(nc, E, "a2_cc")
            with nc.Block() as blk:
                @blk.gpsimd
                def _(eng):
                    en = "gpsimd"
                    eng = CE(eng, chains[en])
                    bwait(eng, en)
                    s_cl.inc(eng.dma_start(out=vsumt[:, :], in_=dram["vsumT"].ap()), 16)
                    s_cl.inc(eng.dma_start(out=bufP[:, :], in_=s0T.ap()), 16)
                    s_ex.wait(eng, en, 1)
                    s_cl.inc(eng.dma_start(out=cc_in.ap(), in_=bufQ[:, :]), 16)
                    s_cl.wait(eng, en, 48)
                    inst = eng.collective_compute(
                        "AllReduce", ALU.add, replica_groups=[list(range(8))],
                        ins=[cc_in.ap().opt()], outs=[cc_out.ap().opt()])
                    s_cc.inc(inst, 1)
                    s_cc.wait(eng, en, 1)
                    s_cl.inc(eng.dma_start(out=bufR[:, :], in_=cc_out.ap()), 16)
                    s_prob.wait(eng, en, 1)
                    s_cl.inc(eng.dma_start(out=dram["probsH"].ap(), in_=probsb[:, :]), 16)
                    s_cl.wait(eng, en, 80)
                    chains[en].flush(eng)

                @blk.scalar
                def _(eng):
                    en = "scalar"
                    eng = CE(eng, chains[en])
                    bwait(eng, en)
                    s_cl.wait(eng, en, 32)
                    inst = eng.activation(out=bufQ[:, :], in_=bufP[:, :], func=AF.Exp, scale=ISQ)
                    s_ex.inc(inst)
                    chains[en].flush(eng)

                @blk.vector
                def _(eng):
                    en = "vector"
                    eng = CE(eng, chains[en])
                    bwait(eng, en)
                    s_cl.wait(eng, en, 64)
                    eng.reciprocal(bufP[:, :], bufR[:, :])
                    inst = eng.tensor_tensor(probsb[:, :], bufQ[:, :], bufP[:, :], ALU.mult)
                    s_prob.inc(inst)
                    chains[en].flush(eng)
            barrier_end(chains)

        ctx = ExitStack()
        A = ctx.enter_context
        with ctx:
            xbuf = A(nc.sbuf_tensor("a2_xbuf", [128, 2 * NCH * FT], F32))
            g0buf = A(nc.sbuf_tensor("a2_g0buf", [128, NCH * FT], BF16))
            x1buf = A(nc.sbuf_tensor("a2_x1buf", [128, 2 * NCH * FT], F32))
            x1bf = A(nc.sbuf_tensor("a2_x1bf", [128, 4 * FT], BF16))
            sqbf = A(nc.sbuf_tensor("a2_sqbf", [128, 4 * FT], BF16))
            xn1 = A(nc.sbuf_tensor("a2_xn1", [128, NCH * FT], BF16))
            vselw = A(nc.sbuf_tensor("a2_vselw", [16, NCH * 128], BF16))
            pbuf = A(nc.sbuf_tensor("a2_pbuf", [16, 2 * FT], BF16))
            tmpscr = A(nc.sbuf_tensor("a2_tmp", [128, FT], F32))
            t1scr = tmpscr
            qstage = A(nc.sbuf_tensor("a2_qstage", [128, NCH * FT], BF16))
            kstage = A(nc.sbuf_tensor("a2_kstage", [128, NCH * FT], BF16))
            vstage = A(nc.sbuf_tensor("a2_vstage", [128, 4 * D], BF16))
            s1stage = A(nc.sbuf_tensor("a2_s1stage", [64, 2 * FT], BF16))
            bd1s = A(nc.sbuf_tensor("a2_bd1s", [64, 1024], BF16))
            bd2s = A(nc.sbuf_tensor("a2_bd2s", [128, NCH * 64], BF16))
            w1os = A(nc.sbuf_tensor("a2_w1os", [64, 1024], BF16))
            b1fs = A(nc.sbuf_tensor("a2_b1fs", [128, NCH], F32))
            b1b2s = A(nc.sbuf_tensor("a2_b1b2s", [64, 1], F32))
            b1outs = A(nc.sbuf_tensor("a2_b1outs", [128, NCH], F32))
            hstage = A(nc.sbuf_tensor("a2_hstage", [128, 2 * FT], BF16))
            g1sb = A(nc.sbuf_tensor("a2_g1sb", [64, FT], BF16))
            sgscr = A(nc.sbuf_tensor("a2_sgscr", [128, 2 * FT], BF16))
            fscr = A(nc.sbuf_tensor("a2_fscr", [128, FT], F32))
            gstage = A(nc.sbuf_tensor("a2_gstage", [128, NCH * FT], BF16))
            mrow = A(nc.sbuf_tensor("a2_mrow", [1, FT], F32))
            exrow = A(nc.sbuf_tensor("a2_exrow", [1, FT], F32))
            vrow = A(nc.sbuf_tensor("a2_vrow", [1, FT], F32))
            rrow = A(nc.sbuf_tensor("a2_rrow", [1, FT], F32))
            srow = A(nc.sbuf_tensor("a2_srow", [1, FT], BF16))
            ones128 = A(nc.sbuf_tensor("a2_ones128", [128, 1], BF16))
            onesrow = A(nc.sbuf_tensor("a2_onesrow", [1, 128], F32))
            hbc = A(nc.sbuf_tensor("a2_hbc", [16, NCH * 128], BF16))
            w1q = A(nc.sbuf_tensor("a2_w1q", [128, NCH * D], BF16))
            w1k = A(nc.sbuf_tensor("a2_w1k", [128, NCH * D], BF16))
            w1v = A(nc.sbuf_tensor("a2_w1v", [128, NCH * D], BF16))
            w1i = A(nc.sbuf_tensor("a2_w1i", [128, NCH * 64], BF16))
            b1qs = A(nc.sbuf_tensor("a2_b1qs", [128, NCH], F32))
            b1ks = A(nc.sbuf_tensor("a2_b1ks", [128, NCH], F32))
            b1ins = A(nc.sbuf_tensor("a2_b1ins", [64, 1], F32))
            bvrow = A(nc.sbuf_tensor("a2_bvrow", [1, D], BF16))
            onesrowb = A(nc.sbuf_tensor("a2_onesrowb", [1, 128], BF16))
            bvbc = A(nc.sbuf_tensor("a2_bvbc", [128, D], BF16))

            psP = [A(nc.psum_tensor(f"a2_psP{k}", [128, 512], F32)) for k in range(2)]
            psLN1 = A(nc.psum_tensor("a2_psLN1", [128, 512], F32))
            psLN2 = A(nc.psum_tensor("a2_psLN2", [128, 512], F32))
            psF = [A(nc.psum_tensor(f"a2_psF{k}", [128, 512], F32)) for k in range(2)]
            psV = A(nc.psum_tensor("a2_psV", [128, 512], F32))
            psW1 = A(nc.psum_tensor("a2_psW1", [64, 512], F32))

            s_ld = SemGroup(nc, E, "a2_ld", 2, per=16)
            s_ldg = SemGroup(nc, E, "a2_ldg", 2, per=16)
            s_ldp = SemGroup(nc, E, "a2_ldp", 2, per=16)
            s_wld = Sem(nc, E, "a2_wld")
            s_init = pool.get()
            s_mmbv = pool.get()
            s_evbv = pool.get()
            s_mmP = pool.get()
            s_sa = pool.get()
            s_x1 = pool.get()
            s_cast = pool.get()
            s_mmstat = pool.get()
            s_rowd = pool.get()
            s_sqr = pool.get()
            s_rowg = pool.get()
            s_bc = pool.get()
            s_xn1 = pool.get()
            s_mmQ = pool.get()
            s_qev = pool.get()
            s_mmK = pool.get()
            s_kev = pool.get()
            s_mmV = pool.get()
            s_vev = pool.get()
            s_mmW1 = pool.get()
            s_s1ev = pool.get()
            s_mmvw = pool.get()
            s_vw = pool.get()
            s_x1st = SemGroup(nc, E, "a2_x1st", 2)
            s_qst = SemGroup(nc, E, "a2_qst", 2)
            s_kst = SemGroup(nc, E, "a2_kst", 2)
            s_vst = SemGroup(nc, E, "a2_vst", 2)
            s_gst = SemGroup(nc, E, "a2_gst", 2)
            s_mmE = pool.get()
            s_h = pool.get()
            s_mmG = pool.get()
            s_g1 = pool.get()
            s_mmW = pool.get()
            s_sg = pool.get()
            s_ep = pool.get()

            def xb(i, c):
                o = ((i % 2) * NCH + c) * FT
                return xbuf[:, o:o + FT]

            def g0s(i, c):
                return g0buf[:, c * FT:(c + 1) * FT]

            def x1s(i, c):
                o = ((i % 2) * NCH + c) * FT
                return x1buf[:, o:o + FT]

            def wsl(w, c, o):
                return w[:, c * D + o * 128:c * D + o * 128 + 128]

            def row(buf, i):
                if buf.shape[1] == FT:
                    return buf[:, 0:FT]
                o = (i % 2) * FT
                return buf[:, o:o + FT]

            with nc.Block() as blk:
                @blk.gpsimd
                def _(eng):
                    en = "gpsimd"
                    eng = CE(eng, chains[en])
                    bwait(eng, en)
                    s_wld.inc(eng.dma_start(out=hbc[:, :], in_=dram["hselbc"].ap()), 16)
                    for nm, dst in (("w1qT", w1q), ("w1kT", w1k), ("w1vT", w1v)):
                        src = dram[nm].ap().rearrange("(c p) n -> p c n", p=128)
                        d2 = dst[:, :].rearrange("p (c n) -> p c n", c=NCH)
                        s_wld.inc(eng.dma_start(out=d2, in_=src), 16)
                    src = dram["w1in"].ap().rearrange("(c p) n -> p c n", p=128)
                    d2 = w1i[:, :].rearrange("p (c n) -> p c n", c=NCH)
                    s_wld.inc(eng.dma_start(out=d2, in_=src), 16)
                    s_wld.inc(eng.dma_start(out=b1qs[:, :], in_=dram["b1q"].ap()), 16)
                    s_wld.inc(eng.dma_start(out=b1ks[:, :], in_=dram["b1k"].ap()), 16)
                    s_wld.inc(eng.dma_start(out=b1ins[:, :], in_=dram["b1in"].ap()), 16)
                    s_wld.inc(eng.dma_start(out=bvrow[:, :], in_=dram["b1vrow"].ap()), 16)
                    s_wld.inc(eng.dma_start(out=bd1s[:, :], in_=dram["bd1"].ap()), 16)
                    src2 = dram["bd2"].ap().rearrange("(c p) n -> p c n", p=128)
                    d2b = bd2s[:, :].rearrange("p (c n) -> p c n", c=NCH)
                    s_wld.inc(eng.dma_start(out=d2b, in_=src2), 16)
                    s_wld.inc(eng.dma_start(out=w1os[:, :], in_=dram["w1out"].ap()), 16)
                    s_wld.inc(eng.dma_start(out=b1fs[:, :], in_=dram["b1f"].ap()), 16)
                    s_wld.inc(eng.dma_start(out=b1b2s[:, :], in_=dram["b1b2"].ap()), 16)
                    s_wld.inc(eng.dma_start(out=b1outs[:, :], in_=dram["b1out"].ap()), 16)
                    chains[en].flush(eng)

                @blk.vector
                def _(eng):
                    en = "vector"
                    eng = CE(eng, chains[en])
                    bwait(eng, en)
                    s_init.inc(eng.memset(ones128[:, :], 1.0))
                    s_init.inc(eng.memset(onesrow[:, :], 1.0))
                    s_init.inc(eng.memset(onesrowb[:, :], 1.0))
                    for h in range(2):
                        s_mmbv.wait(eng, en, h + 1)
                        inst = eng.tensor_copy(bvbc[:, h * 512:(h + 1) * 512], psP[h][:, :])
                        s_evbv.inc(inst)
                    for c in range(NCH):
                        s_mmvw.wait(eng, en, c + 1)
                        inst = eng.tensor_tensor(vselw[:, c * 128:(c + 1) * 128],
                                                 hbc[:, c * 128:(c + 1) * 128],
                                                 psP[c % 2][0:16, 0:128], ALU.mult)
                        s_vw.inc(inst)

                    def v_x1chain(j):
                        s_ld.wait_done(eng, en, j)
                        s_ldg.wait_done(eng, en, j)
                        s_x1st.wait_done(eng, en, j - 2)
                        s_cast.wait(eng, en, 16 * (j - 1))
                        for c in range(NCH):
                            s_mmP.wait(eng, en, 8 * j + c + 1)
                            inst = eng.tensor_tensor(t1scr[:, :], psP[c % 2][:, :], g0s(j, c),
                                                     ALU.mult)
                            s_sa.inc(inst)
                            inst = eng.tensor_tensor(x1s(j, c), t1scr[:, :], xb(j, c), ALU.add)
                        s_x1.inc(inst)

                    def v_rows(j):
                        s_mmstat.wait(eng, en, 8 * (j + 1))
                        s_bc.wait(eng, en, j)
                        eng.tensor_scalar(row(mrow, j), psLN1[0:1, :], 1.0 / D, None, ALU.mult)
                        eng.tensor_scalar(row(exrow, j), psLN2[0:1, :], 1.0 / D, None, ALU.mult)
                        eng.tensor_tensor(row(vrow, j), row(mrow, j), row(mrow, j), ALU.mult)
                        eng.tensor_tensor(row(vrow, j), row(exrow, j), row(vrow, j), ALU.subtract)
                        inst = eng.tensor_scalar(row(vrow, j), row(vrow, j), EPS, None, ALU.add)
                        s_rowd.inc(inst)
                        s_sqr.wait(eng, en, j + 1)
                        inst = eng.reciprocal(row(rrow, j), srow[:, :])
                        s_rowg.inc(inst)

                    v_x1chain(0)
                    v_rows(0)
                    for i in range(NT):
                        s_bc.wait(eng, en, i + 1)
                        s_mmW1.wait(eng, en, i)
                        for c in range(NCH):
                            eng.tensor_tensor(tmpscr[:, :], x1s(i, c), psLN1[:, :], ALU.subtract)
                            inst = eng.tensor_tensor(xn1[:, c * FT:(c + 1) * FT], tmpscr[:, :],
                                                     psLN2[:, :], ALU.mult)
                        s_xn1.inc(inst)
                        if i + 1 < NT:
                            v_x1chain(i + 1)
                        s_vst.wait_done(eng, en, i - 1)
                        for g in range(4):
                            for eh in range(2):
                                n = 2 * g + eh
                                s_mmV.wait(eng, en, 8 * i + n + 1)
                                inst = eng.tensor_tensor(
                                    vstage[:, g * D + eh * 512:g * D + eh * 512 + 512],
                                    psV[:, :], bvbc[:, eh * 512:(eh + 1) * 512], ALU.add)
                                s_vev.inc(inst)
                        if i + 1 < NT:
                            v_rows(i + 1)
                        if i >= 1:
                            k = i - 1
                            s_gst.wait_done(eng, en, k - 1)
                            for c in range(NCH):
                                s_sg.wait(eng, en, 8 * k + c + 1)
                                eng.tensor_scalar(fscr[:, :], psP[0][:, :], b1outs[:, c:c + 1],
                                                  None, ALU.add)
                                inst = eng.tensor_tensor(gstage[:, c * FT:(c + 1) * FT],
                                                         fscr[:, :],
                                                         sgscr[:, (c % 2) * FT:(c % 2 + 1) * FT],
                                                         ALU.mult)
                                s_ep.inc(inst)
                        s_mmW1.wait(eng, en, i + 1)
                        s_mmE.wait(eng, en, 8 * (i - 1))
                        inst = eng.tensor_scalar(row(s1stage, i), psW1[:, :], b1ins[:, :],
                                                 None, ALU.add)
                        s_s1ev.inc(inst)
                    k = NT - 1
                    s_gst.wait_done(eng, en, k - 1)
                    for c in range(NCH):
                        s_sg.wait(eng, en, 8 * k + c + 1)
                        eng.tensor_scalar(fscr[:, :], psP[0][:, :], b1outs[:, c:c + 1],
                                          None, ALU.add)
                        inst = eng.tensor_tensor(gstage[:, c * FT:(c + 1) * FT], fscr[:, :],
                                                 sgscr[:, (c % 2) * FT:(c % 2 + 1) * FT],
                                                 ALU.mult)
                        s_ep.inc(inst)
                    chains[en].flush(eng)

                @blk.scalar
                def _(eng):
                    en = "scalar"
                    eng = CE(eng, chains[en])
                    bwait(eng, en)

                    def s_casts(j):
                        s_x1.wait(eng, en, j + 1)
                        for c in range(NCH):
                            s_mmstat.wait(eng, en, 8 * j + c - 3)
                            sl = (c % 4) * FT
                            s_cast.inc(eng.copy(x1bf[:, sl:sl + FT], x1s(j, c)))
                            s_cast.inc(eng.activation(out=sqbf[:, sl:sl + FT],
                                                      in_=x1s(j, c), func=AF.Square))

                    def s_sqrt(j):
                        s_rowd.wait(eng, en, j + 1)
                        s_sqr.inc(eng.activation(out=srow[:, :], in_=row(vrow, j), func=AF.Sqrt))

                    def s_ffn(k):
                        for c in range(NCH):
                            s_mmE.wait(eng, en, 8 * k + c + 1)
                            s_mmG.wait(eng, en, 8 * k + c - 1)
                            inst = eng.activation(out=hstage[:, (c % 2) * FT:(c % 2 + 1) * FT],
                                                  in_=psP[0][:, :], func=AF.Relu,
                                                  bias=b1fs[:, c:c + 1], scale=1.0)
                            s_h.inc(inst)
                        s_mmG.wait(eng, en, 8 * (k + 1))
                        s_mmW.wait(eng, en, 8 * k)
                        inst = eng.activation(out=g1sb[:, :], in_=psP[1][0:64, :], func=AF.Identity,
                                              bias=b1b2s[:, :], scale=1.0)
                        s_g1.inc(inst)

                    def s_sgs(k):
                        for c in range(NCH):
                            s_mmW.wait(eng, en, 8 * k + c + 1)
                            s_ep.wait(eng, en, 8 * k + c - 1)
                            inst = eng.activation(out=sgscr[:, (c % 2) * FT:(c % 2 + 1) * FT],
                                                  in_=psP[0][:, :], func=AF.Sigmoid,
                                                  bias=b1outs[:, c:c + 1], scale=1.0)
                            s_sg.inc(inst)

                    s_casts(0)
                    s_sqrt(0)
                    for i in range(NT):
                        for o in range(NCH):
                            s_mmQ.wait(eng, en, 8 * i + o + 1)
                            s_qst.wait_done(eng, en, i - 1)
                            inst = eng.activation(out=qstage[:, o * FT:(o + 1) * FT],
                                                  in_=psF[o % 2][:, :], func=AF.Identity,
                                                  bias=b1qs[:, o:o + 1], scale=1.0)
                            s_qev.inc(inst)
                        for o in range(NCH):
                            s_mmK.wait(eng, en, 8 * i + o + 1)
                            s_kst.wait_done(eng, en, i - 1)
                            inst = eng.activation(out=kstage[:, o * FT:(o + 1) * FT],
                                                  in_=psF[o % 2][:, :], func=AF.Identity,
                                                  bias=b1ks[:, o:o + 1], scale=1.0)
                            s_kev.inc(inst)
                        if i + 1 < NT:
                            s_casts(i + 1)
                        if i >= 1:
                            s_ffn(i - 1)
                        if i + 1 < NT:
                            s_sqrt(i + 1)
                        if i >= 1:
                            s_sgs(i - 1)
                    s_ffn(NT - 1)
                    s_sgs(NT - 1)
                    chains[en].flush(eng)

                @blk.tensor
                def _(eng):
                    en = "tensor"
                    bwait(eng, en)
                    s_wld.wait(eng, en, 16 * 15)
                    s_init.wait(eng, en, 3)
                    for h in range(2):
                        inst = eng.matmul(psP[h][:, :], onesrowb[:, :],
                                          bvrow[:, h * 512:(h + 1) * 512], start=True, stop=True)
                        s_mmbv.inc(inst)
                    for c in range(NCH):
                        if c < 2:
                            s_evbv.wait(eng, en, c + 1)
                        else:
                            s_vw.wait(eng, en, c - 1)
                        inst = eng.matmul(psP[c % 2][0:16, 0:128], onesrowb[0:1, 0:16],
                                          vsumt[0:1, c * 128:(c + 1) * 128], start=True, stop=True)
                        s_mmvw.inc(inst)

                    def t_bcastP(j):
                        s_ldp.wait_done(eng, en, j)
                        s_ep.wait(eng, en, 8 * (j - 2))
                        s_g1.wait(eng, en, j - 2)
                        for c in range(NCH):
                            if j == 0:
                                s_vw.wait(eng, en, 8)
                            s_sa.wait(eng, en, 8 * j + c - 1)
                            inst = eng.matmul(psP[c % 2][:, :], vselw[:, c * 128:(c + 1) * 128],
                                              pbuf[:, (j % 2) * FT:(j % 2 + 1) * FT],
                                              start=True, stop=True)
                            s_mmP.inc(inst)

                    def t_stats(j):
                        s_xn1.wait(eng, en, j)
                        for c in range(NCH):
                            s_cast.wait(eng, en, 16 * j + 2 * (c + 1))
                            sl = (c % 4) * FT
                            eng.matmul(psLN1[0:1, :], ones128[:, :], x1bf[:, sl:sl + FT],
                                       start=(c == 0), stop=(c == NCH - 1))
                            inst = eng.matmul(psLN2[0:1, :], ones128[:, :], sqbf[:, sl:sl + FT],
                                              start=(c == 0), stop=(c == NCH - 1))
                            s_mmstat.inc(inst)

                    def t_lnbcast(j):
                        s_rowg.wait(eng, en, j + 1)
                        eng.matmul(psLN1[:, :], onesrow[:, :], row(mrow, j), start=True, stop=True)
                        inst = eng.matmul(psLN2[:, :], onesrow[:, :], row(rrow, j),
                                          start=True, stop=True)
                        s_bc.inc(inst)

                    def t_ffn(k):
                        s_s1ev.wait(eng, en, k + 1)
                        for c in range(NCH):
                            if c == 0:
                                s_sa.wait(eng, en, min(8 * (k + 2), 8 * NT))
                                s_ep.wait(eng, en, 8 * k)
                            else:
                                s_h.wait(eng, en, 8 * k + c)
                            inst = eng.matmul(psP[0][:, :], bd1s[:, c * 128:(c + 1) * 128],
                                              s1stage[:, (k % 2) * FT:(k % 2 + 1) * FT],
                                              start=True, stop=True)
                            s_mmE.inc(inst)
                            if c >= 1:
                                cc = c - 1
                                s_h.wait(eng, en, 8 * k + cc + 1)
                                if cc == 0:
                                    s_g1.wait(eng, en, k)
                                inst = eng.matmul(psP[1][0:64, :], bd2s[:, cc * 64:(cc + 1) * 64],
                                                  hstage[:, (cc % 2) * FT:(cc % 2 + 1) * FT],
                                                  start=(cc == 0), stop=False)
                                s_mmG.inc(inst)
                        cc = NCH - 1
                        s_h.wait(eng, en, 8 * k + cc + 1)
                        inst = eng.matmul(psP[1][0:64, :], bd2s[:, cc * 64:(cc + 1) * 64],
                                          hstage[:, (cc % 2) * FT:(cc % 2 + 1) * FT],
                                          start=False, stop=True)
                        s_mmG.inc(inst)

                    def t_wout(k):
                        s_g1.wait(eng, en, k + 1)
                        for c in range(NCH):
                            if c >= 1:
                                s_ep.wait(eng, en, 8 * k + c)
                            inst = eng.matmul(psP[0][:, :], w1os[:, c * 128:(c + 1) * 128],
                                              g1sb[:, :], start=True, stop=True)
                            s_mmW.inc(inst)

                    t_bcastP(0)
                    t_stats(0)
                    t_lnbcast(0)
                    for i in range(NT):
                        s_xn1.wait(eng, en, i + 1)
                        for o in range(NCH):
                            if o < 2:
                                s_kev.wait(eng, en, 8 * (i - 1) + o + 7)
                            else:
                                s_qev.wait(eng, en, 8 * i + o - 1)
                            for c in range(NCH):
                                inst = eng.matmul(psF[o % 2][:, :], wsl(w1q, c, o),
                                                  xn1[:, c * FT:(c + 1) * FT],
                                                  start=(c == 0), stop=(c == NCH - 1))
                            s_mmQ.inc(inst)
                        if i + 1 < NT:
                            t_bcastP(i + 1)
                        for o in range(NCH):
                            if o == 0:
                                s_qev.wait(eng, en, 8 * i + 7)
                            elif o == 1:
                                s_qev.wait(eng, en, 8 * i + 8)
                            else:
                                s_kev.wait(eng, en, 8 * i + o - 1)
                            for c in range(NCH):
                                inst = eng.matmul(psF[o % 2][:, :], wsl(w1k, c, o),
                                                  xn1[:, c * FT:(c + 1) * FT],
                                                  start=(c == 0), stop=(c == NCH - 1))
                            s_mmK.inc(inst)
                        if i + 1 < NT:
                            t_stats(i + 1)
                        if i >= 1:
                            t_ffn(i - 1)
                        for g in range(4):
                            for eh in range(2):
                                n = 2 * g + eh
                                s_vev.wait(eng, en, 8 * i + n)
                                for c in range(NCH):
                                    lhs = xn1[:, c * FT + g * 128:c * FT + (g + 1) * 128]
                                    rhs = w1v[:, c * D + eh * 512:c * D + eh * 512 + 512]
                                    inst = eng.matmul(psV[:, :], lhs, rhs,
                                                      start=(c == 0), stop=(c == NCH - 1))
                                s_mmV.inc(inst)
                        if i >= 1:
                            t_wout(i - 1)
                        if i + 1 < NT:
                            t_lnbcast(i + 1)
                        s_s1ev.wait(eng, en, i)
                        for c in range(NCH):
                            inst = eng.matmul(psW1[:, :], w1i[:, c * 64:(c + 1) * 64],
                                              xn1[:, c * FT:(c + 1) * FT],
                                              start=(c == 0), stop=(c == NCH - 1))
                        s_mmW1.inc(inst)
                    t_ffn(NT - 1)
                    t_wout(NT - 1)

                @blk.sync
                def _(eng):
                    en = "sync"
                    bwait(eng, en)
                    for i in range(min(2, NT)):
                        s_ld.inc(chunked_load(eng, xbuf, (i % 2) * NCH * FT, xT, i * FT, FT), i)
                        s_ldp.inc(eng.dma_start(
                            out=pbuf[:, (i % 2) * FT:(i % 2 + 1) * FT],
                            in_=dram["probsH"].ap()[:, i * FT:(i + 1) * FT]), i)
                    s_ldg.inc(chunked_load(eng, g0buf, 0, g0T, 0, FT), 0)
                    for i in range(NT):
                        if i + 1 < NT:
                            s_x1.wait(eng, en, i + 1)
                            s_ldg.inc(chunked_load(eng, g0buf, 0, g0T, (i + 1) * FT, FT), i + 1)
                        s_x1.wait(eng, en, i + 1)
                        s_x1st.inc(chunked_store(eng, x1T, i * FT, FT, x1buf, (i % 2) * NCH * FT), i)
                        s_qev.wait(eng, en, 8 * (i + 1))
                        s_qst.inc(chunked_store(eng, qT, i * FT, FT, qstage, 0), i)
                        s_kev.wait(eng, en, 8 * (i + 1))
                        s_kst.inc(chunked_store(eng, kT, i * FT, FT, kstage, 0), i)
                        s_vev.wait(eng, en, 8 * (i + 1))
                        dst = v1.ap()[i * FT:(i + 1) * FT, :].rearrange("(g p) e -> p g e", p=128)
                        srcap = vstage[:, :].rearrange("p (g e) -> p g e", g=4)
                        s_vst.inc(eng.dma_start(out=dst, in_=srcap), i)
                        if i >= 1:
                            s_ep.wait(eng, en, 8 * i)
                            s_gst.inc(chunked_store(eng, gffT, (i - 1) * FT, FT, gstage, 0), i - 1)
                        if i + 2 < NT:
                            s_x1.wait(eng, en, i + 1)
                            s_ld.inc(chunked_load(eng, xbuf, (i % 2) * NCH * FT, xT,
                                                  (i + 2) * FT, FT), i + 2)
                            s_mmP.wait(eng, en, 8 * (i + 1))
                            s_ldp.inc(eng.dma_start(
                                out=pbuf[:, (i % 2) * FT:(i % 2 + 1) * FT],
                                in_=dram["probsH"].ap()[:, (i + 2) * FT:(i + 3) * FT]), i + 2)
                    s_ep.wait(eng, en, 8 * NT)
                    s_gst.inc(chunked_store(eng, gffT, (NT - 1) * FT, FT, gstage, 0), NT - 1)
                    s_x1st.wait_all(eng, en, NT)
                    s_qst.wait_all(eng, en, NT)
                    s_kst.wait_all(eng, en, NT)
                    s_vst.wait_all(eng, en, NT)
                    s_gst.wait_all(eng, en, NT)

            assert s_mmP.count == 8 * NT
            assert s_vev.count == 8 * NT
        return chains



def build_B(nc, E, dram, bwait, pool):
    x1T, qT, kT, v1, gffT, yT = (dram["x1T"], dram["qT"], dram["kT"], dram["v1"],
                                 dram["gffT"], dram["yT"])
    ctx = ExitStack()
    A = ctx.enter_context
    with ctx:
        qbuf = A(nc.sbuf_tensor("b_qbuf", [128, 2 * NCH * NQ], BF16))
        kbuf = A(nc.sbuf_tensor("b_kbuf", [128, 2 * NCH * KV], BF16))
        vring = A(nc.sbuf_tensor("b_vring", [128, SEG * D], BF16))
        pring = A(nc.sbuf_tensor("b_pring", [128, SEG * NQ], BF16))
        oacc = A(nc.sbuf_tensor("b_oacc", [128, 2 * NCH * NQ], F32))
        gffpan = A(nc.sbuf_tensor("b_gffpan", [128, 2 * NCH * NQ], BF16))
        x1pan = A(nc.sbuf_tensor("b_x1pan", [128, NCH * NQ], F32))
        outsb = A(nc.sbuf_tensor("b_outsb", [128, NCH * NQ], F32))
        rden = A(nc.sbuf_tensor("b_rden", [1, NQ], F32))
        maskbf = A(nc.sbuf_tensor("b_maskbf", [128, 4 * NQ], BF16))
        onescol = A(nc.sbuf_tensor("b_onescol", [128, 1], BF16))
        onesrow = A(nc.sbuf_tensor("b_onesrow", [1, 128], F32))

        psS = [A(nc.psum_tensor(f"b_psS{k}", [128, 512], F32)) for k in range(2)]
        psD = A(nc.psum_tensor("b_psD", [1, 512], F32))
        psO = [A(nc.psum_tensor(f"b_psO{k}", [128, 512], F32)) for k in range(4)]
        psBC = A(nc.psum_tensor("b_psBC", [128, 512], F32))

        s_k = SemGroup(nc, E, "b_k", 2)
        s_v = SemGroup(nc, E, "b_v", SEG)
        s_qld = SemGroup(nc, E, "b_qld", 2)
        s_x1ld = SemGroup(nc, E, "b_x1ld", 2)
        s_gld = SemGroup(nc, E, "b_gld", 2)
        s_wld = Sem(nc, E, "b_wld")
        s_yst = Sem(nc, E, "b_yst")
        s_init = pool.get()
        s_mmS = pool.get()
        s_exp = pool.get()
        s_msk = pool.get()
        s_tb = pool.get()
        s_swp = pool.get()
        s_oev = pool.get()
        s_mmD = pool.get()
        s_recD = pool.get()
        s_mmBc = pool.get()
        s_sat = pool.get()
        s_out = pool.get()
        chains = {"vector": Chain(pool.get(), "b_chv"), "scalar": Chain(pool.get(), "b_chs"),
                  "gpsimd": Chain(pool.get(), "b_chg")}

        def kslice(jg, c):
            o = (jg % 2) * NCH * KV + c * KV
            return kbuf[:, o:o + KV]

        def qslice(p, c):
            o = (p % 2) * NCH * NQ + c * NQ
            return qbuf[:, o:o + NQ]

        def vslice(jg, c):
            o = (jg % SEG) * D + c * 128
            return vring[:, o:o + 128]

        def pslice(jg):
            o = (jg % SEG) * NQ
            return pring[:, o:o + NQ]

        def oslice(p, c):
            o = ((p % 2) * NCH + c) * NQ
            return oacc[:, o:o + NQ]

        def gslice(p, c):
            o = ((p % 2) * NCH + c) * NQ
            return gffpan[:, o:o + NQ]

        def is_masked(p, j):
            return j >= tiles_in_panel(p) - 4

        nmask_before = {}
        cnt = 0
        for p in range(NP):
            for j in range(tiles_in_panel(p)):
                nmask_before[(p, j)] = cnt
                if is_masked(p, j):
                    cnt += 1

        def seg_tiles(p, s):
            J = tiles_in_panel(p)
            return list(range(s * SEG, min(J, (s + 1) * SEG)))

        with nc.Block() as blk:
            @blk.gpsimd
            def _(eng):
                en = "gpsimd"
                eng = CE(eng, chains[en])
                bwait(eng, en)
                s_wld.inc(eng.dma_start(out=maskbf[:, :], in_=dram["masks"].ap()), 16)
                for p in range(min(2, NP)):
                    s_qld.inc(chunked_load(eng, qbuf, (p % 2) * NCH * NQ, qT, p * NQ, NQ), p)
                    s_gld.inc(chunked_load(eng, gffpan, (p % 2) * NCH * NQ, gffT, p * NQ, NQ), p)
                s_x1ld.inc(chunked_load(eng, x1pan, 0, x1T, 0, NQ), 0)
                for p in range(NP):
                    if p + 1 < NP:
                        s_out.wait(eng, en, p + 1)
                        s_x1ld.inc(chunked_load(eng, x1pan, 0, x1T, (p + 1) * NQ, NQ), p + 1)
                    if p + 2 < NP:
                        s_mmS.wait(eng, en, tiles_before_panel(p + 1))
                        s_qld.inc(chunked_load(eng, qbuf, ((p + 2) % 2) * NCH * NQ, qT,
                                               (p + 2) * NQ, NQ), p + 2)
                        s_out.wait(eng, en, p + 1)
                        s_gld.inc(chunked_load(eng, gffpan, ((p + 2) % 2) * NCH * NQ, gffT,
                                               (p + 2) * NQ, NQ), p + 2)
                    s_out.wait(eng, en, p + 1)
                    s_yst.inc(chunked_store(eng, yT, p * NQ, NQ, outsb, 0), 16)
                s_yst.wait(eng, en, 16 * NP)

            @blk.sync
            def _(eng):
                en = "sync"
                bwait(eng, en)
                for p in range(NP):
                    jg0 = tiles_before_panel(p)
                    for j in range(tiles_in_panel(p)):
                        jg = jg0 + j
                        kv0 = j * KV
                        s_mmS.wait(eng, en, jg - 1)
                        srcap = kT.ap()[:, kv0:kv0 + KV].rearrange("(c p) n -> p c n", p=128)
                        dst = kbuf[:, (jg % 2) * NCH * KV:((jg % 2) + 1) * NCH * KV].rearrange(
                            "p (c n) -> p c n", c=NCH)
                        s_k.inc(eng.dma_start(out=dst, in_=srcap), jg)
                        s_tb.wait(eng, en, jg - (SEG - 1))
                        s_v.inc(eng.dma_start(out=vring[:, (jg % SEG) * D:((jg % SEG) + 1) * D],
                                              in_=v1.ap()[kv0:kv0 + KV, :]), jg)

            @blk.tensor
            def _(eng):
                en = "tensor"
                bwait(eng, en)
                s_init.wait(eng, en, 2)
                sg_global = 0
                for p in range(NP):
                    jg0 = tiles_before_panel(p)
                    J = tiles_in_panel(p)
                    nseg = segs_in_panel(p)
                    s_qld.wait_done(eng, en, p)

                    def emit_S(j):
                        jg = jg0 + j
                        s_k.wait_done(eng, en, jg)
                        s_exp.wait(eng, en, jg - 1)
                        for c in range(NCH):
                            inst = eng.matmul(psS[jg % 2][:, :], kslice(jg, c), qslice(p, c),
                                              start=(c == 0), stop=(c == NCH - 1))
                        s_mmS.inc(inst)

                    for s in range(nseg):
                        sg = sg_global
                        tiles = seg_tiles(p, s)
                        if s == 0:
                            emit_S(tiles[0])
                        for idx, j in enumerate(tiles):
                            jg = jg0 + j
                            if idx + 1 < len(tiles):
                                emit_S(tiles[idx + 1])
                            elif (s + 1) * SEG < J:
                                emit_S((s + 1) * SEG)
                            if is_masked(p, j):
                                s_msk.wait(eng, en, nmask_before[(p, j)] + 1)
                            else:
                                s_exp.wait(eng, en, jg + 1)
                            if j == 0:
                                s_recD.wait(eng, en, p)
                            inst = eng.matmul(psD[:, :], onescol[:, :], pslice(jg),
                                              start=(j == 0), stop=(j == J - 1))
                            if j == J - 1:
                                s_mmD.inc(inst)
                            s_v.wait_done(eng, en, jg)
                            if idx == 0:
                                s_oev.wait(eng, en, 8 * sg)
                            for c in range(4):
                                inst = eng.matmul(psO[c][:, :], vslice(jg, c), pslice(jg),
                                                  start=(idx == 0), stop=(idx == len(tiles) - 1))
                            if idx == len(tiles) - 1:
                                s_swp.inc(inst)
                        s_oev.wait(eng, en, 8 * sg + 4)
                        for idx, j in enumerate(tiles):
                            jg = jg0 + j
                            for c in range(4, 8):
                                inst = eng.matmul(psO[c - 4][:, :], vslice(jg, c), pslice(jg),
                                                  start=(idx == 0), stop=(idx == len(tiles) - 1))
                            s_tb.inc(inst)
                        sg_global += 1
                    s_recD.wait(eng, en, p + 1)
                    s_sat.wait(eng, en, p)
                    inst = eng.matmul(psBC[:, :], onesrow[:, :], rden[:, :], start=True, stop=True)
                    s_mmBc.inc(inst)

            @blk.scalar
            def _(eng):
                en = "scalar"
                eng = CE(eng, chains[en])
                bwait(eng, en)
                s_wld.wait(eng, en, 16)
                for p in range(NP):
                    jg0 = tiles_before_panel(p)
                    for j in range(tiles_in_panel(p)):
                        jg = jg0 + j
                        s_mmS.wait(eng, en, jg + 1)
                        s_tb.wait(eng, en, jg - (SEG - 1))
                        inst = eng.activation(out=pslice(jg), in_=psS[jg % 2][:, :],
                                              func=AF.Exp, scale=ISQ)
                        s_exp.inc(inst)

            @blk.vector
            def _(eng):
                en = "vector"
                eng = CE(eng, chains[en])
                bwait(eng, en)
                s_wld.wait(eng, en, 16)
                s_init.inc(eng.memset(onescol[:, :], 1.0))
                s_init.inc(eng.memset(onesrow[:, :], 1.0))
                sg_global = 0
                for p in range(NP):
                    jg0 = tiles_before_panel(p)
                    J = tiles_in_panel(p)
                    nseg = segs_in_panel(p)
                    for s in range(nseg):
                        sg = sg_global
                        tiles = seg_tiles(p, s)
                        for j in tiles:
                            if is_masked(p, j):
                                jg = jg0 + j
                                jd = j - (J - 4)
                                s_exp.wait(eng, en, jg + 1)
                                inst = eng.tensor_tensor(pslice(jg), pslice(jg),
                                                         maskbf[:, jd * NQ:(jd + 1) * NQ],
                                                         ALU.mult)
                                s_msk.inc(inst)
                        s_swp.wait(eng, en, sg + 1)
                        for c in range(4):
                            dst = oslice(p, c)
                            if s == 0:
                                inst = eng.tensor_copy(dst, psO[c][:, :])
                            else:
                                inst = eng.tensor_tensor(dst, dst, psO[c][:, :], ALU.add)
                            s_oev.inc(inst)
                        s_tb.wait(eng, en, jg0 + tiles[-1] + 1)
                        for c in range(4, 8):
                            dst = oslice(p, c)
                            if s == 0:
                                inst = eng.tensor_copy(dst, psO[c - 4][:, :])
                            else:
                                inst = eng.tensor_tensor(dst, dst, psO[c - 4][:, :], ALU.add)
                            s_oev.inc(inst)
                        sg_global += 1
                    s_mmD.wait(eng, en, p + 1)
                    inst = eng.reciprocal(rden[:, :], psD[0:1, 0:NQ])
                    s_recD.inc(inst)
                    s_mmBc.wait(eng, en, p + 1)
                    s_x1ld.wait_done(eng, en, p)
                    s_gld.wait_done(eng, en, p)
                    s_yst.wait(eng, en, 16 * p)
                    for c in range(NCH):
                        inst = eng.tensor_tensor(oslice(p, c), oslice(p, c), psBC[:, :], ALU.mult)
                        if c == NCH - 1:
                            s_sat.inc(inst)
                        eng.tensor_tensor(outsb[:, c * NQ:(c + 1) * NQ], oslice(p, c),
                                          gslice(p, c), ALU.mult)
                        inst = eng.tensor_tensor(outsb[:, c * NQ:(c + 1) * NQ],
                                                 outsb[:, c * NQ:(c + 1) * NQ],
                                                 x1pan[:, c * NQ:(c + 1) * NQ], ALU.add)
                    s_out.inc(inst)
                chains[en].flush(eng)

        assert s_exp.count == TOTAL_KV
        assert s_out.count == NP
        return chains



def kernel(**inputs):
    from concourse.bass_utils import run_bass_kernel_spmd
    C = prep_common(inputs)
    xTs = prep_percore(inputs)
    nc, input_names = make_program("FUSED")
    in_maps = []
    for c in range(8):
        m = {}
        for n in input_names:
            m[n] = xTs[c] if n == "xT" else C[n]
        in_maps.append(m)
    res = run_bass_kernel_spmd(nc, in_maps, core_ids=list(range(8)))
    out = np.stack([np.asarray(res.results[c]["yT"]).T for c in range(8)])
    return np.ascontiguousarray(out.astype(np.float32))


# version 08
# speedup vs baseline: 1.0055x; 1.0055x over previous
import numpy as np
import concourse.bass as bass
import concourse.mybir as mybir
from contextlib import ExitStack

F32 = mybir.dt.float32
BF16 = mybir.dt.bfloat16
AF = mybir.ActivationFunctionType
ALU = mybir.AluOpType

T = 8192
D = 1024
NCH = 8
FT = 512
NT = T // FT
NQ = 512
NP = T // NQ
KV = 128
SEG = 16
EPS = 1e-5
ISQ = 1.0 / 32.0


def tiles_in_panel(p):
    return 4 * p + 4


def tiles_before_panel(p):
    return 2 * p * p + 2 * p


def segs_in_panel(p):
    return (tiles_in_panel(p) + SEG - 1) // SEG


def segs_before_panel(p):
    return sum(segs_in_panel(q) for q in range(p))


TOTAL_KV = tiles_before_panel(NP)


LASTINC = {}
KEEPALIVE = []


class Sem:
    def __init__(self, nc, E, name):
        self.h = E(nc.semaphore(name))
        self.name = name
        self.count = 0
        self.hw = {}

    def inc(self, inst, n=1):
        inst.then_inc(self.h, n)
        self.count += n
        KEEPALIVE.append(inst)
        LASTINC[id(inst)] = (self, self.count)

    def wait(self, eng, ename, val):
        if val <= 0:
            return
        if self.hw.get(ename, 0) >= val:
            return
        self.hw[ename] = val
        eng.wait_ge(self.h, val)


class PhaseSem:

    def __init__(self, s):
        self.s = s
        self.base = s.count
        self.name = s.name

    def inc(self, inst, n=1):
        self.s.inc(inst, n)

    @property
    def count(self):
        return self.s.count - self.base

    def wait(self, eng, ename, val):
        if val <= 0:
            return
        self.s.wait(eng, ename, self.base + val)


class SemPool:
    def __init__(self, nc, E):
        self.nc = nc
        self.E = E
        self.pool = []
        self.idx = 0

    def phase_reset(self):
        self.idx = 0

    def get(self, name=None):
        if self.idx == len(self.pool):
            self.pool.append(Sem(self.nc, self.E, f"pool{self.idx}"))
        s = self.pool[self.idx]
        self.idx += 1
        return PhaseSem(s)


class Chain:

    def __init__(self, sem, name):
        self.sem = sem
        self.name = name
        self.last = None
        self.pending = None

    def before(self, eng):
        if self.pending is not None:
            inst = self.pending
            li = LASTINC.get(id(inst))
            if li is None:
                self.sem.inc(inst, 1)
                li = (self.sem, self.sem.count)
            self.last = li
            self.pending = None
        if self.last is not None:
            sem, val = self.last
            sem.wait(eng, self.name, val)

    def after(self, inst):
        self.pending = inst

    def flush(self, eng):
        self.before(eng)


class SemGroup:

    def __init__(self, nc, E, name, n, per=16):
        self.sems = [Sem(nc, E, f"{name}{k}") for k in range(n)]
        self.n = n
        self.per = per

    def inc(self, inst, idx):
        self.sems[idx % self.n].inc(inst, self.per)

    def wait_done(self, eng, en, idx):
        if idx < 0:
            return
        self.sems[idx % self.n].wait(eng, en, self.per * (idx // self.n + 1))

    def wait_all(self, eng, en, total):
        for idx in range(max(0, total - self.n), total):
            self.wait_done(eng, en, idx)


COMPUTE_OPS = {"tensor_tensor", "tensor_scalar", "tensor_copy", "tensor_reduce",
               "memset", "reciprocal", "activation", "copy", "add", "mul", "iota",
               "reduce_sum", "reduce_max", "select", "bn_stats"}


class CE:

    def __init__(self, eng, chain):
        self._e = eng
        self._c = chain

    def __getattr__(self, a):
        f = getattr(self._e, a)
        if a in COMPUTE_OPS:
            def g(*args, **kw):
                self._c.before(self._e)
                inst = f(*args, **kw)
                KEEPALIVE.append(inst)
                self._c.after(inst)
                return inst
            return g
        return f



def prep_common(I):
    C = {}
    C["w0rT"] = np.ascontiguousarray(I["L0_ffn_W_r"].T)
    C["w0kT"] = np.ascontiguousarray(I["L0_ffn_W_k"].T)
    C["w0vT"] = np.ascontiguousarray(I["L0_ffn_W_v"].T)
    C["wq0"] = np.ascontiguousarray(I["L0_attn_Wq"])
    C["wk0"] = np.ascontiguousarray(I["L0_attn_Wk"])
    C["wv0T"] = np.ascontiguousarray(I["L0_attn_Wv"].T)
    C["mur"] = np.ascontiguousarray(np.asarray(I["L0_ffn_mu_r"]).reshape(NCH, 128).T)
    C["muk"] = np.ascontiguousarray(np.asarray(I["L0_ffn_mu_k"]).reshape(NCH, 128).T)
    C["w1qT"] = np.ascontiguousarray(I["L1_attn_Wq"].T)
    C["w1kT"] = np.ascontiguousarray(I["L1_attn_Wk"].T)
    C["w1vT"] = np.ascontiguousarray(I["L1_attn_Wv"].T)
    C["b1q"] = np.ascontiguousarray(np.asarray(I["L1_attn_bq"]).reshape(NCH, 128).T)
    C["b1k"] = np.ascontiguousarray(np.asarray(I["L1_attn_bk"]).reshape(NCH, 128).T)
    C["b1vrow"] = np.ascontiguousarray(np.asarray(I["L1_attn_bv"]).reshape(1, D))
    C["w1in"] = np.ascontiguousarray(I["L1_ffn_Win"])
    C["b1in"] = np.ascontiguousarray(np.asarray(I["L1_ffn_bin_"]).reshape(64, 1))
    W1 = np.asarray(I["L1_ffn_W1"])
    bd1 = np.zeros((64, 1024), np.float32)
    for m in range(64):
        bd1[m, 16 * m:16 * m + 16] = W1[m]
    C["bd1"] = bd1
    C["b1f"] = np.ascontiguousarray(
        np.asarray(I["L1_ffn_b1"]).reshape(64 * 16).reshape(NCH, 128).T)
    W2 = np.asarray(I["L1_ffn_W2"])
    bd2 = np.zeros((1024, 64), np.float32)
    for m in range(64):
        bd2[16 * m:16 * m + 16, m] = W2[m]
    C["bd2"] = bd2
    C["b1b2"] = np.ascontiguousarray(np.asarray(I["L1_ffn_b2"]).reshape(64, 1))
    C["w1out"] = np.ascontiguousarray(I["L1_ffn_Wout"])
    C["b1out"] = np.ascontiguousarray(np.asarray(I["L1_ffn_bout"]).reshape(NCH, 128).T)
    hselred = np.zeros((128, NCH * 16), np.float32)
    for c in range(NCH):
        for p in range(128):
            hselred[p, c * 16 + 2 * c + p // 64] = 1.0
    C["hselred"] = hselred
    hselbc = np.zeros((16, NCH * 128), np.float32)
    for c in range(NCH):
        for p in range(128):
            hselbc[2 * c + p // 64, c * 128 + p] = 1.0
    C["hselbc"] = hselbc
    tk = np.arange(128)[:, None]
    tq = np.arange(512)[None, :]
    masks = np.concatenate([(tq >= tk + 128 * j).astype(np.float32) for j in range(4)],
                           axis=1)
    C["masks"] = np.ascontiguousarray(masks)
    return {k: np.asarray(v, np.float32) for k, v in C.items()}


def prep_percore(I):
    x = np.asarray(I["L0_attn_x"])
    return [np.ascontiguousarray(x[c].T) for c in range(8)]


PHASE_SETS = {"A1": ("A1",), "A2": ("A2",), "B": ("B",), "FUSED": ("A1", "A2", "B")}

DRAM_SPECS = {
    "xT": ((D, T), F32, None, ("A1", "A2")),
    "w0rT": ((D, D), F32, None, ("A1",)),
    "w0kT": ((D, D), F32, None, ("A1",)),
    "w0vT": ((D, D), F32, None, ("A1",)),
    "wq0": ((64, 64), F32, None, ("A1",)),
    "wk0": ((64, 64), F32, None, ("A1",)),
    "wv0T": ((64, 64), F32, None, ("A1",)),
    "mur": ((128, NCH), F32, None, ("A1",)),
    "muk": ((128, NCH), F32, None, ("A1",)),
    "hselred": ((128, NCH * 16), F32, None, ("A1",)),
    "g0T": ((D, T), BF16, "A1", ("A2",)),
    "s0T": ((16, T), F32, "A1", ("A2",)),
    "vsumH": ((128, NCH), F32, "A1", ("A2",)),
    "vsumT": ((1, NCH * 128), F32, "A1", ("A2",)),
    "hselbc": ((16, NCH * 128), F32, None, ("A2",)),
    "w1qT": ((D, D), F32, None, ("A2",)),
    "w1kT": ((D, D), F32, None, ("A2",)),
    "w1vT": ((D, D), F32, None, ("A2",)),
    "b1q": ((128, NCH), F32, None, ("A2",)),
    "b1k": ((128, NCH), F32, None, ("A2",)),
    "b1vrow": ((1, D), F32, None, ("A2",)),
    "w1in": ((D, 64), F32, None, ("A2",)),
    "b1in": ((64, 1), F32, None, ("A2",)),
    "x1T": ((D, T), F32, "A2", ("B",)),
    "qT": ((D, T), BF16, "A2", ("B",)),
    "kT": ((D, T), BF16, "A2", ("B",)),
    "v1": ((T, D), BF16, "A2", ("B",)),
    "gffT": ((D, T), BF16, "A2", ("B",)),
    "masks": ((128, 4 * 512), F32, None, ("B",)),
    "bd1": ((64, 1024), F32, None, ("A2",)),
    "b1f": ((128, NCH), F32, None, ("A2",)),
    "bd2": ((1024, 64), F32, None, ("A2",)),
    "b1b2": ((64, 1), F32, None, ("A2",)),
    "w1out": ((64, 1024), F32, None, ("A2",)),
    "b1out": ((128, NCH), F32, None, ("A2",)),
    "yT": ((D, T), F32, "B", ()),
}


def chunked_load(eng, dst_sb, dst_off, src_dram, col0, ncol, nch=NCH):
    src = src_dram.ap()[:, col0:col0 + ncol].rearrange("(c p) n -> p c n", p=128)
    dst = dst_sb[:, dst_off:dst_off + nch * ncol].rearrange("p (c n) -> p c n", c=nch)
    return eng.dma_start(out=dst, in_=src)


def chunked_store(eng, dst_dram, col0, ncol, src_sb, src_off, nch=NCH):
    dst = dst_dram.ap()[:, col0:col0 + ncol].rearrange("(c p) n -> p c n", p=128)
    src = src_sb[:, src_off:src_off + nch * ncol].rearrange("p (c n) -> p c n", c=nch)
    return eng.dma_start(out=dst, in_=src)


def make_program(mode):
    phases = PHASE_SETS[mode]
    nc = bass.Bass("TRN2", target_bir_lowering=False)
    dram = {}
    input_names = []
    for name, (shape, dt, prod, cons) in DRAM_SPECS.items():
        used_c = any(p in phases for p in cons)
        prod_here = prod in phases
        if prod is None:
            if used_c:
                dram[name] = nc.dram_tensor(name, list(shape), dt, kind="ExternalInput")
                input_names.append(name)
        elif name == "yT":
            if prod_here:
                dram[name] = nc.dram_tensor(name, list(shape), dt, kind="ExternalOutput")
        elif prod_here and used_c:
            dram[name] = nc.dram_tensor(name, list(shape), dt)
        elif prod_here:
            dram[name] = nc.dram_tensor(name, list(shape), dt, kind="ExternalOutput")
        elif used_c:
            dram[name] = nc.dram_tensor(name, list(shape), dt, kind="ExternalInput")
            input_names.append(name)
    if "A2" in phases:
        dram["cc_in"] = nc.dram_tensor("cc_in", [16, T], F32)
        dram["cc_out"] = nc.dram_tensor("cc_out", [16, T], F32)
        dram["probsH"] = nc.dram_tensor("probsH", [16, T], BF16)

    with ExitStack() as octx:
        E = octx.enter_context
        bar = Sem(nc, E, "bar")
        pool = SemPool(nc, E)
        state = {"gen": 0}

        def barrier_end(chains=None):
            with nc.Block() as blk:
                for nm, engf in (("sync", blk.sync), ("tensor", blk.tensor),
                                 ("scalar", blk.scalar), ("vector", blk.vector),
                                 ("gpsimd", blk.gpsimd)):
                    @engf
                    def _(eng, nm=nm):
                        if chains and nm in chains and chains[nm].last is not None:
                            sem, val = chains[nm].last
                            sem.wait(eng, nm, val)
                        bar.inc(eng.nop(), 1)
            state["gen"] += 5

        def bwait(eng, ename):
            bar.wait(eng, ename, state["gen"])

        for ph in phases:
            pool.phase_reset()
            if ph == "A1":
                chains = build_A1(nc, E, dram, bwait, pool)
            elif ph == "A2":
                chains = build_A2(nc, E, dram, bwait, barrier_end, pool)
            else:
                chains = build_B(nc, E, dram, bwait, pool)
            barrier_end(chains)
    return nc, input_names



def build_A1(nc, E, dram, bwait, pool):
    xT, g0T, s0T, vsumH = dram["xT"], dram["g0T"], dram["s0T"], dram["vsumH"]

    ctx = ExitStack()
    A = ctx.enter_context
    with ctx:
        xbuf = A(nc.sbuf_tensor("a1_xbuf", [128, 2 * NCH * FT], F32))
        xbf = A(nc.sbuf_tensor("a1_xbf", [128, NCH * FT], BF16))
        sqbf = A(nc.sbuf_tensor("a1_sqbf", [128, NCH * FT], BF16))
        xn = A(nc.sbuf_tensor("a1_xn", [128, 2 * NCH * 513], BF16))
        xr = A(nc.sbuf_tensor("a1_xr", [128, 2 * NCH * FT], BF16))
        xk = A(nc.sbuf_tensor("a1_xk", [128, 2 * NCH * FT], BF16))
        dscr = A(nc.sbuf_tensor("a1_d", [128, FT], BF16))
        escr = A(nc.sbuf_tensor("a1_e", [128, FT], BF16))
        tmpscr = A(nc.sbuf_tensor("a1_tmp", [128, FT], F32))
        r_sb = A(nc.sbuf_tensor("a1_r", [128, NCH * FT], BF16))
        relu_sb = A(nc.sbuf_tensor("a1_relu", [128, 2 * FT], BF16))
        kact = A(nc.sbuf_tensor("a1_kact", [128, NCH * FT], BF16))
        kvprod = A(nc.sbuf_tensor("a1_kvp", [128, 2 * FT], BF16))
        g0sb = A(nc.sbuf_tensor("a1_g0", [128, 2 * NCH * FT], BF16))
        qksb = A(nc.sbuf_tensor("a1_qk", [128, 2 * FT], BF16))
        vp = A(nc.sbuf_tensor("a1_vp", [128, NCH], F32))
        vsum = A(nc.sbuf_tensor("a1_vsum", [128, NCH], F32))
        s0row = A(nc.sbuf_tensor("a1_s0row", [16, 2 * FT], F32))
        mrow = A(nc.sbuf_tensor("a1_mrow", [1, FT], BF16))
        exrow = A(nc.sbuf_tensor("a1_exrow", [1, FT], F32))
        vrow = A(nc.sbuf_tensor("a1_vrow", [1, FT], F32))
        rrow = A(nc.sbuf_tensor("a1_rrow", [1, FT], BF16))
        srow = A(nc.sbuf_tensor("a1_srow", [1, FT], F32))
        bmrow = A(nc.sbuf_tensor("a1_bmrow", [128, 2 * FT], BF16))
        brrow = A(nc.sbuf_tensor("a1_brrow", [128, 2 * FT], BF16))
        ones128 = A(nc.sbuf_tensor("a1_ones128", [128, 1], BF16))
        onesrow = A(nc.sbuf_tensor("a1_onesrow", [1, 128], BF16))
        w0r = A(nc.sbuf_tensor("a1_w0r", [128, NCH * D], BF16))
        w0k = A(nc.sbuf_tensor("a1_w0k", [128, NCH * D], BF16))
        w0v = A(nc.sbuf_tensor("a1_w0v", [128, NCH * D], BF16))
        wq0b = A(nc.sbuf_tensor("a1_wq0b", [64, 64], BF16))
        wk0b = A(nc.sbuf_tensor("a1_wk0b", [64, 64], BF16))
        wv0b = A(nc.sbuf_tensor("a1_wv0b", [64, 64], BF16))
        mscr = A(nc.sbuf_tensor("a1_mscr", [64, 64], BF16))
        Mbd = A(nc.sbuf_tensor("a1_Mbd", [128, 128], BF16))
        Vbd = A(nc.sbuf_tensor("a1_Vbd", [128, 128], BF16))
        murs = A(nc.sbuf_tensor("a1_murs", [128, NCH], F32))
        muks = A(nc.sbuf_tensor("a1_muks", [128, NCH], F32))
        hred = A(nc.sbuf_tensor("a1_hred", [128, NCH * 16], BF16))

        psLN1 = A(nc.psum_tensor("a1_psLN1", [128, 512], F32))
        psLN2 = A(nc.psum_tensor("a1_psLN2", [128, 512], F32))
        psF = [A(nc.psum_tensor(f"a1_psF{k}", [128, 512], F32)) for k in range(2)]
        psQ = [A(nc.psum_tensor(f"a1_psQ{k}", [128, 512], F32)) for k in range(2)]
        psS0 = A(nc.psum_tensor("a1_psS0", [16, 512], F32))
        psM = A(nc.psum_tensor("a1_psM", [64, 64], F32))

        s_ld = SemGroup(nc, E, "a1_ld", 2)
        s_wld = Sem(nc, E, "a1_wld")
        s_init = pool.get()
        s_mmM = pool.get()
        s_evM = pool.get()
        s_cast = pool.get()
        s_mmstat = pool.get()
        s_rowd = pool.get()
        s_sqr = pool.get()
        s_rowg = pool.get()
        s_bc = pool.get()
        s_bcevS = pool.get()
        s_xnLN = pool.get()
        s_mix = pool.get()
        s_mmR = pool.get()
        s_sig = pool.get()
        s_mmK = pool.get()
        s_rel = pool.get()
        s_kact = pool.get()
        s_mmV = pool.get()
        s_kvev = pool.get()
        s_silu = pool.get()
        s_rk = pool.get()
        s_mmU = pool.get()
        s_qk = pool.get()
        s_mmV0 = pool.get()
        s_mms0 = pool.get()
        s_vev = pool.get()
        s_vsadd = pool.get()
        s_xnfree = pool.get()
        s_s0ev = pool.get()
        s_g0st = SemGroup(nc, E, "a1_g0st", 2)
        s_s0st = SemGroup(nc, E, "a1_s0st", 2)
        s_fin = Sem(nc, E, "a1_fin")
        chains = {"vector": Chain(pool.get(), "a1_chv"), "scalar": Chain(pool.get(), "a1_chs"),
                  "gpsimd": Chain(pool.get(), "a1_chg")}

        def xb(i, c):
            o = ((i % 2) * NCH + c) * FT
            return xbuf[:, o:o + FT]

        def xns(i, c, a, b):
            o = ((i % 2) * NCH + c) * 513
            return xn[:, o + a:o + b]

        def xrs(i, c):
            o = ((i % 2) * NCH + c) * FT
            return xr[:, o:o + FT]

        def xks(i, c):
            o = ((i % 2) * NCH + c) * FT
            return xk[:, o:o + FT]

        def wsl(w, c, o):
            return w[:, c * D + o * 128:c * D + o * 128 + 128]

        def row(buf, i):
            if buf.shape[1] == FT:
                return buf[:, 0:FT]
            o = (i % 2) * FT
            return buf[:, o:o + FT]

        with nc.Block() as blk:
            @blk.gpsimd
            def _(eng):
                en = "gpsimd"
                eng = CE(eng, chains[en])
                bwait(eng, en)
                for nm, dst in (("wq0", wq0b), ("wk0", wk0b), ("wv0T", wv0b)):
                    s_wld.inc(eng.dma_start(out=dst[:, :], in_=dram[nm].ap()), 16)
                for nm, dst in (("mur", murs), ("muk", muks)):
                    s_wld.inc(eng.dma_start(out=dst[:, :], in_=dram[nm].ap()), 16)
                s_wld.inc(eng.dma_start(out=hred[:, :], in_=dram["hselred"].ap()), 16)
                for nm, dst in (("w0rT", w0r), ("w0kT", w0k), ("w0vT", w0v)):
                    src = dram[nm].ap().rearrange("(c p) n -> p c n", p=128)
                    d2 = dst[:, :].rearrange("p (c n) -> p c n", c=NCH)
                    s_wld.inc(eng.dma_start(out=d2, in_=src), 16)
                chains[en].flush(eng)

            @blk.vector
            def _(eng):
                en = "vector"
                eng = CE(eng, chains[en])
                bwait(eng, en)
                s_init.inc(eng.memset(Mbd[:, :], 0.0))
                s_init.inc(eng.memset(Vbd[:, :], 0.0))
                s_init.inc(eng.memset(ones128[:, :], 1.0))
                s_init.inc(eng.memset(onesrow[:, :], 1.0))
                s_init.inc(eng.memset(vp[:, :], 0.0))
                s_init.inc(eng.memset(vsum[:, :], 0.0))
                for c in range(NCH):
                    s_init.inc(eng.memset(xn[:, c * 513:c * 513 + 1], 0.0))
                s_evM.wait(eng, en, 1)
                s_init.inc(eng.tensor_copy(Mbd[0:64, 0:64], mscr[:, :]))
                s_init.inc(eng.tensor_copy(Mbd[64:128, 64:128], mscr[:, :]))
                s_wld.wait(eng, en, 144)
                s_init.inc(eng.tensor_copy(Vbd[0:64, 0:64], wv0b[:, :]))
                s_init.inc(eng.tensor_copy(Vbd[64:128, 64:128], wv0b[:, :]))

                def v_rows(i):
                    s_mmstat.wait(eng, en, 8 * (i + 1))
                    s_bc.wait(eng, en, i)
                    eng.tensor_scalar(row(mrow, i), psLN1[0:1, :], 1.0 / D, None, ALU.mult)
                    eng.tensor_scalar(row(exrow, i), psLN2[0:1, :], 1.0 / D, None, ALU.mult)
                    eng.tensor_tensor(row(vrow, i), row(mrow, i), row(mrow, i), ALU.mult)
                    eng.tensor_tensor(row(vrow, i), row(exrow, i), row(vrow, i), ALU.subtract)
                    inst = eng.tensor_scalar(row(vrow, i), row(vrow, i), EPS, None, ALU.add)
                    s_rowd.inc(inst)
                    s_sqr.wait(eng, en, i + 1)
                    eng.reciprocal(row(vrow, i), srow[:, :])
                    inst = eng.tensor_copy(row(rrow, i), row(vrow, i))
                    s_rowg.inc(inst)

                def v_xn(i):
                    s_bcevS.wait(eng, en, i + 1)
                    s_xnfree.wait(eng, en, i - 1)
                    sl = (i % 2) * FT
                    for c in range(NCH):
                        eng.tensor_tensor(tmpscr[:, :], xb(i, c), bmrow[:, sl:sl + FT], ALU.subtract)
                        inst = eng.tensor_tensor(xns(i, c, 1, 513), tmpscr[:, :],
                                                 brrow[:, sl:sl + FT], ALU.mult)
                    s_xnLN.inc(inst)
                    if i + 1 < NT:
                        for c in range(NCH):
                            eng.tensor_copy(xns(i + 1, c, 0, 1), xns(i, c, 512, 513))

                def v_mix(i):
                    s_rk.wait(eng, en, i - 1)
                    for c in range(NCH):
                        eng.tensor_tensor(dscr[:, :], xns(i, c, 1, 513), xns(i, c, 0, 512), ALU.subtract)
                        eng.tensor_scalar(escr[:, :], dscr[:, :], murs[:, c:c + 1], None, ALU.mult)
                        eng.tensor_tensor(xrs(i, c), xns(i, c, 0, 512), escr[:, :], ALU.add)
                        eng.tensor_scalar(escr[:, :], dscr[:, :], muks[:, c:c + 1], None, ALU.mult)
                        inst = eng.tensor_tensor(xks(i, c), xns(i, c, 0, 512), escr[:, :], ALU.add)
                    s_mix.inc(inst)

                def v_vsum(i):
                    s_vev.wait(eng, en, 8 * (i + 1))
                    inst = eng.tensor_tensor(vsum[:, :], vsum[:, :], vp[:, :], ALU.add)
                    s_vsadd.inc(inst)

                v_rows(0)
                v_xn(0)
                v_mix(0)
                for i in range(NT):
                    if i + 1 < NT:
                        v_rows(i + 1)
                        v_xn(i + 1)
                    if i >= 1:
                        v_vsum(i - 1)
                    if i + 1 < NT:
                        v_mix(i + 1)
                    for o in range(NCH):
                        s_rel.wait(eng, en, 8 * i + o + 1)
                        inst = eng.tensor_tensor(kact[:, o * FT:(o + 1) * FT], psF[o % 2][:, :],
                                                 relu_sb[:, (o % 2) * FT:(o % 2 + 1) * FT], ALU.mult)
                        s_kact.inc(inst)
                    for o in range(NCH):
                        s_mmV.wait(eng, en, 8 * i + o + 1)
                        s_silu.wait(eng, en, 8 * i + o - 1)
                        inst = eng.tensor_tensor(kvprod[:, (o % 2) * FT:(o % 2 + 1) * FT],
                                                 psF[o % 2][:, :], r_sb[:, o * FT:(o + 1) * FT], ALU.mult)
                        s_kvev.inc(inst)
                        s_mmU.wait(eng, en, 8 * i + o + 1)
                        s_mms0.wait(eng, en, 8 * i + o - 1)
                        inst = eng.tensor_tensor(qksb[:, (o % 2) * FT:(o % 2 + 1) * FT],
                                                 xns(i, o, 1, 513), psQ[o % 2][:, :], ALU.mult)
                        s_qk.inc(inst)
                v_vsum(NT - 1)
                chains[en].flush(eng)

            @blk.scalar
            def _(eng):
                en = "scalar"
                eng = CE(eng, chains[en])
                bwait(eng, en)
                s_mmM.wait(eng, en, 1)
                s_evM.inc(eng.copy(mscr[:, :], psM[:, :]))

                def s_casts(j):
                    s_ld.wait_done(eng, en, j)
                    for c in range(NCH):
                        s_mmstat.wait(eng, en, 8 * j + c - 7)
                        sl = c * FT
                        s_cast.inc(eng.copy(xbf[:, sl:sl + FT], xb(j, c)))
                        s_cast.inc(eng.activation(out=sqbf[:, sl:sl + FT], in_=xb(j, c),
                                                  func=AF.Square))

                def s_sqrt(j):
                    s_rowd.wait(eng, en, j + 1)
                    s_sqr.inc(eng.activation(out=srow[:, :], in_=row(vrow, j), func=AF.Sqrt))

                def s_bcev(j):
                    s_bc.wait(eng, en, j + 1)
                    sl = (j % 2) * FT
                    eng.copy(bmrow[:, sl:sl + FT], psLN1[:, :])
                    inst = eng.copy(brrow[:, sl:sl + FT], psLN2[:, :])
                    s_bcevS.inc(inst)

                s_casts(0)
                s_sqrt(0)
                s_bcev(0)
                if NT > 1:
                    s_casts(1)

                def s_vev1(k, c):
                    s_mmV0.wait(eng, en, 8 * k + c + 1)
                    s_vsadd.wait(eng, en, k)
                    inst = eng.activation(out=relu_sb[:, 0:FT], in_=psQ[c % 2][:, :],
                                          func=AF.Copy, accum_out=vp[:, c:c + 1])
                    s_vev.inc(inst)

                def s_s0(k):
                    s_mms0.wait(eng, en, 8 * (k + 1))
                    s_s0st.wait_done(eng, en, k - 2)
                    inst = eng.copy(row(s0row, k), psS0[:, :])
                    s_s0ev.inc(inst)

                for i in range(NT):
                    for o in range(NCH):
                        s_mmR.wait(eng, en, 8 * i + o + 1)
                        s_kvev.wait(eng, en, 8 * (i - 1) + o + 1)
                        inst = eng.activation(out=r_sb[:, o * FT:(o + 1) * FT], in_=psF[o % 2][:, :],
                                              func=AF.Sigmoid)
                        s_sig.inc(inst)
                        if i >= 1:
                            s_vev1(i - 1, o)
                    if i >= 1:
                        s_s0(i - 1)
                    if i + 2 < NT:
                        s_casts(i + 2)
                    if i + 1 < NT:
                        s_sqrt(i + 1)
                        s_bcev(i + 1)
                    for o in range(NCH):
                        s_mmK.wait(eng, en, 8 * i + o + 1)
                        s_kact.wait(eng, en, 8 * i + o - 1)
                        inst = eng.activation(out=relu_sb[:, (o % 2) * FT:(o % 2 + 1) * FT],
                                              in_=psF[o % 2][:, :], func=AF.Relu)
                        s_rel.inc(inst)
                    for o in range(NCH):
                        s_kvev.wait(eng, en, 8 * i + o + 1)
                        s_g0st.wait_done(eng, en, i - 2)
                        dst = g0sb[:, ((i % 2) * NCH + o) * FT:((i % 2) * NCH + o + 1) * FT]
                        inst = eng.activation(out=dst, in_=kvprod[:, (o % 2) * FT:(o % 2 + 1) * FT],
                                              func=AF.Silu)
                        s_silu.inc(inst)
                for c in range(NCH):
                    s_vev1(NT - 1, c)
                s_s0(NT - 1)
                chains[en].flush(eng)

            @blk.tensor
            def _(eng):
                en = "tensor"
                bwait(eng, en)
                s_wld.wait(eng, en, 144)
                inst = eng.matmul(psM[:, :], wk0b[:, :], wq0b[:, :], start=True, stop=True)
                s_mmM.inc(inst)
                s_init.wait(eng, en, 18)

                def t_stats(j):
                    s_bcevS.wait(eng, en, j)
                    for c in range(NCH):
                        s_cast.wait(eng, en, 16 * j + 2 * (c + 1))
                        sl = c * FT
                        eng.matmul(psLN1[0:1, :], ones128[:, :], xbf[:, sl:sl + FT],
                                   start=(c == 0), stop=(c == NCH - 1))
                        inst = eng.matmul(psLN2[0:1, :], ones128[:, :], sqbf[:, sl:sl + FT],
                                          start=(c == 0), stop=(c == NCH - 1))
                        s_mmstat.inc(inst)

                def t_bcast(j):
                    s_rowg.wait(eng, en, j + 1)
                    eng.matmul(psLN1[:, :], onesrow[:, :], row(mrow, j), start=True, stop=True)
                    inst = eng.matmul(psLN2[:, :], onesrow[:, :], row(rrow, j), start=True, stop=True)
                    s_bc.inc(inst)

                t_stats(0)
                t_bcast(0)

                def emit_v(k, c):
                    if c < 2:
                        s_qk.wait(eng, en, 8 * k + c + 7)
                    else:
                        s_vev.wait(eng, en, 8 * k + c - 1)
                    inst = eng.matmul(psQ[c % 2][:, :], Vbd[:, :], xns(k, c, 1, 513),
                                      start=True, stop=True)
                    s_mmV0.inc(inst)
                    if c == NCH - 1:
                        s_xnfree.inc(eng.nop())

                for i in range(NT):
                    if i + 1 < NT:
                        t_stats(i + 1)
                    s_mix.wait(eng, en, i + 1)
                    for o in range(NCH):
                        if o < 2:
                            s_kvev.wait(eng, en, 8 * (i - 1) + o + 7)
                        else:
                            s_sig.wait(eng, en, 8 * i + o - 1)
                        for c in range(NCH):
                            inst = eng.matmul(psF[o % 2][:, :], wsl(w0r, c, o), xrs(i, c),
                                              start=(c == 0), stop=(c == NCH - 1))
                        s_mmR.inc(inst)
                        if i >= 1:
                            emit_v(i - 1, o)
                    if i + 1 < NT:
                        t_bcast(i + 1)
                    for o in range(NCH):
                        if o == 0:
                            s_sig.wait(eng, en, 8 * i + 7)
                        elif o == 1:
                            s_sig.wait(eng, en, 8 * i + 8)
                        else:
                            s_kact.wait(eng, en, 8 * i + o - 1)
                        for c in range(NCH):
                            inst = eng.matmul(psF[o % 2][:, :], wsl(w0k, c, o), xks(i, c),
                                              start=(c == 0), stop=(c == NCH - 1))
                        s_mmK.inc(inst)
                        if o == NCH - 1:
                            s_rk.inc(eng.nop())
                    def emit_s0red(c):
                        s_qk.wait(eng, en, 8 * i + c + 1)
                        if c == 0:
                            s_s0ev.wait(eng, en, i)
                        inst = eng.matmul(psS0[:, :], hred[:, c * 16:(c + 1) * 16],
                                          qksb[:, (c % 2) * FT:(c % 2 + 1) * FT],
                                          start=(c == 0), stop=(c == NCH - 1))
                        s_mms0.inc(inst)

                    s_kact.wait(eng, en, 8 * (i + 1))
                    for o in range(NCH):
                        if o >= 2:
                            s_kvev.wait(eng, en, 8 * i + o - 1)
                        for c in range(NCH):
                            inst = eng.matmul(psF[o % 2][:, :], wsl(w0v, c, o),
                                              kact[:, c * FT:(c + 1) * FT],
                                              start=(c == 0), stop=(c == NCH - 1))
                        s_mmV.inc(inst)
                        if o < 2:
                            s_vev.wait(eng, en, 8 * i + o - 1)
                        else:
                            s_qk.wait(eng, en, 8 * i + o - 1)
                        inst = eng.matmul(psQ[o % 2][:, :], Mbd[:, :], xns(i, o, 1, 513),
                                          start=True, stop=True)
                        s_mmU.inc(inst)
                        if o >= 2:
                            emit_s0red(o - 2)
                    emit_s0red(NCH - 2)
                    emit_s0red(NCH - 1)
                for c in range(NCH):
                    emit_v(NT - 1, c)

            @blk.sync
            def _(eng):
                en = "sync"
                bwait(eng, en)
                for i in range(min(2, NT)):
                    s_ld.inc(chunked_load(eng, xbuf, (i % 2) * NCH * FT, xT, i * FT, FT), i)
                for i in range(NT):
                    if i + 2 < NT:
                        s_xnLN.wait(eng, en, i + 1)
                        s_ld.inc(chunked_load(eng, xbuf, (i % 2) * NCH * FT, xT, (i + 2) * FT, FT), i + 2)
                    s_silu.wait(eng, en, 8 * (i + 1))
                    s_g0st.inc(chunked_store(eng, g0T, i * FT, FT, g0sb, (i % 2) * NCH * FT), i)
                    s_s0ev.wait(eng, en, i + 1)
                    s_s0st.inc(eng.dma_start(out=s0T.ap()[:, i * FT:(i + 1) * FT],
                                             in_=row(s0row, i)), i)
                s_vsadd.wait(eng, en, NT)
                s_fin.inc(eng.dma_start(out=vsumH.ap(), in_=vsum[:, :]), 16)
                s_fin.inc(eng.dma_start(
                    out=dram["vsumT"].ap().rearrange("o (c p) -> (o p) c", p=128),
                    in_=vsum[:, :], allow_slow_non_contiguous=True), 16)
                s_fin.wait(eng, en, 32)
                s_g0st.wait_all(eng, en, NT)
                s_s0st.wait_all(eng, en, NT)

        assert s_mix.count == NT and s_mmstat.count == 8 * NT
        assert s_sig.count == 8 * NT and s_kact.count == 8 * NT
        return chains



def build_A2(nc, E, dram, bwait, barrier_end, pool):
    xT, g0T, s0T, vsumH = dram["xT"], dram["g0T"], dram["s0T"], dram["vsumH"]
    x1T, qT, kT, v1, gffT = dram["x1T"], dram["qT"], dram["kT"], dram["v1"], dram["gffT"]
    cc_in, cc_out = dram["cc_in"], dram["cc_out"]

    pctx = ExitStack()
    P = pctx.enter_context
    with pctx:
        vsumt = P(nc.sbuf_tensor("a2_vsumt", [1, NCH * 128], BF16))
        s_prob = pool.get()
        chains = {"vector": Chain(pool.get(), "a2_chv"), "scalar": Chain(pool.get(), "a2_chs"),
                  "gpsimd": Chain(pool.get(), "a2_chg")}

        cctx = ExitStack()
        C = cctx.enter_context
        with cctx:
            bufP = C(nc.sbuf_tensor("a2_bufP", [16, T], F32))
            bufQ = C(nc.sbuf_tensor("a2_bufQ", [16, T], F32))
            bufR = C(nc.sbuf_tensor("a2_bufR", [16, T], F32))
            probsb = C(nc.sbuf_tensor("a2_probsb", [16, T], BF16))
            s_cl = Sem(nc, E, "a2_cl")
            s_ex = pool.get()
            s_cc = Sem(nc, E, "a2_cc")
            with nc.Block() as blk:
                @blk.gpsimd
                def _(eng):
                    en = "gpsimd"
                    eng = CE(eng, chains[en])
                    bwait(eng, en)
                    s_cl.inc(eng.dma_start(out=vsumt[:, :], in_=dram["vsumT"].ap()), 16)
                    s_cl.inc(eng.dma_start(out=bufP[:, :], in_=s0T.ap()), 16)
                    s_ex.wait(eng, en, 1)
                    s_cl.inc(eng.dma_start(out=cc_in.ap(), in_=bufQ[:, :]), 16)
                    s_cl.wait(eng, en, 48)
                    inst = eng.collective_compute(
                        "AllReduce", ALU.add, replica_groups=[list(range(8))],
                        ins=[cc_in.ap().opt()], outs=[cc_out.ap().opt()])
                    s_cc.inc(inst, 1)
                    s_cc.wait(eng, en, 1)
                    s_cl.inc(eng.dma_start(out=bufR[:, :], in_=cc_out.ap()), 16)
                    s_prob.wait(eng, en, 1)
                    s_cl.inc(eng.dma_start(out=dram["probsH"].ap(), in_=probsb[:, :]), 16)
                    s_cl.wait(eng, en, 80)
                    chains[en].flush(eng)

                @blk.scalar
                def _(eng):
                    en = "scalar"
                    eng = CE(eng, chains[en])
                    bwait(eng, en)
                    s_cl.wait(eng, en, 32)
                    inst = eng.activation(out=bufQ[:, :], in_=bufP[:, :], func=AF.Exp, scale=ISQ)
                    s_ex.inc(inst)
                    chains[en].flush(eng)

                @blk.vector
                def _(eng):
                    en = "vector"
                    eng = CE(eng, chains[en])
                    bwait(eng, en)
                    s_cl.wait(eng, en, 64)
                    eng.reciprocal(bufP[:, :], bufR[:, :])
                    inst = eng.tensor_tensor(probsb[:, :], bufQ[:, :], bufP[:, :], ALU.mult)
                    s_prob.inc(inst)
                    chains[en].flush(eng)
            barrier_end(chains)

        ctx = ExitStack()
        A = ctx.enter_context
        with ctx:
            xbuf = A(nc.sbuf_tensor("a2_xbuf", [128, 2 * NCH * FT], F32))
            g0buf = A(nc.sbuf_tensor("a2_g0buf", [128, NCH * FT], BF16))
            x1buf = A(nc.sbuf_tensor("a2_x1buf", [128, 2 * NCH * FT], F32))
            x1bf = A(nc.sbuf_tensor("a2_x1bf", [128, 4 * FT], BF16))
            sqbf = A(nc.sbuf_tensor("a2_sqbf", [128, 4 * FT], BF16))
            xn1 = A(nc.sbuf_tensor("a2_xn1", [128, NCH * FT], BF16))
            vselw = A(nc.sbuf_tensor("a2_vselw", [16, NCH * 128], BF16))
            pbuf = A(nc.sbuf_tensor("a2_pbuf", [16, 2 * FT], BF16))
            tmpscr = A(nc.sbuf_tensor("a2_tmp", [128, FT], F32))
            t1scr = tmpscr
            qstage = A(nc.sbuf_tensor("a2_qstage", [128, NCH * FT], BF16))
            kstage = A(nc.sbuf_tensor("a2_kstage", [128, NCH * FT], BF16))
            vstage = A(nc.sbuf_tensor("a2_vstage", [128, 4 * D], BF16))
            s1stage = A(nc.sbuf_tensor("a2_s1stage", [64, 2 * FT], BF16))
            bd1s = A(nc.sbuf_tensor("a2_bd1s", [64, 1024], BF16))
            bd2s = A(nc.sbuf_tensor("a2_bd2s", [128, NCH * 64], BF16))
            w1os = A(nc.sbuf_tensor("a2_w1os", [64, 1024], BF16))
            b1fs = A(nc.sbuf_tensor("a2_b1fs", [128, NCH], F32))
            b1b2s = A(nc.sbuf_tensor("a2_b1b2s", [64, 1], F32))
            b1outs = A(nc.sbuf_tensor("a2_b1outs", [128, NCH], F32))
            hstage = A(nc.sbuf_tensor("a2_hstage", [128, 2 * FT], BF16))
            g1sb = A(nc.sbuf_tensor("a2_g1sb", [64, FT], BF16))
            sgscr = A(nc.sbuf_tensor("a2_sgscr", [128, 2 * FT], BF16))
            fscr = A(nc.sbuf_tensor("a2_fscr", [128, FT], F32))
            gstage = A(nc.sbuf_tensor("a2_gstage", [128, NCH * FT], BF16))
            mrow = A(nc.sbuf_tensor("a2_mrow", [1, FT], BF16))
            exrow = A(nc.sbuf_tensor("a2_exrow", [1, FT], F32))
            vrow = A(nc.sbuf_tensor("a2_vrow", [1, FT], F32))
            rrow = A(nc.sbuf_tensor("a2_rrow", [1, FT], BF16))
            srow = A(nc.sbuf_tensor("a2_srow", [1, FT], BF16))
            ones128 = A(nc.sbuf_tensor("a2_ones128", [128, 1], BF16))
            onesrow = A(nc.sbuf_tensor("a2_onesrow", [1, 128], F32))
            hbc = A(nc.sbuf_tensor("a2_hbc", [16, NCH * 128], BF16))
            w1q = A(nc.sbuf_tensor("a2_w1q", [128, NCH * D], BF16))
            w1k = A(nc.sbuf_tensor("a2_w1k", [128, NCH * D], BF16))
            w1v = A(nc.sbuf_tensor("a2_w1v", [128, NCH * D], BF16))
            w1i = A(nc.sbuf_tensor("a2_w1i", [128, NCH * 64], BF16))
            b1qs = A(nc.sbuf_tensor("a2_b1qs", [128, NCH], F32))
            b1ks = A(nc.sbuf_tensor("a2_b1ks", [128, NCH], F32))
            b1ins = A(nc.sbuf_tensor("a2_b1ins", [64, 1], F32))
            bvrow = A(nc.sbuf_tensor("a2_bvrow", [1, D], BF16))
            onesrowb = A(nc.sbuf_tensor("a2_onesrowb", [1, 128], BF16))
            bvbc = A(nc.sbuf_tensor("a2_bvbc", [128, D], BF16))

            psP = [A(nc.psum_tensor(f"a2_psP{k}", [128, 512], F32)) for k in range(2)]
            psLN1 = A(nc.psum_tensor("a2_psLN1", [128, 512], F32))
            psLN2 = A(nc.psum_tensor("a2_psLN2", [128, 512], F32))
            psF = [A(nc.psum_tensor(f"a2_psF{k}", [128, 512], F32)) for k in range(2)]
            psV = A(nc.psum_tensor("a2_psV", [128, 512], F32))
            psW1 = A(nc.psum_tensor("a2_psW1", [64, 512], F32))

            s_ld = SemGroup(nc, E, "a2_ld", 2, per=16)
            s_ldg = SemGroup(nc, E, "a2_ldg", 2, per=16)
            s_ldp = SemGroup(nc, E, "a2_ldp", 2, per=16)
            s_wld = Sem(nc, E, "a2_wld")
            s_init = pool.get()
            s_mmbv = pool.get()
            s_evbv = pool.get()
            s_mmP = pool.get()
            s_sa = pool.get()
            s_x1 = pool.get()
            s_cast = pool.get()
            s_mmstat = pool.get()
            s_rowd = pool.get()
            s_sqr = pool.get()
            s_rowg = pool.get()
            s_bc = pool.get()
            s_xn1 = pool.get()
            s_mmQ = pool.get()
            s_qev = pool.get()
            s_mmK = pool.get()
            s_kev = pool.get()
            s_mmV = pool.get()
            s_vev = pool.get()
            s_mmW1 = pool.get()
            s_s1ev = pool.get()
            s_mmvw = pool.get()
            s_vw = pool.get()
            s_x1st = SemGroup(nc, E, "a2_x1st", 2)
            s_qst = SemGroup(nc, E, "a2_qst", 2)
            s_kst = SemGroup(nc, E, "a2_kst", 2)
            s_vst = SemGroup(nc, E, "a2_vst", 2)
            s_gst = SemGroup(nc, E, "a2_gst", 2)
            s_mmE = pool.get()
            s_h = pool.get()
            s_mmG = pool.get()
            s_g1 = pool.get()
            s_mmW = pool.get()
            s_sg = pool.get()
            s_ep = pool.get()

            def xb(i, c):
                o = ((i % 2) * NCH + c) * FT
                return xbuf[:, o:o + FT]

            def g0s(i, c):
                return g0buf[:, c * FT:(c + 1) * FT]

            def x1s(i, c):
                o = ((i % 2) * NCH + c) * FT
                return x1buf[:, o:o + FT]

            def wsl(w, c, o):
                return w[:, c * D + o * 128:c * D + o * 128 + 128]

            def row(buf, i):
                if buf.shape[1] == FT:
                    return buf[:, 0:FT]
                o = (i % 2) * FT
                return buf[:, o:o + FT]

            with nc.Block() as blk:
                @blk.gpsimd
                def _(eng):
                    en = "gpsimd"
                    eng = CE(eng, chains[en])
                    bwait(eng, en)
                    s_wld.inc(eng.dma_start(out=hbc[:, :], in_=dram["hselbc"].ap()), 16)
                    for nm, dst in (("w1qT", w1q), ("w1kT", w1k), ("w1vT", w1v)):
                        src = dram[nm].ap().rearrange("(c p) n -> p c n", p=128)
                        d2 = dst[:, :].rearrange("p (c n) -> p c n", c=NCH)
                        s_wld.inc(eng.dma_start(out=d2, in_=src), 16)
                    src = dram["w1in"].ap().rearrange("(c p) n -> p c n", p=128)
                    d2 = w1i[:, :].rearrange("p (c n) -> p c n", c=NCH)
                    s_wld.inc(eng.dma_start(out=d2, in_=src), 16)
                    s_wld.inc(eng.dma_start(out=b1qs[:, :], in_=dram["b1q"].ap()), 16)
                    s_wld.inc(eng.dma_start(out=b1ks[:, :], in_=dram["b1k"].ap()), 16)
                    s_wld.inc(eng.dma_start(out=b1ins[:, :], in_=dram["b1in"].ap()), 16)
                    s_wld.inc(eng.dma_start(out=bvrow[:, :], in_=dram["b1vrow"].ap()), 16)
                    s_wld.inc(eng.dma_start(out=bd1s[:, :], in_=dram["bd1"].ap()), 16)
                    src2 = dram["bd2"].ap().rearrange("(c p) n -> p c n", p=128)
                    d2b = bd2s[:, :].rearrange("p (c n) -> p c n", c=NCH)
                    s_wld.inc(eng.dma_start(out=d2b, in_=src2), 16)
                    s_wld.inc(eng.dma_start(out=w1os[:, :], in_=dram["w1out"].ap()), 16)
                    s_wld.inc(eng.dma_start(out=b1fs[:, :], in_=dram["b1f"].ap()), 16)
                    s_wld.inc(eng.dma_start(out=b1b2s[:, :], in_=dram["b1b2"].ap()), 16)
                    s_wld.inc(eng.dma_start(out=b1outs[:, :], in_=dram["b1out"].ap()), 16)
                    chains[en].flush(eng)

                @blk.vector
                def _(eng):
                    en = "vector"
                    eng = CE(eng, chains[en])
                    bwait(eng, en)
                    s_init.inc(eng.memset(ones128[:, :], 1.0))
                    s_init.inc(eng.memset(onesrow[:, :], 1.0))
                    s_init.inc(eng.memset(onesrowb[:, :], 1.0))
                    for h in range(2):
                        s_mmbv.wait(eng, en, h + 1)
                        inst = eng.tensor_copy(bvbc[:, h * 512:(h + 1) * 512], psP[h][:, :])
                        s_evbv.inc(inst)
                    for c in range(NCH):
                        s_mmvw.wait(eng, en, c + 1)
                        inst = eng.tensor_tensor(vselw[:, c * 128:(c + 1) * 128],
                                                 hbc[:, c * 128:(c + 1) * 128],
                                                 psP[c % 2][0:16, 0:128], ALU.mult)
                        s_vw.inc(inst)

                    def v_x1chain(j):
                        s_ld.wait_done(eng, en, j)
                        s_ldg.wait_done(eng, en, j)
                        s_x1st.wait_done(eng, en, j - 2)
                        s_cast.wait(eng, en, 16 * (j - 1))
                        for c in range(NCH):
                            s_mmP.wait(eng, en, 8 * j + c + 1)
                            inst = eng.tensor_tensor(t1scr[:, :], psP[c % 2][:, :], g0s(j, c),
                                                     ALU.mult)
                            s_sa.inc(inst)
                            inst = eng.tensor_tensor(x1s(j, c), t1scr[:, :], xb(j, c), ALU.add)
                        s_x1.inc(inst)

                    def v_rows(j):
                        s_mmstat.wait(eng, en, 8 * (j + 1))
                        s_bc.wait(eng, en, j)
                        eng.tensor_scalar(row(mrow, j), psLN1[0:1, :], 1.0 / D, None, ALU.mult)
                        eng.tensor_scalar(row(exrow, j), psLN2[0:1, :], 1.0 / D, None, ALU.mult)
                        eng.tensor_tensor(row(vrow, j), row(mrow, j), row(mrow, j), ALU.mult)
                        eng.tensor_tensor(row(vrow, j), row(exrow, j), row(vrow, j), ALU.subtract)
                        inst = eng.tensor_scalar(row(vrow, j), row(vrow, j), EPS, None, ALU.add)
                        s_rowd.inc(inst)
                        s_sqr.wait(eng, en, j + 1)
                        eng.reciprocal(row(vrow, j), srow[:, :])
                        inst = eng.tensor_copy(row(rrow, j), row(vrow, j))
                        s_rowg.inc(inst)

                    v_x1chain(0)
                    v_rows(0)
                    for i in range(NT):
                        s_bc.wait(eng, en, i + 1)
                        s_mmW1.wait(eng, en, i)
                        for c in range(NCH):
                            eng.tensor_tensor(tmpscr[:, :], x1s(i, c), psLN1[:, :], ALU.subtract)
                            inst = eng.tensor_tensor(xn1[:, c * FT:(c + 1) * FT], tmpscr[:, :],
                                                     psLN2[:, :], ALU.mult)
                        s_xn1.inc(inst)
                        if i + 1 < NT:
                            v_x1chain(i + 1)
                        s_vst.wait_done(eng, en, i - 1)
                        for g in range(4):
                            for eh in range(2):
                                n = 2 * g + eh
                                s_mmV.wait(eng, en, 8 * i + n + 1)
                                inst = eng.tensor_tensor(
                                    vstage[:, g * D + eh * 512:g * D + eh * 512 + 512],
                                    psV[:, :], bvbc[:, eh * 512:(eh + 1) * 512], ALU.add)
                                s_vev.inc(inst)
                        if i + 1 < NT:
                            v_rows(i + 1)
                        if i >= 1:
                            k = i - 1
                            s_gst.wait_done(eng, en, k - 1)
                            for c in range(NCH):
                                s_sg.wait(eng, en, 8 * k + c + 1)
                                eng.tensor_scalar(fscr[:, :], psP[0][:, :], b1outs[:, c:c + 1],
                                                  None, ALU.add)
                                inst = eng.tensor_tensor(gstage[:, c * FT:(c + 1) * FT],
                                                         fscr[:, :],
                                                         sgscr[:, (c % 2) * FT:(c % 2 + 1) * FT],
                                                         ALU.mult)
                                s_ep.inc(inst)
                        s_mmW1.wait(eng, en, i + 1)
                        s_mmE.wait(eng, en, 8 * (i - 1))
                        inst = eng.tensor_scalar(row(s1stage, i), psW1[:, :], b1ins[:, :],
                                                 None, ALU.add)
                        s_s1ev.inc(inst)
                    k = NT - 1
                    s_gst.wait_done(eng, en, k - 1)
                    for c in range(NCH):
                        s_sg.wait(eng, en, 8 * k + c + 1)
                        eng.tensor_scalar(fscr[:, :], psP[0][:, :], b1outs[:, c:c + 1],
                                          None, ALU.add)
                        inst = eng.tensor_tensor(gstage[:, c * FT:(c + 1) * FT], fscr[:, :],
                                                 sgscr[:, (c % 2) * FT:(c % 2 + 1) * FT],
                                                 ALU.mult)
                        s_ep.inc(inst)
                    chains[en].flush(eng)

                @blk.scalar
                def _(eng):
                    en = "scalar"
                    eng = CE(eng, chains[en])
                    bwait(eng, en)

                    def s_casts(j):
                        s_x1.wait(eng, en, j + 1)
                        for c in range(NCH):
                            s_mmstat.wait(eng, en, 8 * j + c - 3)
                            sl = (c % 4) * FT
                            s_cast.inc(eng.copy(x1bf[:, sl:sl + FT], x1s(j, c)))
                            s_cast.inc(eng.activation(out=sqbf[:, sl:sl + FT],
                                                      in_=x1s(j, c), func=AF.Square))

                    def s_sqrt(j):
                        s_rowd.wait(eng, en, j + 1)
                        s_sqr.inc(eng.activation(out=srow[:, :], in_=row(vrow, j), func=AF.Sqrt))

                    def s_ffn(k):
                        for c in range(NCH):
                            s_mmE.wait(eng, en, 8 * k + c + 1)
                            s_mmG.wait(eng, en, 8 * k + c - 1)
                            inst = eng.activation(out=hstage[:, (c % 2) * FT:(c % 2 + 1) * FT],
                                                  in_=psP[0][:, :], func=AF.Relu,
                                                  bias=b1fs[:, c:c + 1], scale=1.0)
                            s_h.inc(inst)
                        s_mmG.wait(eng, en, 8 * (k + 1))
                        s_mmW.wait(eng, en, 8 * k)
                        inst = eng.activation(out=g1sb[:, :], in_=psP[1][0:64, :], func=AF.Identity,
                                              bias=b1b2s[:, :], scale=1.0)
                        s_g1.inc(inst)

                    def s_sgs(k):
                        for c in range(NCH):
                            s_mmW.wait(eng, en, 8 * k + c + 1)
                            s_ep.wait(eng, en, 8 * k + c - 1)
                            inst = eng.activation(out=sgscr[:, (c % 2) * FT:(c % 2 + 1) * FT],
                                                  in_=psP[0][:, :], func=AF.Sigmoid,
                                                  bias=b1outs[:, c:c + 1], scale=1.0)
                            s_sg.inc(inst)

                    s_casts(0)
                    s_sqrt(0)
                    for i in range(NT):
                        for o in range(NCH):
                            s_mmQ.wait(eng, en, 8 * i + o + 1)
                            s_qst.wait_done(eng, en, i - 1)
                            inst = eng.activation(out=qstage[:, o * FT:(o + 1) * FT],
                                                  in_=psF[o % 2][:, :], func=AF.Identity,
                                                  bias=b1qs[:, o:o + 1], scale=1.0)
                            s_qev.inc(inst)
                        for o in range(NCH):
                            s_mmK.wait(eng, en, 8 * i + o + 1)
                            s_kst.wait_done(eng, en, i - 1)
                            inst = eng.activation(out=kstage[:, o * FT:(o + 1) * FT],
                                                  in_=psF[o % 2][:, :], func=AF.Identity,
                                                  bias=b1ks[:, o:o + 1], scale=1.0)
                            s_kev.inc(inst)
                        if i + 1 < NT:
                            s_casts(i + 1)
                        if i >= 1:
                            s_ffn(i - 1)
                        if i + 1 < NT:
                            s_sqrt(i + 1)
                        if i >= 1:
                            s_sgs(i - 1)
                    s_ffn(NT - 1)
                    s_sgs(NT - 1)
                    chains[en].flush(eng)

                @blk.tensor
                def _(eng):
                    en = "tensor"
                    bwait(eng, en)
                    s_wld.wait(eng, en, 16 * 15)
                    s_init.wait(eng, en, 3)
                    for h in range(2):
                        inst = eng.matmul(psP[h][:, :], onesrowb[:, :],
                                          bvrow[:, h * 512:(h + 1) * 512], start=True, stop=True)
                        s_mmbv.inc(inst)
                    for c in range(NCH):
                        if c < 2:
                            s_evbv.wait(eng, en, c + 1)
                        else:
                            s_vw.wait(eng, en, c - 1)
                        inst = eng.matmul(psP[c % 2][0:16, 0:128], onesrowb[0:1, 0:16],
                                          vsumt[0:1, c * 128:(c + 1) * 128], start=True, stop=True)
                        s_mmvw.inc(inst)

                    def t_bcastP(j):
                        s_ldp.wait_done(eng, en, j)
                        s_ep.wait(eng, en, 8 * (j - 2))
                        s_g1.wait(eng, en, j - 2)
                        for c in range(NCH):
                            if j == 0:
                                s_vw.wait(eng, en, 8)
                            s_sa.wait(eng, en, 8 * j + c - 1)
                            inst = eng.matmul(psP[c % 2][:, :], vselw[:, c * 128:(c + 1) * 128],
                                              pbuf[:, (j % 2) * FT:(j % 2 + 1) * FT],
                                              start=True, stop=True)
                            s_mmP.inc(inst)

                    def t_stats(j):
                        s_xn1.wait(eng, en, j)
                        for c in range(NCH):
                            s_cast.wait(eng, en, 16 * j + 2 * (c + 1))
                            sl = (c % 4) * FT
                            eng.matmul(psLN1[0:1, :], ones128[:, :], x1bf[:, sl:sl + FT],
                                       start=(c == 0), stop=(c == NCH - 1))
                            inst = eng.matmul(psLN2[0:1, :], ones128[:, :], sqbf[:, sl:sl + FT],
                                              start=(c == 0), stop=(c == NCH - 1))
                            s_mmstat.inc(inst)

                    def t_lnbcast(j):
                        s_rowg.wait(eng, en, j + 1)
                        eng.matmul(psLN1[:, :], onesrowb[:, :], row(mrow, j), start=True, stop=True)
                        inst = eng.matmul(psLN2[:, :], onesrowb[:, :], row(rrow, j),
                                          start=True, stop=True)
                        s_bc.inc(inst)

                    def t_ffn(k):
                        s_s1ev.wait(eng, en, k + 1)
                        for c in range(NCH):
                            if c == 0:
                                s_sa.wait(eng, en, min(8 * (k + 2), 8 * NT))
                                s_ep.wait(eng, en, 8 * k)
                            else:
                                s_h.wait(eng, en, 8 * k + c)
                            inst = eng.matmul(psP[0][:, :], bd1s[:, c * 128:(c + 1) * 128],
                                              s1stage[:, (k % 2) * FT:(k % 2 + 1) * FT],
                                              start=True, stop=True)
                            s_mmE.inc(inst)
                            if c >= 1:
                                cc = c - 1
                                s_h.wait(eng, en, 8 * k + cc + 1)
                                if cc == 0:
                                    s_g1.wait(eng, en, k)
                                inst = eng.matmul(psP[1][0:64, :], bd2s[:, cc * 64:(cc + 1) * 64],
                                                  hstage[:, (cc % 2) * FT:(cc % 2 + 1) * FT],
                                                  start=(cc == 0), stop=False)
                                s_mmG.inc(inst)
                        cc = NCH - 1
                        s_h.wait(eng, en, 8 * k + cc + 1)
                        inst = eng.matmul(psP[1][0:64, :], bd2s[:, cc * 64:(cc + 1) * 64],
                                          hstage[:, (cc % 2) * FT:(cc % 2 + 1) * FT],
                                          start=False, stop=True)
                        s_mmG.inc(inst)

                    def t_wout(k):
                        s_g1.wait(eng, en, k + 1)
                        for c in range(NCH):
                            if c >= 1:
                                s_ep.wait(eng, en, 8 * k + c)
                            inst = eng.matmul(psP[0][:, :], w1os[:, c * 128:(c + 1) * 128],
                                              g1sb[:, :], start=True, stop=True)
                            s_mmW.inc(inst)

                    t_bcastP(0)
                    t_stats(0)
                    t_lnbcast(0)
                    for i in range(NT):
                        s_xn1.wait(eng, en, i + 1)
                        for o in range(NCH):
                            if o < 2:
                                s_kev.wait(eng, en, 8 * (i - 1) + o + 7)
                            else:
                                s_qev.wait(eng, en, 8 * i + o - 1)
                            for c in range(NCH):
                                inst = eng.matmul(psF[o % 2][:, :], wsl(w1q, c, o),
                                                  xn1[:, c * FT:(c + 1) * FT],
                                                  start=(c == 0), stop=(c == NCH - 1))
                            s_mmQ.inc(inst)
                        if i + 1 < NT:
                            t_bcastP(i + 1)
                        for o in range(NCH):
                            if o == 0:
                                s_qev.wait(eng, en, 8 * i + 7)
                            elif o == 1:
                                s_qev.wait(eng, en, 8 * i + 8)
                            else:
                                s_kev.wait(eng, en, 8 * i + o - 1)
                            for c in range(NCH):
                                inst = eng.matmul(psF[o % 2][:, :], wsl(w1k, c, o),
                                                  xn1[:, c * FT:(c + 1) * FT],
                                                  start=(c == 0), stop=(c == NCH - 1))
                            s_mmK.inc(inst)
                        if i + 1 < NT:
                            t_stats(i + 1)
                        if i >= 1:
                            t_ffn(i - 1)
                        for g in range(4):
                            for eh in range(2):
                                n = 2 * g + eh
                                s_vev.wait(eng, en, 8 * i + n)
                                for c in range(NCH):
                                    lhs = xn1[:, c * FT + g * 128:c * FT + (g + 1) * 128]
                                    rhs = w1v[:, c * D + eh * 512:c * D + eh * 512 + 512]
                                    inst = eng.matmul(psV[:, :], lhs, rhs,
                                                      start=(c == 0), stop=(c == NCH - 1))
                                s_mmV.inc(inst)
                        if i >= 1:
                            t_wout(i - 1)
                        if i + 1 < NT:
                            t_lnbcast(i + 1)
                        s_s1ev.wait(eng, en, i)
                        for c in range(NCH):
                            inst = eng.matmul(psW1[:, :], w1i[:, c * 64:(c + 1) * 64],
                                              xn1[:, c * FT:(c + 1) * FT],
                                              start=(c == 0), stop=(c == NCH - 1))
                        s_mmW1.inc(inst)
                    t_ffn(NT - 1)
                    t_wout(NT - 1)

                @blk.sync
                def _(eng):
                    en = "sync"
                    bwait(eng, en)
                    for i in range(min(2, NT)):
                        s_ld.inc(chunked_load(eng, xbuf, (i % 2) * NCH * FT, xT, i * FT, FT), i)
                        s_ldp.inc(eng.dma_start(
                            out=pbuf[:, (i % 2) * FT:(i % 2 + 1) * FT],
                            in_=dram["probsH"].ap()[:, i * FT:(i + 1) * FT]), i)
                    s_ldg.inc(chunked_load(eng, g0buf, 0, g0T, 0, FT), 0)
                    for i in range(NT):
                        if i + 1 < NT:
                            s_x1.wait(eng, en, i + 1)
                            s_ldg.inc(chunked_load(eng, g0buf, 0, g0T, (i + 1) * FT, FT), i + 1)
                        s_x1.wait(eng, en, i + 1)
                        s_x1st.inc(chunked_store(eng, x1T, i * FT, FT, x1buf, (i % 2) * NCH * FT), i)
                        s_qev.wait(eng, en, 8 * (i + 1))
                        s_qst.inc(chunked_store(eng, qT, i * FT, FT, qstage, 0), i)
                        s_kev.wait(eng, en, 8 * (i + 1))
                        s_kst.inc(chunked_store(eng, kT, i * FT, FT, kstage, 0), i)
                        s_vev.wait(eng, en, 8 * (i + 1))
                        dst = v1.ap()[i * FT:(i + 1) * FT, :].rearrange("(g p) e -> p g e", p=128)
                        srcap = vstage[:, :].rearrange("p (g e) -> p g e", g=4)
                        s_vst.inc(eng.dma_start(out=dst, in_=srcap), i)
                        if i >= 1:
                            s_ep.wait(eng, en, 8 * i)
                            s_gst.inc(chunked_store(eng, gffT, (i - 1) * FT, FT, gstage, 0), i - 1)
                        if i + 2 < NT:
                            s_x1.wait(eng, en, i + 1)
                            s_ld.inc(chunked_load(eng, xbuf, (i % 2) * NCH * FT, xT,
                                                  (i + 2) * FT, FT), i + 2)
                            s_mmP.wait(eng, en, 8 * (i + 1))
                            s_ldp.inc(eng.dma_start(
                                out=pbuf[:, (i % 2) * FT:(i % 2 + 1) * FT],
                                in_=dram["probsH"].ap()[:, (i + 2) * FT:(i + 3) * FT]), i + 2)
                    s_ep.wait(eng, en, 8 * NT)
                    s_gst.inc(chunked_store(eng, gffT, (NT - 1) * FT, FT, gstage, 0), NT - 1)
                    s_x1st.wait_all(eng, en, NT)
                    s_qst.wait_all(eng, en, NT)
                    s_kst.wait_all(eng, en, NT)
                    s_vst.wait_all(eng, en, NT)
                    s_gst.wait_all(eng, en, NT)

            assert s_mmP.count == 8 * NT
            assert s_vev.count == 8 * NT
        return chains



def build_B(nc, E, dram, bwait, pool):
    x1T, qT, kT, v1, gffT, yT = (dram["x1T"], dram["qT"], dram["kT"], dram["v1"],
                                 dram["gffT"], dram["yT"])
    ctx = ExitStack()
    A = ctx.enter_context
    with ctx:
        qbuf = A(nc.sbuf_tensor("b_qbuf", [128, 2 * NCH * NQ], BF16))
        kbuf = A(nc.sbuf_tensor("b_kbuf", [128, 2 * NCH * KV], BF16))
        vring = A(nc.sbuf_tensor("b_vring", [128, SEG * D], BF16))
        pring = A(nc.sbuf_tensor("b_pring", [128, SEG * NQ], BF16))
        oacc = A(nc.sbuf_tensor("b_oacc", [128, 2 * NCH * NQ], F32))
        gffpan = A(nc.sbuf_tensor("b_gffpan", [128, 2 * NCH * NQ], BF16))
        x1pan = A(nc.sbuf_tensor("b_x1pan", [128, NCH * NQ], F32))
        outsb = A(nc.sbuf_tensor("b_outsb", [128, NCH * NQ], F32))
        rden = A(nc.sbuf_tensor("b_rden", [1, NQ], F32))
        maskbf = A(nc.sbuf_tensor("b_maskbf", [128, 4 * NQ], BF16))
        onescol = A(nc.sbuf_tensor("b_onescol", [128, 1], BF16))
        onesrow = A(nc.sbuf_tensor("b_onesrow", [1, 128], F32))

        psS = [A(nc.psum_tensor(f"b_psS{k}", [128, 512], F32)) for k in range(2)]
        psD = A(nc.psum_tensor("b_psD", [1, 512], F32))
        psO = [A(nc.psum_tensor(f"b_psO{k}", [128, 512], F32)) for k in range(4)]
        psBC = A(nc.psum_tensor("b_psBC", [128, 512], F32))

        s_k = SemGroup(nc, E, "b_k", 2)
        s_v = SemGroup(nc, E, "b_v", SEG)
        s_qld = SemGroup(nc, E, "b_qld", 2)
        s_x1ld = SemGroup(nc, E, "b_x1ld", 2)
        s_gld = SemGroup(nc, E, "b_gld", 2)
        s_wld = Sem(nc, E, "b_wld")
        s_yst = Sem(nc, E, "b_yst")
        s_init = pool.get()
        s_mmS = pool.get()
        s_exp = pool.get()
        s_msk = pool.get()
        s_tb = pool.get()
        s_swp = pool.get()
        s_oev = pool.get()
        s_mmD = pool.get()
        s_recD = pool.get()
        s_mmBc = pool.get()
        s_sat = pool.get()
        s_out = pool.get()
        chains = {"vector": Chain(pool.get(), "b_chv"), "scalar": Chain(pool.get(), "b_chs"),
                  "gpsimd": Chain(pool.get(), "b_chg")}

        def kslice(jg, c):
            o = (jg % 2) * NCH * KV + c * KV
            return kbuf[:, o:o + KV]

        def qslice(p, c):
            o = (p % 2) * NCH * NQ + c * NQ
            return qbuf[:, o:o + NQ]

        def vslice(jg, c):
            o = (jg % SEG) * D + c * 128
            return vring[:, o:o + 128]

        def pslice(jg):
            o = (jg % SEG) * NQ
            return pring[:, o:o + NQ]

        def oslice(p, c):
            o = ((p % 2) * NCH + c) * NQ
            return oacc[:, o:o + NQ]

        def gslice(p, c):
            o = ((p % 2) * NCH + c) * NQ
            return gffpan[:, o:o + NQ]

        def is_masked(p, j):
            return j >= tiles_in_panel(p) - 4

        nmask_before = {}
        cnt = 0
        for p in range(NP):
            for j in range(tiles_in_panel(p)):
                nmask_before[(p, j)] = cnt
                if is_masked(p, j):
                    cnt += 1

        def seg_tiles(p, s):
            J = tiles_in_panel(p)
            return list(range(s * SEG, min(J, (s + 1) * SEG)))

        with nc.Block() as blk:
            @blk.gpsimd
            def _(eng):
                en = "gpsimd"
                eng = CE(eng, chains[en])
                bwait(eng, en)
                s_wld.inc(eng.dma_start(out=maskbf[:, :], in_=dram["masks"].ap()), 16)
                for p in range(min(2, NP)):
                    s_qld.inc(chunked_load(eng, qbuf, (p % 2) * NCH * NQ, qT, p * NQ, NQ), p)
                    s_gld.inc(chunked_load(eng, gffpan, (p % 2) * NCH * NQ, gffT, p * NQ, NQ), p)
                s_x1ld.inc(chunked_load(eng, x1pan, 0, x1T, 0, NQ), 0)
                for p in range(NP):
                    if p + 1 < NP:
                        s_out.wait(eng, en, p + 1)
                        s_x1ld.inc(chunked_load(eng, x1pan, 0, x1T, (p + 1) * NQ, NQ), p + 1)
                    if p + 2 < NP:
                        s_mmS.wait(eng, en, tiles_before_panel(p + 1))
                        s_qld.inc(chunked_load(eng, qbuf, ((p + 2) % 2) * NCH * NQ, qT,
                                               (p + 2) * NQ, NQ), p + 2)
                        s_out.wait(eng, en, p + 1)
                        s_gld.inc(chunked_load(eng, gffpan, ((p + 2) % 2) * NCH * NQ, gffT,
                                               (p + 2) * NQ, NQ), p + 2)
                    s_out.wait(eng, en, p + 1)
                    s_yst.inc(chunked_store(eng, yT, p * NQ, NQ, outsb, 0), 16)
                s_yst.wait(eng, en, 16 * NP)

            @blk.sync
            def _(eng):
                en = "sync"
                bwait(eng, en)
                for p in range(NP):
                    jg0 = tiles_before_panel(p)
                    for j in range(tiles_in_panel(p)):
                        jg = jg0 + j
                        kv0 = j * KV
                        s_mmS.wait(eng, en, jg - 1)
                        srcap = kT.ap()[:, kv0:kv0 + KV].rearrange("(c p) n -> p c n", p=128)
                        dst = kbuf[:, (jg % 2) * NCH * KV:((jg % 2) + 1) * NCH * KV].rearrange(
                            "p (c n) -> p c n", c=NCH)
                        s_k.inc(eng.dma_start(out=dst, in_=srcap), jg)
                        s_tb.wait(eng, en, jg - (SEG - 1))
                        s_v.inc(eng.dma_start(out=vring[:, (jg % SEG) * D:((jg % SEG) + 1) * D],
                                              in_=v1.ap()[kv0:kv0 + KV, :]), jg)

            @blk.tensor
            def _(eng):
                en = "tensor"
                bwait(eng, en)
                s_init.wait(eng, en, 2)
                sg_global = 0
                for p in range(NP):
                    jg0 = tiles_before_panel(p)
                    J = tiles_in_panel(p)
                    nseg = segs_in_panel(p)
                    s_qld.wait_done(eng, en, p)

                    def emit_S(j):
                        jg = jg0 + j
                        s_k.wait_done(eng, en, jg)
                        s_exp.wait(eng, en, jg - 1)
                        for c in range(NCH):
                            inst = eng.matmul(psS[jg % 2][:, :], kslice(jg, c), qslice(p, c),
                                              start=(c == 0), stop=(c == NCH - 1))
                        s_mmS.inc(inst)

                    for s in range(nseg):
                        sg = sg_global
                        tiles = seg_tiles(p, s)
                        if s == 0:
                            emit_S(tiles[0])
                        for idx, j in enumerate(tiles):
                            jg = jg0 + j
                            if idx + 1 < len(tiles):
                                emit_S(tiles[idx + 1])
                            elif (s + 1) * SEG < J:
                                emit_S((s + 1) * SEG)
                            if is_masked(p, j):
                                s_msk.wait(eng, en, nmask_before[(p, j)] + 1)
                            else:
                                s_exp.wait(eng, en, jg + 1)
                            if j == 0:
                                s_recD.wait(eng, en, p)
                            inst = eng.matmul(psD[:, :], onescol[:, :], pslice(jg),
                                              start=(j == 0), stop=(j == J - 1))
                            if j == J - 1:
                                s_mmD.inc(inst)
                            s_v.wait_done(eng, en, jg)
                            if idx == 0:
                                s_oev.wait(eng, en, 8 * sg)
                            for c in range(4):
                                inst = eng.matmul(psO[c][:, :], vslice(jg, c), pslice(jg),
                                                  start=(idx == 0), stop=(idx == len(tiles) - 1))
                            if idx == len(tiles) - 1:
                                s_swp.inc(inst)
                        s_oev.wait(eng, en, 8 * sg + 4)
                        for idx, j in enumerate(tiles):
                            jg = jg0 + j
                            for c in range(4, 8):
                                inst = eng.matmul(psO[c - 4][:, :], vslice(jg, c), pslice(jg),
                                                  start=(idx == 0), stop=(idx == len(tiles) - 1))
                            s_tb.inc(inst)
                        sg_global += 1
                    s_recD.wait(eng, en, p + 1)
                    s_sat.wait(eng, en, p)
                    inst = eng.matmul(psBC[:, :], onesrow[:, :], rden[:, :], start=True, stop=True)
                    s_mmBc.inc(inst)

            @blk.scalar
            def _(eng):
                en = "scalar"
                eng = CE(eng, chains[en])
                bwait(eng, en)
                s_wld.wait(eng, en, 16)
                for p in range(NP):
                    jg0 = tiles_before_panel(p)
                    for j in range(tiles_in_panel(p)):
                        jg = jg0 + j
                        s_mmS.wait(eng, en, jg + 1)
                        s_tb.wait(eng, en, jg - (SEG - 1))
                        inst = eng.activation(out=pslice(jg), in_=psS[jg % 2][:, :],
                                              func=AF.Exp, scale=ISQ)
                        s_exp.inc(inst)

            @blk.vector
            def _(eng):
                en = "vector"
                eng = CE(eng, chains[en])
                bwait(eng, en)
                s_wld.wait(eng, en, 16)
                s_init.inc(eng.memset(onescol[:, :], 1.0))
                s_init.inc(eng.memset(onesrow[:, :], 1.0))
                sg_global = 0
                for p in range(NP):
                    jg0 = tiles_before_panel(p)
                    J = tiles_in_panel(p)
                    nseg = segs_in_panel(p)
                    for s in range(nseg):
                        sg = sg_global
                        tiles = seg_tiles(p, s)
                        for j in tiles:
                            if is_masked(p, j):
                                jg = jg0 + j
                                jd = j - (J - 4)
                                s_exp.wait(eng, en, jg + 1)
                                inst = eng.tensor_tensor(pslice(jg), pslice(jg),
                                                         maskbf[:, jd * NQ:(jd + 1) * NQ],
                                                         ALU.mult)
                                s_msk.inc(inst)
                        s_swp.wait(eng, en, sg + 1)
                        for c in range(4):
                            dst = oslice(p, c)
                            if s == 0:
                                inst = eng.tensor_copy(dst, psO[c][:, :])
                            else:
                                inst = eng.tensor_tensor(dst, dst, psO[c][:, :], ALU.add)
                            s_oev.inc(inst)
                        s_tb.wait(eng, en, jg0 + tiles[-1] + 1)
                        for c in range(4, 8):
                            dst = oslice(p, c)
                            if s == 0:
                                inst = eng.tensor_copy(dst, psO[c - 4][:, :])
                            else:
                                inst = eng.tensor_tensor(dst, dst, psO[c - 4][:, :], ALU.add)
                            s_oev.inc(inst)
                        sg_global += 1
                    s_mmD.wait(eng, en, p + 1)
                    inst = eng.reciprocal(rden[:, :], psD[0:1, 0:NQ])
                    s_recD.inc(inst)
                    s_mmBc.wait(eng, en, p + 1)
                    s_x1ld.wait_done(eng, en, p)
                    s_gld.wait_done(eng, en, p)
                    s_yst.wait(eng, en, 16 * p)
                    for c in range(NCH):
                        inst = eng.tensor_tensor(oslice(p, c), oslice(p, c), psBC[:, :], ALU.mult)
                        if c == NCH - 1:
                            s_sat.inc(inst)
                        eng.tensor_tensor(outsb[:, c * NQ:(c + 1) * NQ], oslice(p, c),
                                          gslice(p, c), ALU.mult)
                        inst = eng.tensor_tensor(outsb[:, c * NQ:(c + 1) * NQ],
                                                 outsb[:, c * NQ:(c + 1) * NQ],
                                                 x1pan[:, c * NQ:(c + 1) * NQ], ALU.add)
                    s_out.inc(inst)
                chains[en].flush(eng)

        assert s_exp.count == TOTAL_KV
        assert s_out.count == NP
        return chains



def kernel(**inputs):
    from concourse.bass_utils import run_bass_kernel_spmd
    C = prep_common(inputs)
    xTs = prep_percore(inputs)
    nc, input_names = make_program("FUSED")
    in_maps = []
    for c in range(8):
        m = {}
        for n in input_names:
            m[n] = xTs[c] if n == "xT" else C[n]
        in_maps.append(m)
    res = run_bass_kernel_spmd(nc, in_maps, core_ids=list(range(8)))
    out = np.stack([np.asarray(res.results[c]["yT"]).T for c in range(8)])
    return np.ascontiguousarray(out.astype(np.float32))


# version 10
# speedup vs baseline: 1.0979x; 1.0919x over previous
import numpy as np
import concourse.bass as bass
import concourse.mybir as mybir
from contextlib import ExitStack

F32 = mybir.dt.float32
BF16 = mybir.dt.bfloat16
AF = mybir.ActivationFunctionType
ALU = mybir.AluOpType

T = 8192
D = 1024
NCH = 8
FT = 512
NT = T // FT
NQ = 512
NP = T // NQ
KV = 128
SEG = 16
EPS = 1e-5
ISQ = 1.0 / 32.0


def tiles_in_panel(p):
    return 4 * p + 4


def tiles_before_panel(p):
    return 2 * p * p + 2 * p


def segs_in_panel(p):
    return (tiles_in_panel(p) + SEG - 1) // SEG


def segs_before_panel(p):
    return sum(segs_in_panel(q) for q in range(p))


TOTAL_KV = tiles_before_panel(NP)


LASTINC = {}
KEEPALIVE = []


class Sem:
    def __init__(self, nc, E, name):
        self.h = E(nc.semaphore(name))
        self.name = name
        self.count = 0
        self.hw = {}

    def inc(self, inst, n=1):
        inst.then_inc(self.h, n)
        self.count += n
        KEEPALIVE.append(inst)
        LASTINC[id(inst)] = (self, self.count)

    def wait(self, eng, ename, val):
        if val <= 0:
            return
        if self.hw.get(ename, 0) >= val:
            return
        self.hw[ename] = val
        eng.wait_ge(self.h, val)


class PhaseSem:

    def __init__(self, s):
        self.s = s
        self.base = s.count
        self.name = s.name

    def inc(self, inst, n=1):
        self.s.inc(inst, n)

    @property
    def count(self):
        return self.s.count - self.base

    def wait(self, eng, ename, val):
        if val <= 0:
            return
        self.s.wait(eng, ename, self.base + val)


class SemPool:
    def __init__(self, nc, E):
        self.nc = nc
        self.E = E
        self.pool = []
        self.idx = 0

    def phase_reset(self):
        self.idx = 0

    def get(self, name=None):
        if self.idx == len(self.pool):
            self.pool.append(Sem(self.nc, self.E, f"pool{self.idx}"))
        s = self.pool[self.idx]
        self.idx += 1
        return PhaseSem(s)


class Chain:

    def __init__(self, sem, name):
        self.sem = sem
        self.name = name
        self.last = None
        self.pending = None

    def before(self, eng):
        if self.pending is not None:
            inst = self.pending
            li = LASTINC.get(id(inst))
            if li is None:
                self.sem.inc(inst, 1)
                li = (self.sem, self.sem.count)
            self.last = li
            self.pending = None
        if self.last is not None:
            sem, val = self.last
            sem.wait(eng, self.name, val)

    def after(self, inst):
        self.pending = inst

    def flush(self, eng):
        self.before(eng)


class SemGroup:

    def __init__(self, nc, E, name, n, per=16):
        self.sems = [Sem(nc, E, f"{name}{k}") for k in range(n)]
        self.n = n
        self.per = per

    def inc(self, inst, idx):
        self.sems[idx % self.n].inc(inst, self.per)

    def wait_done(self, eng, en, idx):
        if idx < 0:
            return
        self.sems[idx % self.n].wait(eng, en, self.per * (idx // self.n + 1))

    def wait_all(self, eng, en, total):
        for idx in range(max(0, total - self.n), total):
            self.wait_done(eng, en, idx)


COMPUTE_OPS = {"tensor_tensor", "tensor_scalar", "tensor_copy", "tensor_reduce",
               "memset", "reciprocal", "activation", "copy", "add", "mul", "iota",
               "reduce_sum", "reduce_max", "select", "bn_stats"}


class CE:

    def __init__(self, eng, chain):
        self._e = eng
        self._c = chain

    def __getattr__(self, a):
        f = getattr(self._e, a)
        if a in COMPUTE_OPS:
            def g(*args, **kw):
                self._c.before(self._e)
                inst = f(*args, **kw)
                KEEPALIVE.append(inst)
                self._c.after(inst)
                return inst
            return g
        return f



def prep_common(I):
    C = {}
    C["w0rT"] = np.ascontiguousarray(I["L0_ffn_W_r"].T)
    C["w0kT"] = np.ascontiguousarray(I["L0_ffn_W_k"].T)
    C["w0vT"] = np.ascontiguousarray(I["L0_ffn_W_v"].T)
    C["wq0"] = np.ascontiguousarray(I["L0_attn_Wq"])
    C["wk0"] = np.ascontiguousarray(I["L0_attn_Wk"])
    C["wv0T"] = np.ascontiguousarray(I["L0_attn_Wv"].T)
    C["mur"] = np.ascontiguousarray(np.asarray(I["L0_ffn_mu_r"]).reshape(NCH, 128).T)
    C["muk"] = np.ascontiguousarray(np.asarray(I["L0_ffn_mu_k"]).reshape(NCH, 128).T)
    C["w1qT"] = np.ascontiguousarray(I["L1_attn_Wq"].T)
    C["w1kT"] = np.ascontiguousarray(I["L1_attn_Wk"].T)
    C["w1vT"] = np.ascontiguousarray(I["L1_attn_Wv"].T)
    C["b1q"] = np.ascontiguousarray(np.asarray(I["L1_attn_bq"]).reshape(NCH, 128).T)
    C["b1k"] = np.ascontiguousarray(np.asarray(I["L1_attn_bk"]).reshape(NCH, 128).T)
    C["b1vrow"] = np.ascontiguousarray(np.asarray(I["L1_attn_bv"]).reshape(1, D))
    C["w1in"] = np.ascontiguousarray(I["L1_ffn_Win"])
    C["b1in"] = np.ascontiguousarray(np.asarray(I["L1_ffn_bin_"]).reshape(64, 1))
    W1 = np.asarray(I["L1_ffn_W1"])
    bd1 = np.zeros((64, 1024), np.float32)
    for m in range(64):
        bd1[m, 16 * m:16 * m + 16] = W1[m]
    C["bd1"] = bd1
    C["b1f"] = np.ascontiguousarray(
        np.asarray(I["L1_ffn_b1"]).reshape(64 * 16).reshape(NCH, 128).T)
    W2 = np.asarray(I["L1_ffn_W2"])
    bd2 = np.zeros((1024, 64), np.float32)
    for m in range(64):
        bd2[16 * m:16 * m + 16, m] = W2[m]
    C["bd2"] = bd2
    C["b1b2"] = np.ascontiguousarray(np.asarray(I["L1_ffn_b2"]).reshape(64, 1))
    C["w1out"] = np.ascontiguousarray(I["L1_ffn_Wout"])
    C["b1out"] = np.ascontiguousarray(np.asarray(I["L1_ffn_bout"]).reshape(NCH, 128).T)
    hselred = np.zeros((128, NCH * 16), np.float32)
    for c in range(NCH):
        for p in range(128):
            hselred[p, c * 16 + 2 * c + p // 64] = 1.0
    C["hselred"] = hselred
    hselbc = np.zeros((16, NCH * 128), np.float32)
    for c in range(NCH):
        for p in range(128):
            hselbc[2 * c + p // 64, c * 128 + p] = 1.0
    C["hselbc"] = hselbc
    tk = np.arange(128)[:, None]
    tq = np.arange(512)[None, :]
    masks = np.concatenate([(tq >= tk + 128 * j).astype(np.float32) for j in range(4)],
                           axis=1)
    C["masks"] = np.ascontiguousarray(masks)
    return {k: np.asarray(v, np.float32) for k, v in C.items()}


def prep_percore(I):
    x = np.asarray(I["L0_attn_x"])
    return [np.ascontiguousarray(x[c].T) for c in range(8)]


PHASE_SETS = {"A1": ("A1",), "A2": ("A2",), "B": ("B",), "FUSED": ("A1", "A2", "B")}

DRAM_SPECS = {
    "xT": ((D, T), F32, None, ("A1", "A2")),
    "w0rT": ((D, D), F32, None, ("A1",)),
    "w0kT": ((D, D), F32, None, ("A1",)),
    "w0vT": ((D, D), F32, None, ("A1",)),
    "wq0": ((64, 64), F32, None, ("A1",)),
    "wk0": ((64, 64), F32, None, ("A1",)),
    "wv0T": ((64, 64), F32, None, ("A1",)),
    "mur": ((128, NCH), F32, None, ("A1",)),
    "muk": ((128, NCH), F32, None, ("A1",)),
    "hselred": ((128, NCH * 16), F32, None, ("A1",)),
    "g0T": ((D, T), BF16, "A1", ("A2",)),
    "s0T": ((16, T), F32, "A1", ("A2",)),
    "vsumH": ((128, NCH), F32, "A1", ("A2",)),
    "vsumT": ((1, NCH * 128), F32, "A1", ("A2",)),
    "hselbc": ((16, NCH * 128), F32, None, ("A2",)),
    "w1qT": ((D, D), F32, None, ("A2",)),
    "w1kT": ((D, D), F32, None, ("A2",)),
    "w1vT": ((D, D), F32, None, ("A2",)),
    "b1q": ((128, NCH), F32, None, ("A2",)),
    "b1k": ((128, NCH), F32, None, ("A2",)),
    "b1vrow": ((1, D), F32, None, ("A2",)),
    "w1in": ((D, 64), F32, None, ("A2",)),
    "b1in": ((64, 1), F32, None, ("A2",)),
    "x1T": ((D, T), F32, "A2", ("B",)),
    "qT": ((D, T), BF16, "A2", ("B",)),
    "kT": ((D, T), BF16, "A2", ("B",)),
    "v1": ((T, D), BF16, "A2", ("B",)),
    "gffT": ((D, T), BF16, "A2", ("B",)),
    "masks": ((128, 4 * 512), F32, None, ("B",)),
    "bd1": ((64, 1024), F32, None, ("A2",)),
    "b1f": ((128, NCH), F32, None, ("A2",)),
    "bd2": ((1024, 64), F32, None, ("A2",)),
    "b1b2": ((64, 1), F32, None, ("A2",)),
    "w1out": ((64, 1024), F32, None, ("A2",)),
    "b1out": ((128, NCH), F32, None, ("A2",)),
    "yT": ((D, T), F32, "B", ()),
}


def chunked_load(eng, dst_sb, dst_off, src_dram, col0, ncol, nch=NCH):
    src = src_dram.ap()[:, col0:col0 + ncol].rearrange("(c p) n -> p c n", p=128)
    dst = dst_sb[:, dst_off:dst_off + nch * ncol].rearrange("p (c n) -> p c n", c=nch)
    return eng.dma_start(out=dst, in_=src)


def chunked_store(eng, dst_dram, col0, ncol, src_sb, src_off, nch=NCH):
    dst = dst_dram.ap()[:, col0:col0 + ncol].rearrange("(c p) n -> p c n", p=128)
    src = src_sb[:, src_off:src_off + nch * ncol].rearrange("p (c n) -> p c n", c=nch)
    return eng.dma_start(out=dst, in_=src)


def make_program(mode):
    phases = PHASE_SETS[mode]
    nc = bass.Bass("TRN2", target_bir_lowering=False)
    dram = {}
    input_names = []
    for name, (shape, dt, prod, cons) in DRAM_SPECS.items():
        used_c = any(p in phases for p in cons)
        prod_here = prod in phases
        if prod is None:
            if used_c:
                dram[name] = nc.dram_tensor(name, list(shape), dt, kind="ExternalInput")
                input_names.append(name)
        elif name == "yT":
            if prod_here:
                dram[name] = nc.dram_tensor(name, list(shape), dt, kind="ExternalOutput")
        elif prod_here and used_c:
            dram[name] = nc.dram_tensor(name, list(shape), dt)
        elif prod_here:
            dram[name] = nc.dram_tensor(name, list(shape), dt, kind="ExternalOutput")
        elif used_c:
            dram[name] = nc.dram_tensor(name, list(shape), dt, kind="ExternalInput")
            input_names.append(name)
    if "A2" in phases:
        dram["cc_in"] = nc.dram_tensor("cc_in", [16, T], F32)
        dram["cc_out"] = nc.dram_tensor("cc_out", [16, T], F32)
        dram["probsH"] = nc.dram_tensor("probsH", [16, T], BF16)

    with ExitStack() as octx:
        E = octx.enter_context
        bar = Sem(nc, E, "bar")
        pool = SemPool(nc, E)
        state = {"gen": 0}

        def barrier_end(chains=None):
            with nc.Block() as blk:
                for nm, engf in (("sync", blk.sync), ("tensor", blk.tensor),
                                 ("scalar", blk.scalar), ("vector", blk.vector),
                                 ("gpsimd", blk.gpsimd)):
                    @engf
                    def _(eng, nm=nm):
                        if chains and nm in chains and chains[nm].last is not None:
                            sem, val = chains[nm].last
                            sem.wait(eng, nm, val)
                        bar.inc(eng.nop(), 1)
            state["gen"] += 5

        def bwait(eng, ename):
            bar.wait(eng, ename, state["gen"])

        for ph in phases:
            pool.phase_reset()
            if ph == "A1":
                chains = build_A1(nc, E, dram, bwait, pool)
            elif ph == "A2":
                chains = build_A2(nc, E, dram, bwait, barrier_end, pool)
            else:
                chains = build_B(nc, E, dram, bwait, pool)
            barrier_end(chains)
    return nc, input_names



def build_A1(nc, E, dram, bwait, pool):
    xT, g0T, s0T, vsumH = dram["xT"], dram["g0T"], dram["s0T"], dram["vsumH"]

    ctx = ExitStack()
    A = ctx.enter_context
    with ctx:
        xbuf = A(nc.sbuf_tensor("a1_xbuf", [128, 2 * NCH * FT], F32))
        xbf = A(nc.sbuf_tensor("a1_xbf", [128, NCH * FT], BF16))
        sqbf = A(nc.sbuf_tensor("a1_sqbf", [128, NCH * FT], BF16))
        xn = A(nc.sbuf_tensor("a1_xn", [128, 2 * NCH * 513], BF16))
        xr = A(nc.sbuf_tensor("a1_xr", [128, 2 * NCH * FT], BF16))
        xk = A(nc.sbuf_tensor("a1_xk", [128, 2 * NCH * FT], BF16))
        dscr = A(nc.sbuf_tensor("a1_d", [128, FT], BF16))
        escr = A(nc.sbuf_tensor("a1_e", [128, FT], BF16))
        tmpscr = A(nc.sbuf_tensor("a1_tmp", [128, FT], F32))
        r_sb = A(nc.sbuf_tensor("a1_r", [128, NCH * FT], BF16))
        relu_sb = A(nc.sbuf_tensor("a1_relu", [128, 2 * FT], BF16))
        kact = A(nc.sbuf_tensor("a1_kact", [128, NCH * FT], BF16))
        kvprod = A(nc.sbuf_tensor("a1_kvp", [128, 2 * FT], BF16))
        g0sb = A(nc.sbuf_tensor("a1_g0", [128, 2 * NCH * FT], BF16))
        qksb = A(nc.sbuf_tensor("a1_qk", [128, 2 * FT], BF16))
        vp = A(nc.sbuf_tensor("a1_vp", [128, NCH], F32))
        vsum = A(nc.sbuf_tensor("a1_vsum", [128, NCH], F32))
        s0row = A(nc.sbuf_tensor("a1_s0row", [16, 2 * FT], F32))
        mrow = A(nc.sbuf_tensor("a1_mrow", [1, FT], BF16))
        exrow = A(nc.sbuf_tensor("a1_exrow", [1, FT], F32))
        vrow = A(nc.sbuf_tensor("a1_vrow", [1, FT], F32))
        rrow = A(nc.sbuf_tensor("a1_rrow", [1, FT], BF16))
        srow = A(nc.sbuf_tensor("a1_srow", [1, FT], F32))
        bmrow = A(nc.sbuf_tensor("a1_bmrow", [128, 2 * FT], BF16))
        brrow = A(nc.sbuf_tensor("a1_brrow", [128, 2 * FT], BF16))
        ones128 = A(nc.sbuf_tensor("a1_ones128", [128, 1], BF16))
        onesrow = A(nc.sbuf_tensor("a1_onesrow", [1, 128], BF16))
        w0r = A(nc.sbuf_tensor("a1_w0r", [128, NCH * D], BF16))
        w0k = A(nc.sbuf_tensor("a1_w0k", [128, NCH * D], BF16))
        w0v = A(nc.sbuf_tensor("a1_w0v", [128, NCH * D], BF16))
        wq0b = A(nc.sbuf_tensor("a1_wq0b", [64, 64], BF16))
        wk0b = A(nc.sbuf_tensor("a1_wk0b", [64, 64], BF16))
        wv0b = A(nc.sbuf_tensor("a1_wv0b", [64, 64], BF16))
        mscr = A(nc.sbuf_tensor("a1_mscr", [64, 64], BF16))
        Mbd = A(nc.sbuf_tensor("a1_Mbd", [128, 128], BF16))
        Vbd = A(nc.sbuf_tensor("a1_Vbd", [128, 128], BF16))
        murs = A(nc.sbuf_tensor("a1_murs", [128, NCH], F32))
        muks = A(nc.sbuf_tensor("a1_muks", [128, NCH], F32))
        hred = A(nc.sbuf_tensor("a1_hred", [128, NCH * 16], BF16))

        psLN1 = A(nc.psum_tensor("a1_psLN1", [128, 512], F32))
        psLN2 = A(nc.psum_tensor("a1_psLN2", [128, 512], F32))
        psF = [A(nc.psum_tensor(f"a1_psF{k}", [128, 512], F32)) for k in range(2)]
        psQ = [A(nc.psum_tensor(f"a1_psQ{k}", [128, 512], F32)) for k in range(2)]
        psS0 = A(nc.psum_tensor("a1_psS0", [16, 512], F32))
        psM = A(nc.psum_tensor("a1_psM", [64, 64], F32))

        s_ld = SemGroup(nc, E, "a1_ld", 2)
        s_wld = Sem(nc, E, "a1_wld")
        s_init = pool.get()
        s_mmM = pool.get()
        s_evM = pool.get()
        s_cast = pool.get()
        s_mmstat = pool.get()
        s_rowd = pool.get()
        s_sqr = pool.get()
        s_rowg = pool.get()
        s_bc = pool.get()
        s_bcevS = pool.get()
        s_xnLN = pool.get()
        s_mix = pool.get()
        s_mmR = pool.get()
        s_sig = pool.get()
        s_mmK = pool.get()
        s_rel = pool.get()
        s_kact = pool.get()
        s_mmV = pool.get()
        s_kvev = pool.get()
        s_silu = pool.get()
        s_rk = pool.get()
        s_mmU = pool.get()
        s_qk = pool.get()
        s_mmV0 = pool.get()
        s_mms0 = pool.get()
        s_vev = pool.get()
        s_vsadd = pool.get()
        s_xnfree = pool.get()
        s_s0ev = pool.get()
        s_g0st = SemGroup(nc, E, "a1_g0st", 2)
        s_s0st = SemGroup(nc, E, "a1_s0st", 2)
        s_fin = Sem(nc, E, "a1_fin")
        chains = {"vector": Chain(pool.get(), "a1_chv"), "scalar": Chain(pool.get(), "a1_chs"),
                  "gpsimd": Chain(pool.get(), "a1_chg")}

        def xb(i, c):
            o = ((i % 2) * NCH + c) * FT
            return xbuf[:, o:o + FT]

        def xns(i, c, a, b):
            o = ((i % 2) * NCH + c) * 513
            return xn[:, o + a:o + b]

        def xrs(i, c):
            o = ((i % 2) * NCH + c) * FT
            return xr[:, o:o + FT]

        def xks(i, c):
            o = ((i % 2) * NCH + c) * FT
            return xk[:, o:o + FT]

        def wsl(w, c, o):
            return w[:, c * D + o * 128:c * D + o * 128 + 128]

        def row(buf, i):
            if buf.shape[1] == FT:
                return buf[:, 0:FT]
            o = (i % 2) * FT
            return buf[:, o:o + FT]

        with nc.Block() as blk:
            @blk.gpsimd
            def _(eng):
                en = "gpsimd"
                eng = CE(eng, chains[en])
                bwait(eng, en)
                for nm, dst in (("wq0", wq0b), ("wk0", wk0b), ("wv0T", wv0b)):
                    s_wld.inc(eng.dma_start(out=dst[:, :], in_=dram[nm].ap()), 16)
                for nm, dst in (("mur", murs), ("muk", muks)):
                    s_wld.inc(eng.dma_start(out=dst[:, :], in_=dram[nm].ap()), 16)
                s_wld.inc(eng.dma_start(out=hred[:, :], in_=dram["hselred"].ap()), 16)
                for nm, dst in (("w0rT", w0r), ("w0kT", w0k), ("w0vT", w0v)):
                    src = dram[nm].ap().rearrange("(c p) n -> p c n", p=128)
                    d2 = dst[:, :].rearrange("p (c n) -> p c n", c=NCH)
                    s_wld.inc(eng.dma_start(out=d2, in_=src), 16)
                chains[en].flush(eng)

            @blk.vector
            def _(eng):
                en = "vector"
                eng = CE(eng, chains[en])
                bwait(eng, en)
                s_init.inc(eng.memset(Mbd[:, :], 0.0))
                s_init.inc(eng.memset(Vbd[:, :], 0.0))
                s_init.inc(eng.memset(ones128[:, :], 1.0))
                s_init.inc(eng.memset(onesrow[:, :], 1.0))
                s_init.inc(eng.memset(vp[:, :], 0.0))
                s_init.inc(eng.memset(vsum[:, :], 0.0))
                for c in range(NCH):
                    s_init.inc(eng.memset(xn[:, c * 513:c * 513 + 1], 0.0))
                s_evM.wait(eng, en, 1)
                s_init.inc(eng.tensor_copy(Mbd[0:64, 0:64], mscr[:, :]))
                s_init.inc(eng.tensor_copy(Mbd[64:128, 64:128], mscr[:, :]))
                s_wld.wait(eng, en, 144)
                s_init.inc(eng.tensor_copy(Vbd[0:64, 0:64], wv0b[:, :]))
                s_init.inc(eng.tensor_copy(Vbd[64:128, 64:128], wv0b[:, :]))

                def v_rows(i):
                    s_mmstat.wait(eng, en, 8 * (i + 1))
                    s_bc.wait(eng, en, i)
                    eng.tensor_scalar(row(mrow, i), psLN1[0:1, :], 1.0 / D, None, ALU.mult)
                    eng.tensor_scalar(row(exrow, i), psLN2[0:1, :], 1.0 / D, None, ALU.mult)
                    eng.tensor_tensor(row(vrow, i), row(mrow, i), row(mrow, i), ALU.mult)
                    eng.tensor_tensor(row(vrow, i), row(exrow, i), row(vrow, i), ALU.subtract)
                    inst = eng.tensor_scalar(row(vrow, i), row(vrow, i), EPS, None, ALU.add)
                    s_rowd.inc(inst)
                    s_sqr.wait(eng, en, i + 1)
                    eng.reciprocal(row(vrow, i), srow[:, :])
                    inst = eng.tensor_copy(row(rrow, i), row(vrow, i))
                    s_rowg.inc(inst)

                def v_xn(i):
                    s_bcevS.wait(eng, en, i + 1)
                    s_xnfree.wait(eng, en, i - 1)
                    sl = (i % 2) * FT
                    for c in range(NCH):
                        eng.tensor_tensor(tmpscr[:, :], xb(i, c), bmrow[:, sl:sl + FT], ALU.subtract)
                        inst = eng.tensor_tensor(xns(i, c, 1, 513), tmpscr[:, :],
                                                 brrow[:, sl:sl + FT], ALU.mult)
                    s_xnLN.inc(inst)
                    if i + 1 < NT:
                        for c in range(NCH):
                            eng.tensor_copy(xns(i + 1, c, 0, 1), xns(i, c, 512, 513))

                def v_mix(i):
                    s_rk.wait(eng, en, i - 1)
                    for c in range(NCH):
                        eng.tensor_tensor(dscr[:, :], xns(i, c, 1, 513), xns(i, c, 0, 512), ALU.subtract)
                        eng.tensor_scalar(escr[:, :], dscr[:, :], murs[:, c:c + 1], None, ALU.mult)
                        eng.tensor_tensor(xrs(i, c), xns(i, c, 0, 512), escr[:, :], ALU.add)
                        eng.tensor_scalar(escr[:, :], dscr[:, :], muks[:, c:c + 1], None, ALU.mult)
                        inst = eng.tensor_tensor(xks(i, c), xns(i, c, 0, 512), escr[:, :], ALU.add)
                    s_mix.inc(inst)

                def v_vsum(i):
                    s_vev.wait(eng, en, 8 * (i + 1))
                    inst = eng.tensor_tensor(vsum[:, :], vsum[:, :], vp[:, :], ALU.add)
                    s_vsadd.inc(inst)

                v_rows(0)
                v_xn(0)
                v_mix(0)
                for i in range(NT):
                    if i + 1 < NT:
                        v_rows(i + 1)
                        v_xn(i + 1)
                    if i >= 1:
                        v_vsum(i - 1)
                    if i + 1 < NT:
                        v_mix(i + 1)
                    for o in range(NCH):
                        s_rel.wait(eng, en, 8 * i + o + 1)
                        inst = eng.tensor_tensor(kact[:, o * FT:(o + 1) * FT], psF[o % 2][:, :],
                                                 relu_sb[:, (o % 2) * FT:(o % 2 + 1) * FT], ALU.mult)
                        s_kact.inc(inst)
                    for o in range(NCH):
                        s_mmV.wait(eng, en, 8 * i + o + 1)
                        s_silu.wait(eng, en, 8 * i + o - 1)
                        inst = eng.tensor_tensor(kvprod[:, (o % 2) * FT:(o % 2 + 1) * FT],
                                                 psF[o % 2][:, :], r_sb[:, o * FT:(o + 1) * FT], ALU.mult)
                        s_kvev.inc(inst)
                        s_mmU.wait(eng, en, 8 * i + o + 1)
                        s_mms0.wait(eng, en, 8 * i + o - 1)
                        inst = eng.tensor_tensor(qksb[:, (o % 2) * FT:(o % 2 + 1) * FT],
                                                 xns(i, o, 1, 513), psQ[o % 2][:, :], ALU.mult)
                        s_qk.inc(inst)
                v_vsum(NT - 1)
                chains[en].flush(eng)

            @blk.scalar
            def _(eng):
                en = "scalar"
                eng = CE(eng, chains[en])
                bwait(eng, en)
                s_mmM.wait(eng, en, 1)
                s_evM.inc(eng.copy(mscr[:, :], psM[:, :]))

                def s_casts(j):
                    s_ld.wait_done(eng, en, j)
                    for c in range(NCH):
                        s_mmstat.wait(eng, en, 8 * j + c - 7)
                        sl = c * FT
                        s_cast.inc(eng.copy(xbf[:, sl:sl + FT], xb(j, c)))
                        s_cast.inc(eng.activation(out=sqbf[:, sl:sl + FT], in_=xb(j, c),
                                                  func=AF.Square))

                def s_sqrt(j):
                    s_rowd.wait(eng, en, j + 1)
                    s_sqr.inc(eng.activation(out=srow[:, :], in_=row(vrow, j), func=AF.Sqrt))

                def s_bcev(j):
                    s_bc.wait(eng, en, j + 1)
                    sl = (j % 2) * FT
                    eng.copy(bmrow[:, sl:sl + FT], psLN1[:, :])
                    inst = eng.copy(brrow[:, sl:sl + FT], psLN2[:, :])
                    s_bcevS.inc(inst)

                s_casts(0)
                s_sqrt(0)
                s_bcev(0)
                if NT > 1:
                    s_casts(1)

                def s_vev1(k, c):
                    s_mmV0.wait(eng, en, 8 * k + c + 1)
                    s_vsadd.wait(eng, en, k)
                    inst = eng.activation(out=relu_sb[:, 0:FT], in_=psQ[c % 2][:, :],
                                          func=AF.Copy, accum_out=vp[:, c:c + 1])
                    s_vev.inc(inst)

                def s_s0(k):
                    s_mms0.wait(eng, en, 8 * (k + 1))
                    s_s0st.wait_done(eng, en, k - 2)
                    inst = eng.copy(row(s0row, k), psS0[:, :])
                    s_s0ev.inc(inst)

                for i in range(NT):
                    for o in range(NCH):
                        s_mmR.wait(eng, en, 8 * i + o + 1)
                        s_kvev.wait(eng, en, 8 * (i - 1) + o + 1)
                        inst = eng.activation(out=r_sb[:, o * FT:(o + 1) * FT], in_=psF[o % 2][:, :],
                                              func=AF.Sigmoid)
                        s_sig.inc(inst)
                        if i >= 1:
                            s_vev1(i - 1, o)
                    if i >= 1:
                        s_s0(i - 1)
                    if i + 2 < NT:
                        s_casts(i + 2)
                    if i + 1 < NT:
                        s_sqrt(i + 1)
                        s_bcev(i + 1)
                    for o in range(NCH):
                        s_mmK.wait(eng, en, 8 * i + o + 1)
                        s_kact.wait(eng, en, 8 * i + o - 1)
                        inst = eng.activation(out=relu_sb[:, (o % 2) * FT:(o % 2 + 1) * FT],
                                              in_=psF[o % 2][:, :], func=AF.Relu)
                        s_rel.inc(inst)
                    for o in range(NCH):
                        s_kvev.wait(eng, en, 8 * i + o + 1)
                        s_g0st.wait_done(eng, en, i - 2)
                        dst = g0sb[:, ((i % 2) * NCH + o) * FT:((i % 2) * NCH + o + 1) * FT]
                        inst = eng.activation(out=dst, in_=kvprod[:, (o % 2) * FT:(o % 2 + 1) * FT],
                                              func=AF.Silu)
                        s_silu.inc(inst)
                for c in range(NCH):
                    s_vev1(NT - 1, c)
                s_s0(NT - 1)
                chains[en].flush(eng)

            @blk.tensor
            def _(eng):
                en = "tensor"
                bwait(eng, en)
                s_wld.wait(eng, en, 144)
                inst = eng.matmul(psM[:, :], wk0b[:, :], wq0b[:, :], start=True, stop=True)
                s_mmM.inc(inst)
                s_init.wait(eng, en, 18)

                def t_stats(j):
                    s_bcevS.wait(eng, en, j)
                    for c in range(NCH):
                        s_cast.wait(eng, en, 16 * j + 2 * (c + 1))
                        sl = c * FT
                        eng.matmul(psLN1[0:1, :], ones128[:, :], xbf[:, sl:sl + FT],
                                   start=(c == 0), stop=(c == NCH - 1))
                        inst = eng.matmul(psLN2[0:1, :], ones128[:, :], sqbf[:, sl:sl + FT],
                                          start=(c == 0), stop=(c == NCH - 1))
                        s_mmstat.inc(inst)

                def t_bcast(j):
                    s_rowg.wait(eng, en, j + 1)
                    eng.matmul(psLN1[:, :], onesrow[:, :], row(mrow, j), start=True, stop=True)
                    inst = eng.matmul(psLN2[:, :], onesrow[:, :], row(rrow, j), start=True, stop=True)
                    s_bc.inc(inst)

                t_stats(0)
                t_bcast(0)

                def emit_v(k, c):
                    if c < 2:
                        s_qk.wait(eng, en, 8 * k + c + 7)
                    else:
                        s_vev.wait(eng, en, 8 * k + c - 1)
                    inst = eng.matmul(psQ[c % 2][:, :], Vbd[:, :], xns(k, c, 1, 513),
                                      start=True, stop=True)
                    s_mmV0.inc(inst)
                    if c == NCH - 1:
                        s_xnfree.inc(eng.nop())

                for i in range(NT):
                    if i + 1 < NT:
                        t_stats(i + 1)
                    s_mix.wait(eng, en, i + 1)
                    for o in range(NCH):
                        if o < 2:
                            s_kvev.wait(eng, en, 8 * (i - 1) + o + 7)
                        else:
                            s_sig.wait(eng, en, 8 * i + o - 1)
                        for c in range(NCH):
                            inst = eng.matmul(psF[o % 2][:, :], wsl(w0r, c, o), xrs(i, c),
                                              start=(c == 0), stop=(c == NCH - 1))
                        s_mmR.inc(inst)
                        if i >= 1:
                            emit_v(i - 1, o)
                    if i + 1 < NT:
                        t_bcast(i + 1)
                    for o in range(NCH):
                        if o == 0:
                            s_sig.wait(eng, en, 8 * i + 7)
                        elif o == 1:
                            s_sig.wait(eng, en, 8 * i + 8)
                        else:
                            s_kact.wait(eng, en, 8 * i + o - 1)
                        for c in range(NCH):
                            inst = eng.matmul(psF[o % 2][:, :], wsl(w0k, c, o), xks(i, c),
                                              start=(c == 0), stop=(c == NCH - 1))
                        s_mmK.inc(inst)
                        if o == NCH - 1:
                            s_rk.inc(eng.nop())
                    def emit_s0red(c):
                        s_qk.wait(eng, en, 8 * i + c + 1)
                        if c == 0:
                            s_s0ev.wait(eng, en, i)
                        inst = eng.matmul(psS0[:, :], hred[:, c * 16:(c + 1) * 16],
                                          qksb[:, (c % 2) * FT:(c % 2 + 1) * FT],
                                          start=(c == 0), stop=(c == NCH - 1))
                        s_mms0.inc(inst)

                    s_kact.wait(eng, en, 8 * (i + 1))
                    for o in range(NCH):
                        if o >= 2:
                            s_kvev.wait(eng, en, 8 * i + o - 1)
                        for c in range(NCH):
                            inst = eng.matmul(psF[o % 2][:, :], wsl(w0v, c, o),
                                              kact[:, c * FT:(c + 1) * FT],
                                              start=(c == 0), stop=(c == NCH - 1))
                        s_mmV.inc(inst)
                        if o < 2:
                            s_vev.wait(eng, en, 8 * i + o - 1)
                        else:
                            s_qk.wait(eng, en, 8 * i + o - 1)
                        inst = eng.matmul(psQ[o % 2][:, :], Mbd[:, :], xns(i, o, 1, 513),
                                          start=True, stop=True)
                        s_mmU.inc(inst)
                        if o >= 2:
                            emit_s0red(o - 2)
                    emit_s0red(NCH - 2)
                    emit_s0red(NCH - 1)
                for c in range(NCH):
                    emit_v(NT - 1, c)

            @blk.sync
            def _(eng):
                en = "sync"
                bwait(eng, en)
                for i in range(min(2, NT)):
                    s_ld.inc(chunked_load(eng, xbuf, (i % 2) * NCH * FT, xT, i * FT, FT), i)
                for i in range(NT):
                    if i + 2 < NT:
                        s_xnLN.wait(eng, en, i + 1)
                        s_ld.inc(chunked_load(eng, xbuf, (i % 2) * NCH * FT, xT, (i + 2) * FT, FT), i + 2)
                    s_silu.wait(eng, en, 8 * (i + 1))
                    s_g0st.inc(chunked_store(eng, g0T, i * FT, FT, g0sb, (i % 2) * NCH * FT), i)
                    s_s0ev.wait(eng, en, i + 1)
                    s_s0st.inc(eng.dma_start(out=s0T.ap()[:, i * FT:(i + 1) * FT],
                                             in_=row(s0row, i)), i)
                s_vsadd.wait(eng, en, NT)
                s_fin.inc(eng.dma_start(out=vsumH.ap(), in_=vsum[:, :]), 16)
                s_fin.inc(eng.dma_start(
                    out=dram["vsumT"].ap().rearrange("o (c p) -> (o p) c", p=128),
                    in_=vsum[:, :], allow_slow_non_contiguous=True), 16)
                s_fin.wait(eng, en, 32)
                s_g0st.wait_all(eng, en, NT)
                s_s0st.wait_all(eng, en, NT)

        assert s_mix.count == NT and s_mmstat.count == 8 * NT
        assert s_sig.count == 8 * NT and s_kact.count == 8 * NT
        return chains



def build_A2(nc, E, dram, bwait, barrier_end, pool):
    xT, g0T, s0T, vsumH = dram["xT"], dram["g0T"], dram["s0T"], dram["vsumH"]
    x1T, qT, kT, v1, gffT = dram["x1T"], dram["qT"], dram["kT"], dram["v1"], dram["gffT"]
    cc_in, cc_out = dram["cc_in"], dram["cc_out"]

    pctx = ExitStack()
    P = pctx.enter_context
    with pctx:
        vsumt = P(nc.sbuf_tensor("a2_vsumt", [1, NCH * 128], BF16))
        s_prob = pool.get()
        chains = {"vector": Chain(pool.get(), "a2_chv"), "scalar": Chain(pool.get(), "a2_chs"),
                  "gpsimd": Chain(pool.get(), "a2_chg")}

        cctx = ExitStack()
        C = cctx.enter_context
        with cctx:
            bufP = C(nc.sbuf_tensor("a2_bufP", [16, T], F32))
            bufQ = C(nc.sbuf_tensor("a2_bufQ", [16, T], F32))
            bufR = C(nc.sbuf_tensor("a2_bufR", [16, T], F32))
            probsb = C(nc.sbuf_tensor("a2_probsb", [16, T], BF16))
            s_cl = Sem(nc, E, "a2_cl")
            s_ex = pool.get()
            s_cc = Sem(nc, E, "a2_cc")
            with nc.Block() as blk:
                @blk.gpsimd
                def _(eng):
                    en = "gpsimd"
                    eng = CE(eng, chains[en])
                    bwait(eng, en)
                    s_cl.inc(eng.dma_start(out=vsumt[:, :], in_=dram["vsumT"].ap()), 16)
                    s_cl.inc(eng.dma_start(out=bufP[:, :], in_=s0T.ap()), 16)
                    s_ex.wait(eng, en, 1)
                    s_cl.inc(eng.dma_start(out=cc_in.ap(), in_=bufQ[:, :]), 16)
                    s_cl.wait(eng, en, 48)
                    inst = eng.collective_compute(
                        "AllReduce", ALU.add, replica_groups=[list(range(8))],
                        ins=[cc_in.ap().opt()], outs=[cc_out.ap().opt()])
                    s_cc.inc(inst, 1)
                    s_cc.wait(eng, en, 1)
                    s_cl.inc(eng.dma_start(out=bufR[:, :], in_=cc_out.ap()), 16)
                    s_prob.wait(eng, en, 1)
                    s_cl.inc(eng.dma_start(out=dram["probsH"].ap(), in_=probsb[:, :]), 16)
                    s_cl.wait(eng, en, 80)
                    chains[en].flush(eng)

                @blk.scalar
                def _(eng):
                    en = "scalar"
                    eng = CE(eng, chains[en])
                    bwait(eng, en)
                    s_cl.wait(eng, en, 32)
                    inst = eng.activation(out=bufQ[:, :], in_=bufP[:, :], func=AF.Exp, scale=ISQ)
                    s_ex.inc(inst)
                    chains[en].flush(eng)

                @blk.vector
                def _(eng):
                    en = "vector"
                    eng = CE(eng, chains[en])
                    bwait(eng, en)
                    s_cl.wait(eng, en, 64)
                    eng.reciprocal(bufP[:, :], bufR[:, :])
                    inst = eng.tensor_tensor(probsb[:, :], bufQ[:, :], bufP[:, :], ALU.mult)
                    s_prob.inc(inst)
                    chains[en].flush(eng)
            barrier_end(chains)

        ctx = ExitStack()
        A = ctx.enter_context
        with ctx:
            xbuf = A(nc.sbuf_tensor("a2_xbuf", [128, 2 * NCH * FT], F32))
            g0buf = A(nc.sbuf_tensor("a2_g0buf", [128, NCH * FT], BF16))
            x1buf = A(nc.sbuf_tensor("a2_x1buf", [128, 2 * NCH * FT], F32))
            x1bf = A(nc.sbuf_tensor("a2_x1bf", [128, 4 * FT], BF16))
            sqbf = A(nc.sbuf_tensor("a2_sqbf", [128, 4 * FT], BF16))
            xn1 = A(nc.sbuf_tensor("a2_xn1", [128, NCH * FT], BF16))
            vselw = A(nc.sbuf_tensor("a2_vselw", [16, NCH * 128], BF16))
            pbuf = A(nc.sbuf_tensor("a2_pbuf", [16, 2 * FT], BF16))
            tmpscr = A(nc.sbuf_tensor("a2_tmp", [128, FT], F32))
            t1scr = tmpscr
            qstage = A(nc.sbuf_tensor("a2_qstage", [128, NCH * FT], BF16))
            kstage = A(nc.sbuf_tensor("a2_kstage", [128, NCH * FT], BF16))
            vstage = A(nc.sbuf_tensor("a2_vstage", [128, 4 * D], BF16))
            s1stage = A(nc.sbuf_tensor("a2_s1stage", [64, 2 * FT], BF16))
            bd1s = A(nc.sbuf_tensor("a2_bd1s", [64, 1024], BF16))
            bd2s = A(nc.sbuf_tensor("a2_bd2s", [128, NCH * 64], BF16))
            w1os = A(nc.sbuf_tensor("a2_w1os", [64, 1024], BF16))
            b1fs = A(nc.sbuf_tensor("a2_b1fs", [128, NCH], F32))
            b1b2s = A(nc.sbuf_tensor("a2_b1b2s", [64, 1], F32))
            b1outs = A(nc.sbuf_tensor("a2_b1outs", [128, NCH], F32))
            hstage = A(nc.sbuf_tensor("a2_hstage", [128, 2 * FT], BF16))
            g1sb = A(nc.sbuf_tensor("a2_g1sb", [64, FT], BF16))
            sgscr = A(nc.sbuf_tensor("a2_sgscr", [128, 2 * FT], BF16))
            fscr = A(nc.sbuf_tensor("a2_fscr", [128, FT], F32))
            gstage = A(nc.sbuf_tensor("a2_gstage", [128, NCH * FT], BF16))
            mrow = A(nc.sbuf_tensor("a2_mrow", [1, FT], BF16))
            exrow = A(nc.sbuf_tensor("a2_exrow", [1, FT], F32))
            vrow = A(nc.sbuf_tensor("a2_vrow", [1, FT], F32))
            rrow = A(nc.sbuf_tensor("a2_rrow", [1, FT], BF16))
            srow = A(nc.sbuf_tensor("a2_srow", [1, FT], BF16))
            ones128 = A(nc.sbuf_tensor("a2_ones128", [128, 1], BF16))
            onesrow = A(nc.sbuf_tensor("a2_onesrow", [1, 128], F32))
            hbc = A(nc.sbuf_tensor("a2_hbc", [16, NCH * 128], BF16))
            w1q = A(nc.sbuf_tensor("a2_w1q", [128, NCH * D], BF16))
            w1k = A(nc.sbuf_tensor("a2_w1k", [128, NCH * D], BF16))
            w1v = A(nc.sbuf_tensor("a2_w1v", [128, NCH * D], BF16))
            w1i = A(nc.sbuf_tensor("a2_w1i", [128, NCH * 64], BF16))
            b1qs = A(nc.sbuf_tensor("a2_b1qs", [128, NCH], F32))
            b1ks = A(nc.sbuf_tensor("a2_b1ks", [128, NCH], F32))
            b1ins = A(nc.sbuf_tensor("a2_b1ins", [64, 1], F32))
            bvrow = A(nc.sbuf_tensor("a2_bvrow", [1, D], BF16))
            onesrowb = A(nc.sbuf_tensor("a2_onesrowb", [1, 128], BF16))
            bvbc = A(nc.sbuf_tensor("a2_bvbc", [128, D], BF16))

            psP = [A(nc.psum_tensor(f"a2_psP{k}", [128, 512], F32)) for k in range(2)]
            psLN1 = A(nc.psum_tensor("a2_psLN1", [128, 512], F32))
            psLN2 = A(nc.psum_tensor("a2_psLN2", [128, 512], F32))
            psF = [A(nc.psum_tensor(f"a2_psF{k}", [128, 512], F32)) for k in range(2)]
            psV = A(nc.psum_tensor("a2_psV", [128, 512], F32))
            psW1 = A(nc.psum_tensor("a2_psW1", [64, 512], F32))

            s_ld = SemGroup(nc, E, "a2_ld", 2, per=16)
            s_ldg = SemGroup(nc, E, "a2_ldg", 2, per=16)
            s_ldp = SemGroup(nc, E, "a2_ldp", 2, per=16)
            s_wld = Sem(nc, E, "a2_wld")
            s_init = pool.get()
            s_mmbv = pool.get()
            s_evbv = pool.get()
            s_mmP = pool.get()
            s_sa = pool.get()
            s_x1 = pool.get()
            s_cast = pool.get()
            s_mmstat = pool.get()
            s_rowd = pool.get()
            s_sqr = pool.get()
            s_rowg = pool.get()
            s_bc = pool.get()
            s_xn1 = pool.get()
            s_mmQ = pool.get()
            s_qev = pool.get()
            s_mmK = pool.get()
            s_kev = pool.get()
            s_mmV = pool.get()
            s_vev = pool.get()
            s_mmW1 = pool.get()
            s_s1ev = pool.get()
            s_mmvw = pool.get()
            s_vw = pool.get()
            s_x1st = SemGroup(nc, E, "a2_x1st", 2)
            s_qst = SemGroup(nc, E, "a2_qst", 2)
            s_kst = SemGroup(nc, E, "a2_kst", 2)
            s_vst = SemGroup(nc, E, "a2_vst", 2)
            s_gst = SemGroup(nc, E, "a2_gst", 2)
            s_mmE = pool.get()
            s_h = pool.get()
            s_mmG = pool.get()
            s_g1 = pool.get()
            s_mmW = pool.get()
            s_sg = pool.get()
            s_ep = pool.get()

            def xb(i, c):
                o = ((i % 2) * NCH + c) * FT
                return xbuf[:, o:o + FT]

            def g0s(i, c):
                return g0buf[:, c * FT:(c + 1) * FT]

            def x1s(i, c):
                o = ((i % 2) * NCH + c) * FT
                return x1buf[:, o:o + FT]

            def xn1s(i, c):
                o = ((i % 2) * NCH + c) * FT
                return xn1[:, o:o + FT]

            def wsl(w, c, o):
                return w[:, c * D + o * 128:c * D + o * 128 + 128]

            def row(buf, i):
                if buf.shape[1] == FT:
                    return buf[:, 0:FT]
                o = (i % 2) * FT
                return buf[:, o:o + FT]

            with nc.Block() as blk:
                @blk.gpsimd
                def _(eng):
                    en = "gpsimd"
                    eng = CE(eng, chains[en])
                    bwait(eng, en)
                    s_wld.inc(eng.dma_start(out=hbc[:, :], in_=dram["hselbc"].ap()), 16)
                    for nm, dst in (("w1qT", w1q), ("w1kT", w1k), ("w1vT", w1v)):
                        src = dram[nm].ap().rearrange("(c p) n -> p c n", p=128)
                        d2 = dst[:, :].rearrange("p (c n) -> p c n", c=NCH)
                        s_wld.inc(eng.dma_start(out=d2, in_=src), 16)
                    src = dram["w1in"].ap().rearrange("(c p) n -> p c n", p=128)
                    d2 = w1i[:, :].rearrange("p (c n) -> p c n", c=NCH)
                    s_wld.inc(eng.dma_start(out=d2, in_=src), 16)
                    s_wld.inc(eng.dma_start(out=b1qs[:, :], in_=dram["b1q"].ap()), 16)
                    s_wld.inc(eng.dma_start(out=b1ks[:, :], in_=dram["b1k"].ap()), 16)
                    s_wld.inc(eng.dma_start(out=b1ins[:, :], in_=dram["b1in"].ap()), 16)
                    s_wld.inc(eng.dma_start(out=bvrow[:, :], in_=dram["b1vrow"].ap()), 16)
                    s_wld.inc(eng.dma_start(out=bd1s[:, :], in_=dram["bd1"].ap()), 16)
                    src2 = dram["bd2"].ap().rearrange("(c p) n -> p c n", p=128)
                    d2b = bd2s[:, :].rearrange("p (c n) -> p c n", c=NCH)
                    s_wld.inc(eng.dma_start(out=d2b, in_=src2), 16)
                    s_wld.inc(eng.dma_start(out=w1os[:, :], in_=dram["w1out"].ap()), 16)
                    s_wld.inc(eng.dma_start(out=b1fs[:, :], in_=dram["b1f"].ap()), 16)
                    s_wld.inc(eng.dma_start(out=b1b2s[:, :], in_=dram["b1b2"].ap()), 16)
                    s_wld.inc(eng.dma_start(out=b1outs[:, :], in_=dram["b1out"].ap()), 16)
                    chains[en].flush(eng)

                @blk.vector
                def _(eng):
                    en = "vector"
                    eng = CE(eng, chains[en])
                    bwait(eng, en)
                    s_init.inc(eng.memset(ones128[:, :], 1.0))
                    s_init.inc(eng.memset(onesrow[:, :], 1.0))
                    s_init.inc(eng.memset(onesrowb[:, :], 1.0))
                    for h in range(2):
                        s_mmbv.wait(eng, en, h + 1)
                        inst = eng.tensor_copy(bvbc[:, h * 512:(h + 1) * 512], psP[h][:, :])
                        s_evbv.inc(inst)
                    for c in range(NCH):
                        s_mmvw.wait(eng, en, c + 1)
                        inst = eng.tensor_tensor(vselw[:, c * 128:(c + 1) * 128],
                                                 hbc[:, c * 128:(c + 1) * 128],
                                                 psP[c % 2][0:16, 0:128], ALU.mult)
                        s_vw.inc(inst)

                    def v_x1chain(j):
                        s_ld.wait_done(eng, en, j)
                        s_ldg.wait_done(eng, en, j)
                        s_x1st.wait_done(eng, en, j - 2)
                        s_cast.wait(eng, en, 16 * (j - 1))
                        for c in range(NCH):
                            s_mmP.wait(eng, en, 8 * j + c + 1)
                            inst = eng.tensor_tensor(t1scr[:, :], psP[c % 2][:, :], g0s(j, c),
                                                     ALU.mult)
                            s_sa.inc(inst)
                            inst = eng.tensor_tensor(x1s(j, c), t1scr[:, :], xb(j, c), ALU.add)
                        s_x1.inc(inst)

                    def v_rows(j):
                        s_mmstat.wait(eng, en, 8 * (j + 1))
                        s_bc.wait(eng, en, j)
                        eng.tensor_scalar(row(mrow, j), psLN1[0:1, :], 1.0 / D, None, ALU.mult)
                        eng.tensor_scalar(row(exrow, j), psLN2[0:1, :], 1.0 / D, None, ALU.mult)
                        eng.tensor_tensor(row(vrow, j), row(mrow, j), row(mrow, j), ALU.mult)
                        eng.tensor_tensor(row(vrow, j), row(exrow, j), row(vrow, j), ALU.subtract)
                        inst = eng.tensor_scalar(row(vrow, j), row(vrow, j), EPS, None, ALU.add)
                        s_rowd.inc(inst)
                        s_sqr.wait(eng, en, j + 1)
                        eng.reciprocal(row(vrow, j), srow[:, :])
                        inst = eng.tensor_copy(row(rrow, j), row(vrow, j))
                        s_rowg.inc(inst)

                    v_x1chain(0)
                    v_rows(0)
                    for i in range(NT):
                        s_bc.wait(eng, en, i + 1)
                        s_mmW1.wait(eng, en, i)
                        for c in range(NCH):
                            eng.tensor_tensor(tmpscr[:, :], x1s(i, c), psLN1[:, :], ALU.subtract)
                            inst = eng.tensor_tensor(xn1[:, c * FT:(c + 1) * FT], tmpscr[:, :],
                                                     psLN2[:, :], ALU.mult)
                        s_xn1.inc(inst)
                        if i + 1 < NT:
                            v_x1chain(i + 1)
                        s_vst.wait_done(eng, en, i - 1)
                        for g in range(4):
                            for eh in range(2):
                                n = 2 * g + eh
                                s_mmV.wait(eng, en, 8 * i + n + 1)
                                inst = eng.tensor_tensor(
                                    vstage[:, g * D + eh * 512:g * D + eh * 512 + 512],
                                    psV[:, :], bvbc[:, eh * 512:(eh + 1) * 512], ALU.add)
                                s_vev.inc(inst)
                        if i + 1 < NT:
                            v_rows(i + 1)
                        if i >= 1:
                            k = i - 1
                            s_gst.wait_done(eng, en, k - 1)
                            for c in range(NCH):
                                s_sg.wait(eng, en, 8 * k + c + 1)
                                eng.tensor_scalar(fscr[:, :], psP[0][:, :], b1outs[:, c:c + 1],
                                                  None, ALU.add)
                                inst = eng.tensor_tensor(gstage[:, c * FT:(c + 1) * FT],
                                                         fscr[:, :],
                                                         sgscr[:, (c % 2) * FT:(c % 2 + 1) * FT],
                                                         ALU.mult)
                                s_ep.inc(inst)
                        s_mmW1.wait(eng, en, i + 1)
                        s_mmE.wait(eng, en, 8 * (i - 1))
                        inst = eng.tensor_scalar(row(s1stage, i), psW1[:, :], b1ins[:, :],
                                                 None, ALU.add)
                        s_s1ev.inc(inst)
                    k = NT - 1
                    s_gst.wait_done(eng, en, k - 1)
                    for c in range(NCH):
                        s_sg.wait(eng, en, 8 * k + c + 1)
                        eng.tensor_scalar(fscr[:, :], psP[0][:, :], b1outs[:, c:c + 1],
                                          None, ALU.add)
                        inst = eng.tensor_tensor(gstage[:, c * FT:(c + 1) * FT], fscr[:, :],
                                                 sgscr[:, (c % 2) * FT:(c % 2 + 1) * FT],
                                                 ALU.mult)
                        s_ep.inc(inst)
                    chains[en].flush(eng)

                @blk.scalar
                def _(eng):
                    en = "scalar"
                    eng = CE(eng, chains[en])
                    bwait(eng, en)

                    def s_casts(j):
                        s_x1.wait(eng, en, j + 1)
                        for c in range(NCH):
                            s_mmstat.wait(eng, en, 8 * j + c - 3)
                            sl = (c % 4) * FT
                            s_cast.inc(eng.copy(x1bf[:, sl:sl + FT], x1s(j, c)))
                            s_cast.inc(eng.activation(out=sqbf[:, sl:sl + FT],
                                                      in_=x1s(j, c), func=AF.Square))

                    def s_sqrt(j):
                        s_rowd.wait(eng, en, j + 1)
                        s_sqr.inc(eng.activation(out=srow[:, :], in_=row(vrow, j), func=AF.Sqrt))

                    def s_ffn(k):
                        for c in range(NCH):
                            s_mmE.wait(eng, en, 8 * k + c + 1)
                            s_mmG.wait(eng, en, 8 * k + c - 1)
                            inst = eng.activation(out=hstage[:, (c % 2) * FT:(c % 2 + 1) * FT],
                                                  in_=psP[0][:, :], func=AF.Relu,
                                                  bias=b1fs[:, c:c + 1], scale=1.0)
                            s_h.inc(inst)
                        s_mmG.wait(eng, en, 8 * (k + 1))
                        s_mmW.wait(eng, en, 8 * k)
                        inst = eng.activation(out=g1sb[:, :], in_=psP[1][0:64, :], func=AF.Identity,
                                              bias=b1b2s[:, :], scale=1.0)
                        s_g1.inc(inst)

                    def s_sgs(k):
                        for c in range(NCH):
                            s_mmW.wait(eng, en, 8 * k + c + 1)
                            s_ep.wait(eng, en, 8 * k + c - 1)
                            inst = eng.activation(out=sgscr[:, (c % 2) * FT:(c % 2 + 1) * FT],
                                                  in_=psP[0][:, :], func=AF.Sigmoid,
                                                  bias=b1outs[:, c:c + 1], scale=1.0)
                            s_sg.inc(inst)

                    s_casts(0)
                    s_sqrt(0)
                    for i in range(NT):
                        for o in range(NCH):
                            s_mmQ.wait(eng, en, 8 * i + o + 1)
                            s_qst.wait_done(eng, en, i - 1)
                            inst = eng.activation(out=qstage[:, o * FT:(o + 1) * FT],
                                                  in_=psF[o % 2][:, :], func=AF.Identity,
                                                  bias=b1qs[:, o:o + 1], scale=1.0)
                            s_qev.inc(inst)
                        for o in range(NCH):
                            s_mmK.wait(eng, en, 8 * i + o + 1)
                            s_kst.wait_done(eng, en, i - 1)
                            inst = eng.activation(out=kstage[:, o * FT:(o + 1) * FT],
                                                  in_=psF[o % 2][:, :], func=AF.Identity,
                                                  bias=b1ks[:, o:o + 1], scale=1.0)
                            s_kev.inc(inst)
                        if i + 1 < NT:
                            s_casts(i + 1)
                        if i >= 1:
                            s_ffn(i - 1)
                        if i + 1 < NT:
                            s_sqrt(i + 1)
                        if i >= 1:
                            s_sgs(i - 1)
                    s_ffn(NT - 1)
                    s_sgs(NT - 1)
                    chains[en].flush(eng)

                @blk.tensor
                def _(eng):
                    en = "tensor"
                    bwait(eng, en)
                    s_wld.wait(eng, en, 16 * 15)
                    s_init.wait(eng, en, 3)
                    for h in range(2):
                        inst = eng.matmul(psP[h][:, :], onesrowb[:, :],
                                          bvrow[:, h * 512:(h + 1) * 512], start=True, stop=True)
                        s_mmbv.inc(inst)
                    for c in range(NCH):
                        if c < 2:
                            s_evbv.wait(eng, en, c + 1)
                        else:
                            s_vw.wait(eng, en, c - 1)
                        inst = eng.matmul(psP[c % 2][0:16, 0:128], onesrowb[0:1, 0:16],
                                          vsumt[0:1, c * 128:(c + 1) * 128], start=True, stop=True)
                        s_mmvw.inc(inst)

                    def t_bcastP(j):
                        s_ldp.wait_done(eng, en, j)
                        s_ep.wait(eng, en, 8 * (j - 2))
                        s_g1.wait(eng, en, j - 2)
                        for c in range(NCH):
                            if j == 0:
                                s_vw.wait(eng, en, 8)
                            s_sa.wait(eng, en, 8 * j + c - 1)
                            inst = eng.matmul(psP[c % 2][:, :], vselw[:, c * 128:(c + 1) * 128],
                                              pbuf[:, (j % 2) * FT:(j % 2 + 1) * FT],
                                              start=True, stop=True)
                            s_mmP.inc(inst)

                    def t_stats(j):
                        s_xn1.wait(eng, en, j)
                        for c in range(NCH):
                            s_cast.wait(eng, en, 16 * j + 2 * (c + 1))
                            sl = (c % 4) * FT
                            eng.matmul(psLN1[0:1, :], ones128[:, :], x1bf[:, sl:sl + FT],
                                       start=(c == 0), stop=(c == NCH - 1))
                            inst = eng.matmul(psLN2[0:1, :], ones128[:, :], sqbf[:, sl:sl + FT],
                                              start=(c == 0), stop=(c == NCH - 1))
                            s_mmstat.inc(inst)

                    def t_lnbcast(j):
                        s_rowg.wait(eng, en, j + 1)
                        eng.matmul(psLN1[:, :], onesrowb[:, :], row(mrow, j), start=True, stop=True)
                        inst = eng.matmul(psLN2[:, :], onesrowb[:, :], row(rrow, j),
                                          start=True, stop=True)
                        s_bc.inc(inst)

                    def t_ffn(k):
                        s_s1ev.wait(eng, en, k + 1)
                        for c in range(NCH):
                            if c == 0:
                                s_sa.wait(eng, en, min(8 * (k + 2), 8 * NT))
                                s_ep.wait(eng, en, 8 * k)
                            else:
                                s_h.wait(eng, en, 8 * k + c)
                            inst = eng.matmul(psP[0][:, :], bd1s[:, c * 128:(c + 1) * 128],
                                              s1stage[:, (k % 2) * FT:(k % 2 + 1) * FT],
                                              start=True, stop=True)
                            s_mmE.inc(inst)
                            if c >= 1:
                                cc = c - 1
                                s_h.wait(eng, en, 8 * k + cc + 1)
                                if cc == 0:
                                    s_g1.wait(eng, en, k)
                                inst = eng.matmul(psP[1][0:64, :], bd2s[:, cc * 64:(cc + 1) * 64],
                                                  hstage[:, (cc % 2) * FT:(cc % 2 + 1) * FT],
                                                  start=(cc == 0), stop=False)
                                s_mmG.inc(inst)
                        cc = NCH - 1
                        s_h.wait(eng, en, 8 * k + cc + 1)
                        inst = eng.matmul(psP[1][0:64, :], bd2s[:, cc * 64:(cc + 1) * 64],
                                          hstage[:, (cc % 2) * FT:(cc % 2 + 1) * FT],
                                          start=False, stop=True)
                        s_mmG.inc(inst)

                    def t_wout(k):
                        s_g1.wait(eng, en, k + 1)
                        for c in range(NCH):
                            if c >= 1:
                                s_ep.wait(eng, en, 8 * k + c)
                            inst = eng.matmul(psP[0][:, :], w1os[:, c * 128:(c + 1) * 128],
                                              g1sb[:, :], start=True, stop=True)
                            s_mmW.inc(inst)

                    t_bcastP(0)
                    t_stats(0)
                    t_lnbcast(0)
                    for i in range(NT):
                        s_xn1.wait(eng, en, i + 1)
                        for o in range(NCH):
                            if o < 2:
                                s_kev.wait(eng, en, 8 * (i - 1) + o + 7)
                            else:
                                s_qev.wait(eng, en, 8 * i + o - 1)
                            for c in range(NCH):
                                inst = eng.matmul(psF[o % 2][:, :], wsl(w1q, c, o),
                                                  xn1[:, c * FT:(c + 1) * FT],
                                                  start=(c == 0), stop=(c == NCH - 1))
                            s_mmQ.inc(inst)
                        if i + 1 < NT:
                            t_bcastP(i + 1)
                        for o in range(NCH):
                            if o == 0:
                                s_qev.wait(eng, en, 8 * i + 7)
                            elif o == 1:
                                s_qev.wait(eng, en, 8 * i + 8)
                            else:
                                s_kev.wait(eng, en, 8 * i + o - 1)
                            for c in range(NCH):
                                inst = eng.matmul(psF[o % 2][:, :], wsl(w1k, c, o),
                                                  xn1[:, c * FT:(c + 1) * FT],
                                                  start=(c == 0), stop=(c == NCH - 1))
                            s_mmK.inc(inst)
                        if i + 1 < NT:
                            t_stats(i + 1)
                        if i >= 1:
                            t_ffn(i - 1)
                        for g in range(4):
                            for eh in range(2):
                                n = 2 * g + eh
                                s_vev.wait(eng, en, 8 * i + n)
                                for c in range(NCH):
                                    lhs = xn1[:, c * FT + g * 128:c * FT + (g + 1) * 128]
                                    rhs = w1v[:, c * D + eh * 512:c * D + eh * 512 + 512]
                                    inst = eng.matmul(psV[:, :], lhs, rhs,
                                                      start=(c == 0), stop=(c == NCH - 1))
                                s_mmV.inc(inst)
                        if i >= 1:
                            t_wout(i - 1)
                        if i + 1 < NT:
                            t_lnbcast(i + 1)
                        s_s1ev.wait(eng, en, i)
                        for c in range(NCH):
                            inst = eng.matmul(psW1[:, :], w1i[:, c * 64:(c + 1) * 64],
                                              xn1[:, c * FT:(c + 1) * FT],
                                              start=(c == 0), stop=(c == NCH - 1))
                        s_mmW1.inc(inst)
                    t_ffn(NT - 1)
                    t_wout(NT - 1)

                @blk.sync
                def _(eng):
                    en = "sync"
                    bwait(eng, en)
                    for i in range(min(2, NT)):
                        s_ld.inc(chunked_load(eng, xbuf, (i % 2) * NCH * FT, xT, i * FT, FT), i)
                        s_ldp.inc(eng.dma_start(
                            out=pbuf[:, (i % 2) * FT:(i % 2 + 1) * FT],
                            in_=dram["probsH"].ap()[:, i * FT:(i + 1) * FT]), i)
                    s_ldg.inc(chunked_load(eng, g0buf, 0, g0T, 0, FT), 0)
                    for i in range(NT):
                        if i + 1 < NT:
                            s_x1.wait(eng, en, i + 1)
                            s_ldg.inc(chunked_load(eng, g0buf, 0, g0T, (i + 1) * FT, FT), i + 1)
                        s_x1.wait(eng, en, i + 1)
                        s_x1st.inc(chunked_store(eng, x1T, i * FT, FT, x1buf, (i % 2) * NCH * FT), i)
                        s_qev.wait(eng, en, 8 * (i + 1))
                        s_qst.inc(chunked_store(eng, qT, i * FT, FT, qstage, 0), i)
                        s_kev.wait(eng, en, 8 * (i + 1))
                        s_kst.inc(chunked_store(eng, kT, i * FT, FT, kstage, 0), i)
                        s_vev.wait(eng, en, 8 * (i + 1))
                        dst = v1.ap()[i * FT:(i + 1) * FT, :].rearrange("(g p) e -> p g e", p=128)
                        srcap = vstage[:, :].rearrange("p (g e) -> p g e", g=4)
                        s_vst.inc(eng.dma_start(out=dst, in_=srcap), i)
                        if i >= 1:
                            s_ep.wait(eng, en, 8 * i)
                            s_gst.inc(chunked_store(eng, gffT, (i - 1) * FT, FT, gstage, 0), i - 1)
                        if i + 2 < NT:
                            s_x1.wait(eng, en, i + 1)
                            s_ld.inc(chunked_load(eng, xbuf, (i % 2) * NCH * FT, xT,
                                                  (i + 2) * FT, FT), i + 2)
                            s_mmP.wait(eng, en, 8 * (i + 1))
                            s_ldp.inc(eng.dma_start(
                                out=pbuf[:, (i % 2) * FT:(i % 2 + 1) * FT],
                                in_=dram["probsH"].ap()[:, (i + 2) * FT:(i + 3) * FT]), i + 2)
                    s_ep.wait(eng, en, 8 * NT)
                    s_gst.inc(chunked_store(eng, gffT, (NT - 1) * FT, FT, gstage, 0), NT - 1)
                    s_x1st.wait_all(eng, en, NT)
                    s_qst.wait_all(eng, en, NT)
                    s_kst.wait_all(eng, en, NT)
                    s_vst.wait_all(eng, en, NT)
                    s_gst.wait_all(eng, en, NT)

            assert s_mmP.count == 8 * NT
            assert s_vev.count == 8 * NT
        return chains



def build_B(nc, E, dram, bwait, pool):
    x1T, qT, kT, v1, gffT, yT = (dram["x1T"], dram["qT"], dram["kT"], dram["v1"],
                                 dram["gffT"], dram["yT"])
    ctx = ExitStack()
    A = ctx.enter_context
    with ctx:
        qbuf = A(nc.sbuf_tensor("b_qbuf", [128, 2 * NCH * NQ], BF16))
        kbuf = A(nc.sbuf_tensor("b_kbuf", [128, 2 * NCH * KV], BF16))
        vring = A(nc.sbuf_tensor("b_vring", [128, SEG * D], BF16))
        pring = A(nc.sbuf_tensor("b_pring", [128, SEG * NQ], BF16))
        oacc = A(nc.sbuf_tensor("b_oacc", [128, 2 * NCH * NQ], F32))
        gffpan = A(nc.sbuf_tensor("b_gffpan", [128, 2 * NCH * NQ], BF16))
        x1pan = A(nc.sbuf_tensor("b_x1pan", [128, NCH * NQ], F32))
        outsb = A(nc.sbuf_tensor("b_outsb", [128, NCH * NQ], F32))
        rden = A(nc.sbuf_tensor("b_rden", [1, NQ], F32))
        maskbf = A(nc.sbuf_tensor("b_maskbf", [128, 4 * NQ], BF16))
        onescol = A(nc.sbuf_tensor("b_onescol", [128, 1], BF16))
        onesrow = A(nc.sbuf_tensor("b_onesrow", [1, 128], F32))

        psS = [A(nc.psum_tensor(f"b_psS{k}", [128, 512], F32)) for k in range(2)]
        psD = A(nc.psum_tensor("b_psD", [1, 512], F32))
        psO = [A(nc.psum_tensor(f"b_psO{k}", [128, 512], F32)) for k in range(4)]
        psBC = A(nc.psum_tensor("b_psBC", [128, 512], F32))

        s_k = SemGroup(nc, E, "b_k", 2)
        s_v = SemGroup(nc, E, "b_v", SEG)
        s_qld = SemGroup(nc, E, "b_qld", 2)
        s_x1ld = SemGroup(nc, E, "b_x1ld", 2)
        s_gld = SemGroup(nc, E, "b_gld", 2)
        s_wld = Sem(nc, E, "b_wld")
        s_yst = Sem(nc, E, "b_yst")
        s_init = pool.get()
        s_mmS = pool.get()
        s_exp = pool.get()
        s_msk = pool.get()
        s_tb = pool.get()
        s_swp = pool.get()
        s_oev = pool.get()
        s_mmD = pool.get()
        s_recD = pool.get()
        s_mmBc = pool.get()
        s_sat = pool.get()
        s_out = pool.get()
        chains = {"vector": Chain(pool.get(), "b_chv"), "scalar": Chain(pool.get(), "b_chs"),
                  "gpsimd": Chain(pool.get(), "b_chg")}

        def kslice(jg, c):
            o = (jg % 2) * NCH * KV + c * KV
            return kbuf[:, o:o + KV]

        def qslice(p, c):
            o = (p % 2) * NCH * NQ + c * NQ
            return qbuf[:, o:o + NQ]

        def vslice(jg, c):
            o = (jg % SEG) * D + c * 128
            return vring[:, o:o + 128]

        def pslice(jg):
            o = (jg % SEG) * NQ
            return pring[:, o:o + NQ]

        def oslice(p, c):
            o = ((p % 2) * NCH + c) * NQ
            return oacc[:, o:o + NQ]

        def gslice(p, c):
            o = ((p % 2) * NCH + c) * NQ
            return gffpan[:, o:o + NQ]

        def is_masked(p, j):
            return j >= tiles_in_panel(p) - 4

        nmask_before = {}
        cnt = 0
        for p in range(NP):
            for j in range(tiles_in_panel(p)):
                nmask_before[(p, j)] = cnt
                if is_masked(p, j):
                    cnt += 1

        def seg_tiles(p, s):
            J = tiles_in_panel(p)
            return list(range(s * SEG, min(J, (s + 1) * SEG)))

        with nc.Block() as blk:
            @blk.gpsimd
            def _(eng):
                en = "gpsimd"
                eng = CE(eng, chains[en])
                bwait(eng, en)
                s_wld.inc(eng.dma_start(out=maskbf[:, :], in_=dram["masks"].ap()), 16)
                for p in range(min(2, NP)):
                    s_qld.inc(chunked_load(eng, qbuf, (p % 2) * NCH * NQ, qT, p * NQ, NQ), p)
                    s_gld.inc(chunked_load(eng, gffpan, (p % 2) * NCH * NQ, gffT, p * NQ, NQ), p)
                s_x1ld.inc(chunked_load(eng, x1pan, 0, x1T, 0, NQ), 0)
                for p in range(NP):
                    s_sat.wait(eng, en, p + 1)
                    s_x1ld.wait_done(eng, en, p)
                    s_gld.wait_done(eng, en, p)
                    s_yst.wait(eng, en, 16 * p)
                    for c in range(NCH):
                        eng.tensor_tensor(outsb[:, c * NQ:(c + 1) * NQ], oslice(p, c),
                                          gslice(p, c), ALU.mult)
                        inst = eng.tensor_tensor(outsb[:, c * NQ:(c + 1) * NQ],
                                                 outsb[:, c * NQ:(c + 1) * NQ],
                                                 x1pan[:, c * NQ:(c + 1) * NQ], ALU.add)
                    s_out.inc(inst)
                    s_out.wait(eng, en, p + 1)
                    s_yst.inc(chunked_store(eng, yT, p * NQ, NQ, outsb, 0), 16)
                    if p + 1 < NP:
                        s_x1ld.inc(chunked_load(eng, x1pan, 0, x1T, (p + 1) * NQ, NQ), p + 1)
                    if p + 2 < NP:
                        s_mmS.wait(eng, en, tiles_before_panel(p + 1))
                        s_qld.inc(chunked_load(eng, qbuf, ((p + 2) % 2) * NCH * NQ, qT,
                                               (p + 2) * NQ, NQ), p + 2)
                        s_gld.inc(chunked_load(eng, gffpan, ((p + 2) % 2) * NCH * NQ, gffT,
                                               (p + 2) * NQ, NQ), p + 2)
                s_yst.wait(eng, en, 16 * NP)

            @blk.sync
            def _(eng):
                en = "sync"
                bwait(eng, en)
                for p in range(NP):
                    jg0 = tiles_before_panel(p)
                    for j in range(tiles_in_panel(p)):
                        jg = jg0 + j
                        kv0 = j * KV
                        s_mmS.wait(eng, en, jg - 1)
                        srcap = kT.ap()[:, kv0:kv0 + KV].rearrange("(c p) n -> p c n", p=128)
                        dst = kbuf[:, (jg % 2) * NCH * KV:((jg % 2) + 1) * NCH * KV].rearrange(
                            "p (c n) -> p c n", c=NCH)
                        s_k.inc(eng.dma_start(out=dst, in_=srcap), jg)
                        s_tb.wait(eng, en, jg - (SEG - 1))
                        s_v.inc(eng.dma_start(out=vring[:, (jg % SEG) * D:((jg % SEG) + 1) * D],
                                              in_=v1.ap()[kv0:kv0 + KV, :]), jg)

            @blk.tensor
            def _(eng):
                en = "tensor"
                bwait(eng, en)
                s_init.wait(eng, en, 2)
                sg_global = 0
                for p in range(NP):
                    jg0 = tiles_before_panel(p)
                    J = tiles_in_panel(p)
                    nseg = segs_in_panel(p)
                    s_qld.wait_done(eng, en, p)

                    def emit_S(j):
                        jg = jg0 + j
                        s_k.wait_done(eng, en, jg)
                        s_exp.wait(eng, en, jg - 1)
                        for c in range(NCH):
                            inst = eng.matmul(psS[jg % 2][:, :], kslice(jg, c), qslice(p, c),
                                              start=(c == 0), stop=(c == NCH - 1))
                        s_mmS.inc(inst)

                    for s in range(nseg):
                        sg = sg_global
                        tiles = seg_tiles(p, s)
                        if s == 0:
                            emit_S(tiles[0])
                        for idx, j in enumerate(tiles):
                            jg = jg0 + j
                            if idx + 1 < len(tiles):
                                emit_S(tiles[idx + 1])
                            elif (s + 1) * SEG < J:
                                emit_S((s + 1) * SEG)
                            if is_masked(p, j):
                                s_msk.wait(eng, en, nmask_before[(p, j)] + 1)
                            else:
                                s_exp.wait(eng, en, jg + 1)
                            if j == 0:
                                s_recD.wait(eng, en, p)
                            inst = eng.matmul(psD[:, :], onescol[:, :], pslice(jg),
                                              start=(j == 0), stop=(j == J - 1))
                            if j == J - 1:
                                s_mmD.inc(inst)
                            s_v.wait_done(eng, en, jg)
                            if idx == 0:
                                s_oev.wait(eng, en, 8 * sg)
                            for c in range(4):
                                inst = eng.matmul(psO[c][:, :], vslice(jg, c), pslice(jg),
                                                  start=(idx == 0), stop=(idx == len(tiles) - 1))
                            if idx == len(tiles) - 1:
                                s_swp.inc(inst)
                        s_oev.wait(eng, en, 8 * sg + 4)
                        for idx, j in enumerate(tiles):
                            jg = jg0 + j
                            for c in range(4, 8):
                                inst = eng.matmul(psO[c - 4][:, :], vslice(jg, c), pslice(jg),
                                                  start=(idx == 0), stop=(idx == len(tiles) - 1))
                            s_tb.inc(inst)
                        sg_global += 1
                    s_recD.wait(eng, en, p + 1)
                    s_sat.wait(eng, en, p)
                    inst = eng.matmul(psBC[:, :], onesrow[:, :], rden[:, :], start=True, stop=True)
                    s_mmBc.inc(inst)

            @blk.scalar
            def _(eng):
                en = "scalar"
                eng = CE(eng, chains[en])
                bwait(eng, en)
                s_wld.wait(eng, en, 16)
                for p in range(NP):
                    jg0 = tiles_before_panel(p)
                    for j in range(tiles_in_panel(p)):
                        jg = jg0 + j
                        s_mmS.wait(eng, en, jg + 1)
                        s_tb.wait(eng, en, jg - (SEG - 1))
                        inst = eng.activation(out=pslice(jg), in_=psS[jg % 2][:, :],
                                              func=AF.Exp, scale=ISQ)
                        s_exp.inc(inst)

            @blk.vector
            def _(eng):
                en = "vector"
                eng = CE(eng, chains[en])
                bwait(eng, en)
                s_wld.wait(eng, en, 16)
                s_init.inc(eng.memset(onescol[:, :], 1.0))
                s_init.inc(eng.memset(onesrow[:, :], 1.0))
                sg_global = 0
                for p in range(NP):
                    jg0 = tiles_before_panel(p)
                    J = tiles_in_panel(p)
                    nseg = segs_in_panel(p)
                    for s in range(nseg):
                        sg = sg_global
                        tiles = seg_tiles(p, s)
                        for j in tiles:
                            if is_masked(p, j):
                                jg = jg0 + j
                                jd = j - (J - 4)
                                s_exp.wait(eng, en, jg + 1)
                                inst = eng.tensor_tensor(pslice(jg), pslice(jg),
                                                         maskbf[:, jd * NQ:(jd + 1) * NQ],
                                                         ALU.mult)
                                s_msk.inc(inst)
                        s_swp.wait(eng, en, sg + 1)
                        for c in range(4):
                            dst = oslice(p, c)
                            if s == 0:
                                inst = eng.tensor_copy(dst, psO[c][:, :])
                            else:
                                inst = eng.tensor_tensor(dst, dst, psO[c][:, :], ALU.add)
                            s_oev.inc(inst)
                        s_tb.wait(eng, en, jg0 + tiles[-1] + 1)
                        for c in range(4, 8):
                            dst = oslice(p, c)
                            if s == 0:
                                inst = eng.tensor_copy(dst, psO[c - 4][:, :])
                            else:
                                inst = eng.tensor_tensor(dst, dst, psO[c - 4][:, :], ALU.add)
                            s_oev.inc(inst)
                        sg_global += 1
                    s_mmD.wait(eng, en, p + 1)
                    inst = eng.reciprocal(rden[:, :], psD[0:1, 0:NQ])
                    s_recD.inc(inst)
                    s_mmBc.wait(eng, en, p + 1)
                    for c in range(NCH):
                        inst = eng.tensor_tensor(oslice(p, c), oslice(p, c), psBC[:, :], ALU.mult)
                    s_sat.inc(inst)
                chains[en].flush(eng)

        assert s_exp.count == TOTAL_KV
        assert s_out.count == NP
        return chains



def kernel(**inputs):
    from concourse.bass_utils import run_bass_kernel_spmd
    C = prep_common(inputs)
    xTs = prep_percore(inputs)
    nc, input_names = make_program("FUSED")
    in_maps = []
    for c in range(8):
        m = {}
        for n in input_names:
            m[n] = xTs[c] if n == "xT" else C[n]
        in_maps.append(m)
    res = run_bass_kernel_spmd(nc, in_maps, core_ids=list(range(8)))
    out = np.stack([np.asarray(res.results[c]["yT"]).T for c in range(8)])
    return np.ascontiguousarray(out.astype(np.float32))


# version 11
# speedup vs baseline: 1.1250x; 1.0247x over previous
import numpy as np
import concourse.bass as bass
import concourse.mybir as mybir
from contextlib import ExitStack

F32 = mybir.dt.float32
BF16 = mybir.dt.bfloat16
AF = mybir.ActivationFunctionType
ALU = mybir.AluOpType

T = 8192
D = 1024
NCH = 8
FT = 512
NT = T // FT
NQ = 512
NP = T // NQ
KV = 128
SEG = 16
EPS = 1e-5
ISQ = 1.0 / 32.0


def tiles_in_panel(p):
    return 4 * p + 4


def tiles_before_panel(p):
    return 2 * p * p + 2 * p


def segs_in_panel(p):
    return (tiles_in_panel(p) + SEG - 1) // SEG


def segs_before_panel(p):
    return sum(segs_in_panel(q) for q in range(p))


TOTAL_KV = tiles_before_panel(NP)


LASTINC = {}
KEEPALIVE = []


class Sem:
    def __init__(self, nc, E, name):
        self.h = E(nc.semaphore(name))
        self.name = name
        self.count = 0
        self.hw = {}

    def inc(self, inst, n=1):
        inst.then_inc(self.h, n)
        self.count += n
        KEEPALIVE.append(inst)
        LASTINC[id(inst)] = (self, self.count)

    def wait(self, eng, ename, val):
        if val <= 0:
            return
        if self.hw.get(ename, 0) >= val:
            return
        self.hw[ename] = val
        eng.wait_ge(self.h, val)


class PhaseSem:

    def __init__(self, s):
        self.s = s
        self.base = s.count
        self.name = s.name

    def inc(self, inst, n=1):
        self.s.inc(inst, n)

    @property
    def count(self):
        return self.s.count - self.base

    def wait(self, eng, ename, val):
        if val <= 0:
            return
        self.s.wait(eng, ename, self.base + val)


class SemPool:
    def __init__(self, nc, E):
        self.nc = nc
        self.E = E
        self.pool = []
        self.idx = 0

    def phase_reset(self):
        self.idx = 0

    def get(self, name=None):
        if self.idx == len(self.pool):
            self.pool.append(Sem(self.nc, self.E, f"pool{self.idx}"))
        s = self.pool[self.idx]
        self.idx += 1
        return PhaseSem(s)


class Chain:

    def __init__(self, sem, name):
        self.sem = sem
        self.name = name
        self.last = None
        self.pending = None

    def before(self, eng):
        if self.pending is not None:
            inst = self.pending
            li = LASTINC.get(id(inst))
            if li is None:
                self.sem.inc(inst, 1)
                li = (self.sem, self.sem.count)
            self.last = li
            self.pending = None
        if self.last is not None:
            sem, val = self.last
            sem.wait(eng, self.name, val)

    def after(self, inst):
        self.pending = inst

    def flush(self, eng):
        self.before(eng)


class SemGroup:

    def __init__(self, nc, E, name, n, per=16):
        self.sems = [Sem(nc, E, f"{name}{k}") for k in range(n)]
        self.n = n
        self.per = per

    def inc(self, inst, idx):
        self.sems[idx % self.n].inc(inst, self.per)

    def wait_done(self, eng, en, idx):
        if idx < 0:
            return
        self.sems[idx % self.n].wait(eng, en, self.per * (idx // self.n + 1))

    def wait_all(self, eng, en, total):
        for idx in range(max(0, total - self.n), total):
            self.wait_done(eng, en, idx)


COMPUTE_OPS = {"tensor_tensor", "tensor_scalar", "tensor_copy", "tensor_reduce",
               "memset", "reciprocal", "activation", "copy", "add", "mul", "iota",
               "reduce_sum", "reduce_max", "select", "bn_stats"}


class CE:

    def __init__(self, eng, chain):
        self._e = eng
        self._c = chain

    def __getattr__(self, a):
        f = getattr(self._e, a)
        if a in COMPUTE_OPS:
            def g(*args, **kw):
                self._c.before(self._e)
                inst = f(*args, **kw)
                KEEPALIVE.append(inst)
                self._c.after(inst)
                return inst
            return g
        return f



def prep_common(I):
    C = {}
    C["w0rT"] = np.ascontiguousarray(I["L0_ffn_W_r"].T)
    C["w0kT"] = np.ascontiguousarray(I["L0_ffn_W_k"].T)
    C["w0vT"] = np.ascontiguousarray(I["L0_ffn_W_v"].T)
    C["wq0"] = np.ascontiguousarray(I["L0_attn_Wq"])
    C["wk0"] = np.ascontiguousarray(I["L0_attn_Wk"])
    C["wv0T"] = np.ascontiguousarray(I["L0_attn_Wv"].T)
    C["mur"] = np.ascontiguousarray(np.asarray(I["L0_ffn_mu_r"]).reshape(NCH, 128).T)
    C["muk"] = np.ascontiguousarray(np.asarray(I["L0_ffn_mu_k"]).reshape(NCH, 128).T)
    C["w1qT"] = np.ascontiguousarray(I["L1_attn_Wq"].T)
    C["w1kT"] = np.ascontiguousarray(I["L1_attn_Wk"].T)
    C["w1vT"] = np.ascontiguousarray(I["L1_attn_Wv"].T)
    C["b1q"] = np.ascontiguousarray(np.asarray(I["L1_attn_bq"]).reshape(NCH, 128).T)
    C["b1k"] = np.ascontiguousarray(np.asarray(I["L1_attn_bk"]).reshape(NCH, 128).T)
    C["b1vrow"] = np.ascontiguousarray(np.asarray(I["L1_attn_bv"]).reshape(1, D))
    C["w1in"] = np.ascontiguousarray(I["L1_ffn_Win"])
    C["b1in"] = np.ascontiguousarray(np.asarray(I["L1_ffn_bin_"]).reshape(64, 1))
    W1 = np.asarray(I["L1_ffn_W1"])
    bd1 = np.zeros((64, 1024), np.float32)
    for m in range(64):
        bd1[m, 16 * m:16 * m + 16] = W1[m]
    C["bd1"] = bd1
    C["b1f"] = np.ascontiguousarray(
        np.asarray(I["L1_ffn_b1"]).reshape(64 * 16).reshape(NCH, 128).T)
    W2 = np.asarray(I["L1_ffn_W2"])
    bd2 = np.zeros((1024, 64), np.float32)
    for m in range(64):
        bd2[16 * m:16 * m + 16, m] = W2[m]
    C["bd2"] = bd2
    C["b1b2"] = np.ascontiguousarray(np.asarray(I["L1_ffn_b2"]).reshape(64, 1))
    C["w1out"] = np.ascontiguousarray(I["L1_ffn_Wout"])
    C["b1out"] = np.ascontiguousarray(np.asarray(I["L1_ffn_bout"]).reshape(NCH, 128).T)
    hselred = np.zeros((128, NCH * 16), np.float32)
    for c in range(NCH):
        for p in range(128):
            hselred[p, c * 16 + 2 * c + p // 64] = 1.0
    C["hselred"] = hselred
    hselbc = np.zeros((16, NCH * 128), np.float32)
    for c in range(NCH):
        for p in range(128):
            hselbc[2 * c + p // 64, c * 128 + p] = 1.0
    C["hselbc"] = hselbc
    tk = np.arange(128)[:, None]
    tq = np.arange(512)[None, :]
    masks = np.concatenate([(tq >= tk + 128 * j).astype(np.float32) for j in range(4)],
                           axis=1)
    C["masks"] = np.ascontiguousarray(masks)
    return {k: np.asarray(v, np.float32) for k, v in C.items()}


def prep_percore(I):
    x = np.asarray(I["L0_attn_x"])
    return [np.ascontiguousarray(x[c].T) for c in range(8)]


PHASE_SETS = {"A1": ("A1",), "A2": ("A2",), "B": ("B",), "FUSED": ("A1", "A2", "B")}

DRAM_SPECS = {
    "xT": ((D, T), F32, None, ("A1", "A2")),
    "w0rT": ((D, D), F32, None, ("A1",)),
    "w0kT": ((D, D), F32, None, ("A1",)),
    "w0vT": ((D, D), F32, None, ("A1",)),
    "wq0": ((64, 64), F32, None, ("A1",)),
    "wk0": ((64, 64), F32, None, ("A1",)),
    "wv0T": ((64, 64), F32, None, ("A1",)),
    "mur": ((128, NCH), F32, None, ("A1",)),
    "muk": ((128, NCH), F32, None, ("A1",)),
    "hselred": ((128, NCH * 16), F32, None, ("A1",)),
    "g0T": ((D, T), BF16, "A1", ("A2",)),
    "s0T": ((16, T), F32, "A1", ("A2",)),
    "vsumH": ((128, NCH), F32, "A1", ("A2",)),
    "vsumT": ((1, NCH * 128), F32, "A1", ("A2",)),
    "hselbc": ((16, NCH * 128), F32, None, ("A2",)),
    "w1qT": ((D, D), F32, None, ("A2",)),
    "w1kT": ((D, D), F32, None, ("A2",)),
    "w1vT": ((D, D), F32, None, ("A2",)),
    "b1q": ((128, NCH), F32, None, ("A2",)),
    "b1k": ((128, NCH), F32, None, ("A2",)),
    "b1vrow": ((1, D), F32, None, ("A2",)),
    "w1in": ((D, 64), F32, None, ("A2",)),
    "b1in": ((64, 1), F32, None, ("A2",)),
    "x1T": ((D, T), F32, "A2", ("B",)),
    "qT": ((D, T), BF16, "A2", ("B",)),
    "kT": ((D, T), BF16, "A2", ("B",)),
    "v1": ((T, D), BF16, "A2", ("B",)),
    "gffT": ((D, T), BF16, "A2", ("B",)),
    "masks": ((128, 4 * 512), F32, None, ("B",)),
    "bd1": ((64, 1024), F32, None, ("A2",)),
    "b1f": ((128, NCH), F32, None, ("A2",)),
    "bd2": ((1024, 64), F32, None, ("A2",)),
    "b1b2": ((64, 1), F32, None, ("A2",)),
    "w1out": ((64, 1024), F32, None, ("A2",)),
    "b1out": ((128, NCH), F32, None, ("A2",)),
    "yT": ((D, T), F32, "B", ()),
}


def chunked_load(eng, dst_sb, dst_off, src_dram, col0, ncol, nch=NCH):
    src = src_dram.ap()[:, col0:col0 + ncol].rearrange("(c p) n -> p c n", p=128)
    dst = dst_sb[:, dst_off:dst_off + nch * ncol].rearrange("p (c n) -> p c n", c=nch)
    return eng.dma_start(out=dst, in_=src)


def chunked_store(eng, dst_dram, col0, ncol, src_sb, src_off, nch=NCH):
    dst = dst_dram.ap()[:, col0:col0 + ncol].rearrange("(c p) n -> p c n", p=128)
    src = src_sb[:, src_off:src_off + nch * ncol].rearrange("p (c n) -> p c n", c=nch)
    return eng.dma_start(out=dst, in_=src)


def make_program(mode):
    phases = PHASE_SETS[mode]
    nc = bass.Bass("TRN2", target_bir_lowering=False)
    dram = {}
    input_names = []
    for name, (shape, dt, prod, cons) in DRAM_SPECS.items():
        used_c = any(p in phases for p in cons)
        prod_here = prod in phases
        if prod is None:
            if used_c:
                dram[name] = nc.dram_tensor(name, list(shape), dt, kind="ExternalInput")
                input_names.append(name)
        elif name == "yT":
            if prod_here:
                dram[name] = nc.dram_tensor(name, list(shape), dt, kind="ExternalOutput")
        elif prod_here and used_c:
            dram[name] = nc.dram_tensor(name, list(shape), dt)
        elif prod_here:
            dram[name] = nc.dram_tensor(name, list(shape), dt, kind="ExternalOutput")
        elif used_c:
            dram[name] = nc.dram_tensor(name, list(shape), dt, kind="ExternalInput")
            input_names.append(name)
    if "A2" in phases:
        dram["cc_in"] = nc.dram_tensor("cc_in", [16, T], F32)
        dram["cc_out"] = nc.dram_tensor("cc_out", [16, T], F32)
        dram["probsH"] = nc.dram_tensor("probsH", [16, T], BF16)

    with ExitStack() as octx:
        E = octx.enter_context
        bar = Sem(nc, E, "bar")
        pool = SemPool(nc, E)
        state = {"gen": 0}

        def barrier_end(chains=None):
            with nc.Block() as blk:
                for nm, engf in (("sync", blk.sync), ("tensor", blk.tensor),
                                 ("scalar", blk.scalar), ("vector", blk.vector),
                                 ("gpsimd", blk.gpsimd)):
                    @engf
                    def _(eng, nm=nm):
                        if chains and nm in chains and chains[nm].last is not None:
                            sem, val = chains[nm].last
                            sem.wait(eng, nm, val)
                        bar.inc(eng.nop(), 1)
            state["gen"] += 5

        def bwait(eng, ename):
            bar.wait(eng, ename, state["gen"])

        for ph in phases:
            pool.phase_reset()
            if ph == "A1":
                chains = build_A1(nc, E, dram, bwait, pool)
            elif ph == "A2":
                chains = build_A2(nc, E, dram, bwait, barrier_end, pool)
            else:
                chains = build_B(nc, E, dram, bwait, pool)
            barrier_end(chains)
    return nc, input_names



def build_A1(nc, E, dram, bwait, pool):
    xT, g0T, s0T, vsumH = dram["xT"], dram["g0T"], dram["s0T"], dram["vsumH"]

    ctx = ExitStack()
    A = ctx.enter_context
    with ctx:
        xbuf = A(nc.sbuf_tensor("a1_xbuf", [128, 2 * NCH * FT], F32))
        xbf = A(nc.sbuf_tensor("a1_xbf", [128, NCH * FT], BF16))
        sqbf = A(nc.sbuf_tensor("a1_sqbf", [128, NCH * FT], BF16))
        xn = A(nc.sbuf_tensor("a1_xn", [128, 2 * NCH * 513], BF16))
        xr = A(nc.sbuf_tensor("a1_xr", [128, 2 * NCH * FT], BF16))
        xk = A(nc.sbuf_tensor("a1_xk", [128, 2 * NCH * FT], BF16))
        dscr = A(nc.sbuf_tensor("a1_d", [128, FT], BF16))
        escr = A(nc.sbuf_tensor("a1_e", [128, FT], BF16))
        tmpscr = A(nc.sbuf_tensor("a1_tmp", [128, FT], F32))
        r_sb = A(nc.sbuf_tensor("a1_r", [128, NCH * FT], BF16))
        relu_sb = A(nc.sbuf_tensor("a1_relu", [128, 2 * FT], BF16))
        kact = A(nc.sbuf_tensor("a1_kact", [128, NCH * FT], BF16))
        kvprod = A(nc.sbuf_tensor("a1_kvp", [128, 2 * FT], BF16))
        g0sb = A(nc.sbuf_tensor("a1_g0", [128, 2 * NCH * FT], BF16))
        qksb = A(nc.sbuf_tensor("a1_qk", [128, 2 * FT], BF16))
        vp = A(nc.sbuf_tensor("a1_vp", [128, NCH], F32))
        vsum = A(nc.sbuf_tensor("a1_vsum", [128, NCH], F32))
        s0row = A(nc.sbuf_tensor("a1_s0row", [16, 2 * FT], F32))
        mrow = A(nc.sbuf_tensor("a1_mrow", [1, FT], BF16))
        exrow = A(nc.sbuf_tensor("a1_exrow", [1, FT], F32))
        vrow = A(nc.sbuf_tensor("a1_vrow", [1, FT], F32))
        rrow = A(nc.sbuf_tensor("a1_rrow", [1, FT], BF16))
        srow = A(nc.sbuf_tensor("a1_srow", [1, FT], F32))
        bmrow = A(nc.sbuf_tensor("a1_bmrow", [128, 2 * FT], BF16))
        brrow = A(nc.sbuf_tensor("a1_brrow", [128, 2 * FT], BF16))
        ones128 = A(nc.sbuf_tensor("a1_ones128", [128, 1], BF16))
        onesrow = A(nc.sbuf_tensor("a1_onesrow", [1, 128], BF16))
        w0r = A(nc.sbuf_tensor("a1_w0r", [128, NCH * D], BF16))
        w0k = A(nc.sbuf_tensor("a1_w0k", [128, NCH * D], BF16))
        w0v = A(nc.sbuf_tensor("a1_w0v", [128, NCH * D], BF16))
        wq0b = A(nc.sbuf_tensor("a1_wq0b", [64, 64], BF16))
        wk0b = A(nc.sbuf_tensor("a1_wk0b", [64, 64], BF16))
        wv0b = A(nc.sbuf_tensor("a1_wv0b", [64, 64], BF16))
        mscr = A(nc.sbuf_tensor("a1_mscr", [64, 64], BF16))
        Mbd = A(nc.sbuf_tensor("a1_Mbd", [128, 128], BF16))
        Vbd = A(nc.sbuf_tensor("a1_Vbd", [128, 128], BF16))
        murs = A(nc.sbuf_tensor("a1_murs", [128, NCH], F32))
        muks = A(nc.sbuf_tensor("a1_muks", [128, NCH], F32))
        hred = A(nc.sbuf_tensor("a1_hred", [128, NCH * 16], BF16))

        psLN1 = A(nc.psum_tensor("a1_psLN1", [128, 512], F32))
        psLN2 = A(nc.psum_tensor("a1_psLN2", [128, 512], F32))
        psF = [A(nc.psum_tensor(f"a1_psF{k}", [128, 512], F32)) for k in range(2)]
        psQ = [A(nc.psum_tensor(f"a1_psQ{k}", [128, 512], F32)) for k in range(2)]
        psS0 = A(nc.psum_tensor("a1_psS0", [16, 512], F32))
        psM = A(nc.psum_tensor("a1_psM", [64, 64], F32))

        s_ld = SemGroup(nc, E, "a1_ld", 2)
        s_wld = Sem(nc, E, "a1_wld")
        s_init = pool.get()
        s_mmM = pool.get()
        s_evM = pool.get()
        s_cast = pool.get()
        s_mmstat = pool.get()
        s_rowd = pool.get()
        s_sqr = pool.get()
        s_rowg = pool.get()
        s_bc = pool.get()
        s_bcevS = pool.get()
        s_xnLN = pool.get()
        s_mix = pool.get()
        s_mmR = pool.get()
        s_sig = pool.get()
        s_mmK = pool.get()
        s_rel = pool.get()
        s_kact = pool.get()
        s_mmV = pool.get()
        s_kvev = pool.get()
        s_silu = pool.get()
        s_rk = pool.get()
        s_mmU = pool.get()
        s_qk = pool.get()
        s_mmV0 = pool.get()
        s_mms0 = pool.get()
        s_vev = pool.get()
        s_vsadd = pool.get()
        s_xnfree = pool.get()
        s_s0ev = pool.get()
        s_g0st = SemGroup(nc, E, "a1_g0st", 2)
        s_s0st = SemGroup(nc, E, "a1_s0st", 2)
        s_fin = Sem(nc, E, "a1_fin")
        chains = {"vector": Chain(pool.get(), "a1_chv"), "scalar": Chain(pool.get(), "a1_chs"),
                  "gpsimd": Chain(pool.get(), "a1_chg")}

        def xb(i, c):
            o = ((i % 2) * NCH + c) * FT
            return xbuf[:, o:o + FT]

        def xns(i, c, a, b):
            o = ((i % 2) * NCH + c) * 513
            return xn[:, o + a:o + b]

        def xrs(i, c):
            o = ((i % 2) * NCH + c) * FT
            return xr[:, o:o + FT]

        def xks(i, c):
            o = ((i % 2) * NCH + c) * FT
            return xk[:, o:o + FT]

        def wsl(w, c, o):
            return w[:, c * D + o * 128:c * D + o * 128 + 128]

        def row(buf, i):
            if buf.shape[1] == FT:
                return buf[:, 0:FT]
            o = (i % 2) * FT
            return buf[:, o:o + FT]

        with nc.Block() as blk:
            @blk.gpsimd
            def _(eng):
                en = "gpsimd"
                eng = CE(eng, chains[en])
                bwait(eng, en)
                for nm, dst in (("wq0", wq0b), ("wk0", wk0b), ("wv0T", wv0b)):
                    s_wld.inc(eng.dma_start(out=dst[:, :], in_=dram[nm].ap()), 16)
                for nm, dst in (("mur", murs), ("muk", muks)):
                    s_wld.inc(eng.dma_start(out=dst[:, :], in_=dram[nm].ap()), 16)
                s_wld.inc(eng.dma_start(out=hred[:, :], in_=dram["hselred"].ap()), 16)
                for nm, dst in (("w0rT", w0r), ("w0kT", w0k), ("w0vT", w0v)):
                    src = dram[nm].ap().rearrange("(c p) n -> p c n", p=128)
                    d2 = dst[:, :].rearrange("p (c n) -> p c n", c=NCH)
                    s_wld.inc(eng.dma_start(out=d2, in_=src), 16)
                chains[en].flush(eng)

            @blk.vector
            def _(eng):
                en = "vector"
                eng = CE(eng, chains[en])
                bwait(eng, en)
                s_init.inc(eng.memset(Mbd[:, :], 0.0))
                s_init.inc(eng.memset(Vbd[:, :], 0.0))
                s_init.inc(eng.memset(ones128[:, :], 1.0))
                s_init.inc(eng.memset(onesrow[:, :], 1.0))
                s_init.inc(eng.memset(vp[:, :], 0.0))
                s_init.inc(eng.memset(vsum[:, :], 0.0))
                for c in range(NCH):
                    s_init.inc(eng.memset(xn[:, c * 513:c * 513 + 1], 0.0))
                s_evM.wait(eng, en, 1)
                s_init.inc(eng.tensor_copy(Mbd[0:64, 0:64], mscr[:, :]))
                s_init.inc(eng.tensor_copy(Mbd[64:128, 64:128], mscr[:, :]))
                s_wld.wait(eng, en, 144)
                s_init.inc(eng.tensor_copy(Vbd[0:64, 0:64], wv0b[:, :]))
                s_init.inc(eng.tensor_copy(Vbd[64:128, 64:128], wv0b[:, :]))

                def v_rows(i):
                    s_mmstat.wait(eng, en, 8 * (i + 1))
                    s_bc.wait(eng, en, i)
                    eng.tensor_scalar(row(mrow, i), psLN1[0:1, :], 1.0 / D, None, ALU.mult)
                    eng.tensor_scalar(row(exrow, i), psLN2[0:1, :], 1.0 / D, None, ALU.mult)
                    eng.tensor_tensor(row(vrow, i), row(mrow, i), row(mrow, i), ALU.mult)
                    eng.tensor_tensor(row(vrow, i), row(exrow, i), row(vrow, i), ALU.subtract)
                    inst = eng.tensor_scalar(row(vrow, i), row(vrow, i), EPS, None, ALU.add)
                    s_rowd.inc(inst)
                    s_sqr.wait(eng, en, i + 1)
                    eng.reciprocal(row(vrow, i), srow[:, :])
                    inst = eng.tensor_copy(row(rrow, i), row(vrow, i))
                    s_rowg.inc(inst)

                def v_xn(i):
                    s_bcevS.wait(eng, en, i + 1)
                    s_xnfree.wait(eng, en, i - 1)
                    sl = (i % 2) * FT
                    for c in range(NCH):
                        eng.tensor_tensor(tmpscr[:, :], xb(i, c), bmrow[:, sl:sl + FT], ALU.subtract)
                        inst = eng.tensor_tensor(xns(i, c, 1, 513), tmpscr[:, :],
                                                 brrow[:, sl:sl + FT], ALU.mult)
                    s_xnLN.inc(inst)
                    if i + 1 < NT:
                        for c in range(NCH):
                            eng.tensor_copy(xns(i + 1, c, 0, 1), xns(i, c, 512, 513))

                def v_mix(i):
                    s_rk.wait(eng, en, i - 1)
                    for c in range(NCH):
                        eng.tensor_tensor(dscr[:, :], xns(i, c, 1, 513), xns(i, c, 0, 512), ALU.subtract)
                        eng.tensor_scalar(escr[:, :], dscr[:, :], murs[:, c:c + 1], None, ALU.mult)
                        eng.tensor_tensor(xrs(i, c), xns(i, c, 0, 512), escr[:, :], ALU.add)
                        eng.tensor_scalar(escr[:, :], dscr[:, :], muks[:, c:c + 1], None, ALU.mult)
                        inst = eng.tensor_tensor(xks(i, c), xns(i, c, 0, 512), escr[:, :], ALU.add)
                    s_mix.inc(inst)

                def v_vsum(i):
                    s_vev.wait(eng, en, 8 * (i + 1))
                    inst = eng.tensor_tensor(vsum[:, :], vsum[:, :], vp[:, :], ALU.add)
                    s_vsadd.inc(inst)

                v_rows(0)
                v_xn(0)
                v_mix(0)
                for i in range(NT):
                    if i + 1 < NT:
                        v_rows(i + 1)
                        v_xn(i + 1)
                    if i >= 1:
                        v_vsum(i - 1)
                    if i + 1 < NT:
                        v_mix(i + 1)
                    for o in range(NCH):
                        s_rel.wait(eng, en, 8 * i + o + 1)
                        inst = eng.tensor_tensor(kact[:, o * FT:(o + 1) * FT], psF[o % 2][:, :],
                                                 relu_sb[:, (o % 2) * FT:(o % 2 + 1) * FT], ALU.mult)
                        s_kact.inc(inst)
                    for o in range(NCH):
                        s_mmV.wait(eng, en, 8 * i + o + 1)
                        s_silu.wait(eng, en, 8 * i + o - 1)
                        inst = eng.tensor_tensor(kvprod[:, (o % 2) * FT:(o % 2 + 1) * FT],
                                                 psF[o % 2][:, :], r_sb[:, o * FT:(o + 1) * FT], ALU.mult)
                        s_kvev.inc(inst)
                        s_mmU.wait(eng, en, 8 * i + o + 1)
                        s_mms0.wait(eng, en, 8 * i + o - 1)
                        inst = eng.tensor_tensor(qksb[:, (o % 2) * FT:(o % 2 + 1) * FT],
                                                 xns(i, o, 1, 513), psQ[o % 2][:, :], ALU.mult)
                        s_qk.inc(inst)
                v_vsum(NT - 1)
                chains[en].flush(eng)

            @blk.scalar
            def _(eng):
                en = "scalar"
                eng = CE(eng, chains[en])
                bwait(eng, en)
                s_mmM.wait(eng, en, 1)
                s_evM.inc(eng.copy(mscr[:, :], psM[:, :]))

                def s_casts(j):
                    s_ld.wait_done(eng, en, j)
                    for c in range(NCH):
                        s_mmstat.wait(eng, en, 8 * j + c - 7)
                        sl = c * FT
                        s_cast.inc(eng.copy(xbf[:, sl:sl + FT], xb(j, c)))
                        s_cast.inc(eng.activation(out=sqbf[:, sl:sl + FT], in_=xb(j, c),
                                                  func=AF.Square))

                def s_sqrt(j):
                    s_rowd.wait(eng, en, j + 1)
                    s_sqr.inc(eng.activation(out=srow[:, :], in_=row(vrow, j), func=AF.Sqrt))

                def s_bcev(j):
                    s_bc.wait(eng, en, j + 1)
                    sl = (j % 2) * FT
                    eng.copy(bmrow[:, sl:sl + FT], psLN1[:, :])
                    inst = eng.copy(brrow[:, sl:sl + FT], psLN2[:, :])
                    s_bcevS.inc(inst)

                s_casts(0)
                s_sqrt(0)
                s_bcev(0)
                if NT > 1:
                    s_casts(1)

                def s_vev1(k, c):
                    s_mmV0.wait(eng, en, 8 * k + c + 1)
                    s_vsadd.wait(eng, en, k)
                    inst = eng.activation(out=relu_sb[:, 0:FT], in_=psQ[c % 2][:, :],
                                          func=AF.Copy, accum_out=vp[:, c:c + 1])
                    s_vev.inc(inst)

                def s_s0(k):
                    s_mms0.wait(eng, en, 8 * (k + 1))
                    s_s0st.wait_done(eng, en, k - 2)
                    inst = eng.copy(row(s0row, k), psS0[:, :])
                    s_s0ev.inc(inst)

                for i in range(NT):
                    for o in range(NCH):
                        s_mmR.wait(eng, en, 8 * i + o + 1)
                        s_kvev.wait(eng, en, 8 * (i - 1) + o + 1)
                        inst = eng.activation(out=r_sb[:, o * FT:(o + 1) * FT], in_=psF[o % 2][:, :],
                                              func=AF.Sigmoid)
                        s_sig.inc(inst)
                        if i >= 1:
                            s_vev1(i - 1, o)
                    if i >= 1:
                        s_s0(i - 1)
                    if i + 2 < NT:
                        s_casts(i + 2)
                    if i + 1 < NT:
                        s_sqrt(i + 1)
                        s_bcev(i + 1)
                    for o in range(NCH):
                        s_mmK.wait(eng, en, 8 * i + o + 1)
                        s_kact.wait(eng, en, 8 * i + o - 1)
                        inst = eng.activation(out=relu_sb[:, (o % 2) * FT:(o % 2 + 1) * FT],
                                              in_=psF[o % 2][:, :], func=AF.Relu)
                        s_rel.inc(inst)
                    for o in range(NCH):
                        s_kvev.wait(eng, en, 8 * i + o + 1)
                        s_g0st.wait_done(eng, en, i - 2)
                        dst = g0sb[:, ((i % 2) * NCH + o) * FT:((i % 2) * NCH + o + 1) * FT]
                        inst = eng.activation(out=dst, in_=kvprod[:, (o % 2) * FT:(o % 2 + 1) * FT],
                                              func=AF.Silu)
                        s_silu.inc(inst)
                for c in range(NCH):
                    s_vev1(NT - 1, c)
                s_s0(NT - 1)
                chains[en].flush(eng)

            @blk.tensor
            def _(eng):
                en = "tensor"
                bwait(eng, en)
                s_wld.wait(eng, en, 144)
                inst = eng.matmul(psM[:, :], wk0b[:, :], wq0b[:, :], start=True, stop=True)
                s_mmM.inc(inst)
                s_init.wait(eng, en, 18)

                def t_stats(j):
                    s_bcevS.wait(eng, en, j)
                    for c in range(NCH):
                        s_cast.wait(eng, en, 16 * j + 2 * (c + 1))
                        sl = c * FT
                        eng.matmul(psLN1[0:1, :], ones128[:, :], xbf[:, sl:sl + FT],
                                   start=(c == 0), stop=(c == NCH - 1))
                        inst = eng.matmul(psLN2[0:1, :], ones128[:, :], sqbf[:, sl:sl + FT],
                                          start=(c == 0), stop=(c == NCH - 1))
                        s_mmstat.inc(inst)

                def t_bcast(j):
                    s_rowg.wait(eng, en, j + 1)
                    eng.matmul(psLN1[:, :], onesrow[:, :], row(mrow, j), start=True, stop=True)
                    inst = eng.matmul(psLN2[:, :], onesrow[:, :], row(rrow, j), start=True, stop=True)
                    s_bc.inc(inst)

                t_stats(0)
                t_bcast(0)

                def emit_v(k, c):
                    if c < 2:
                        s_qk.wait(eng, en, 8 * k + c + 7)
                    else:
                        s_vev.wait(eng, en, 8 * k + c - 1)
                    inst = eng.matmul(psQ[c % 2][:, :], Vbd[:, :], xns(k, c, 1, 513),
                                      start=True, stop=True)
                    s_mmV0.inc(inst)
                    if c == NCH - 1:
                        s_xnfree.inc(eng.nop())

                for i in range(NT):
                    if i + 1 < NT:
                        t_stats(i + 1)
                    s_mix.wait(eng, en, i + 1)
                    for o in range(NCH):
                        if o < 2:
                            s_kvev.wait(eng, en, 8 * (i - 1) + o + 7)
                        else:
                            s_sig.wait(eng, en, 8 * i + o - 1)
                        for c in range(NCH):
                            inst = eng.matmul(psF[o % 2][:, :], wsl(w0r, c, o), xrs(i, c),
                                              start=(c == 0), stop=(c == NCH - 1))
                        s_mmR.inc(inst)
                        if i >= 1:
                            emit_v(i - 1, o)
                    if i + 1 < NT:
                        t_bcast(i + 1)
                    for o in range(NCH):
                        if o == 0:
                            s_sig.wait(eng, en, 8 * i + 7)
                        elif o == 1:
                            s_sig.wait(eng, en, 8 * i + 8)
                        else:
                            s_kact.wait(eng, en, 8 * i + o - 1)
                        for c in range(NCH):
                            inst = eng.matmul(psF[o % 2][:, :], wsl(w0k, c, o), xks(i, c),
                                              start=(c == 0), stop=(c == NCH - 1))
                        s_mmK.inc(inst)
                        if o == NCH - 1:
                            s_rk.inc(eng.nop())
                    def emit_s0red(c):
                        s_qk.wait(eng, en, 8 * i + c + 1)
                        if c == 0:
                            s_s0ev.wait(eng, en, i)
                        inst = eng.matmul(psS0[:, :], hred[:, c * 16:(c + 1) * 16],
                                          qksb[:, (c % 2) * FT:(c % 2 + 1) * FT],
                                          start=(c == 0), stop=(c == NCH - 1))
                        s_mms0.inc(inst)

                    s_kact.wait(eng, en, 8 * (i + 1))
                    for o in range(NCH):
                        if o >= 2:
                            s_kvev.wait(eng, en, 8 * i + o - 1)
                        for c in range(NCH):
                            inst = eng.matmul(psF[o % 2][:, :], wsl(w0v, c, o),
                                              kact[:, c * FT:(c + 1) * FT],
                                              start=(c == 0), stop=(c == NCH - 1))
                        s_mmV.inc(inst)
                        if o < 2:
                            s_vev.wait(eng, en, 8 * i + o - 1)
                        else:
                            s_qk.wait(eng, en, 8 * i + o - 1)
                        inst = eng.matmul(psQ[o % 2][:, :], Mbd[:, :], xns(i, o, 1, 513),
                                          start=True, stop=True)
                        s_mmU.inc(inst)
                        if o >= 2:
                            emit_s0red(o - 2)
                    emit_s0red(NCH - 2)
                    emit_s0red(NCH - 1)
                for c in range(NCH):
                    emit_v(NT - 1, c)

            @blk.sync
            def _(eng):
                en = "sync"
                bwait(eng, en)
                for i in range(min(2, NT)):
                    s_ld.inc(chunked_load(eng, xbuf, (i % 2) * NCH * FT, xT, i * FT, FT), i)
                for i in range(NT):
                    if i + 2 < NT:
                        s_xnLN.wait(eng, en, i + 1)
                        s_ld.inc(chunked_load(eng, xbuf, (i % 2) * NCH * FT, xT, (i + 2) * FT, FT), i + 2)
                    s_silu.wait(eng, en, 8 * (i + 1))
                    s_g0st.inc(chunked_store(eng, g0T, i * FT, FT, g0sb, (i % 2) * NCH * FT), i)
                    s_s0ev.wait(eng, en, i + 1)
                    s_s0st.inc(eng.dma_start(out=s0T.ap()[:, i * FT:(i + 1) * FT],
                                             in_=row(s0row, i)), i)
                s_vsadd.wait(eng, en, NT)
                s_fin.inc(eng.dma_start(out=vsumH.ap(), in_=vsum[:, :]), 16)
                s_fin.inc(eng.dma_start(
                    out=dram["vsumT"].ap().rearrange("o (c p) -> (o p) c", p=128),
                    in_=vsum[:, :], allow_slow_non_contiguous=True), 16)
                s_fin.wait(eng, en, 32)
                s_g0st.wait_all(eng, en, NT)
                s_s0st.wait_all(eng, en, NT)

        assert s_mix.count == NT and s_mmstat.count == 8 * NT
        assert s_sig.count == 8 * NT and s_kact.count == 8 * NT
        return chains



def build_A2(nc, E, dram, bwait, barrier_end, pool):
    xT, g0T, s0T, vsumH = dram["xT"], dram["g0T"], dram["s0T"], dram["vsumH"]
    x1T, qT, kT, v1, gffT = dram["x1T"], dram["qT"], dram["kT"], dram["v1"], dram["gffT"]
    cc_in, cc_out = dram["cc_in"], dram["cc_out"]

    pctx = ExitStack()
    P = pctx.enter_context
    with pctx:
        vsumt = P(nc.sbuf_tensor("a2_vsumt", [1, NCH * 128], BF16))
        s_prob = pool.get()
        chains = {"vector": Chain(pool.get(), "a2_chv"), "scalar": Chain(pool.get(), "a2_chs"),
                  "gpsimd": Chain(pool.get(), "a2_chg")}

        cctx = ExitStack()
        C = cctx.enter_context
        with cctx:
            bufP = C(nc.sbuf_tensor("a2_bufP", [16, T], F32))
            bufQ = C(nc.sbuf_tensor("a2_bufQ", [16, T], F32))
            bufR = C(nc.sbuf_tensor("a2_bufR", [16, T], F32))
            probsb = C(nc.sbuf_tensor("a2_probsb", [16, T], BF16))
            s_cl = Sem(nc, E, "a2_cl")
            s_ex = pool.get()
            s_cc = Sem(nc, E, "a2_cc")
            with nc.Block() as blk:
                @blk.gpsimd
                def _(eng):
                    en = "gpsimd"
                    eng = CE(eng, chains[en])
                    bwait(eng, en)
                    s_cl.inc(eng.dma_start(out=vsumt[:, :], in_=dram["vsumT"].ap()), 16)
                    s_cl.inc(eng.dma_start(out=bufP[:, :], in_=s0T.ap()), 16)
                    s_ex.wait(eng, en, 1)
                    s_cl.inc(eng.dma_start(out=cc_in.ap(), in_=bufQ[:, :]), 16)
                    s_cl.wait(eng, en, 48)
                    inst = eng.collective_compute(
                        "AllReduce", ALU.add, replica_groups=[list(range(8))],
                        ins=[cc_in.ap().opt()], outs=[cc_out.ap().opt()])
                    s_cc.inc(inst, 1)
                    s_cc.wait(eng, en, 1)
                    s_cl.inc(eng.dma_start(out=bufR[:, :], in_=cc_out.ap()), 16)
                    s_prob.wait(eng, en, 1)
                    s_cl.inc(eng.dma_start(out=dram["probsH"].ap(), in_=probsb[:, :]), 16)
                    s_cl.wait(eng, en, 80)
                    chains[en].flush(eng)

                @blk.scalar
                def _(eng):
                    en = "scalar"
                    eng = CE(eng, chains[en])
                    bwait(eng, en)
                    s_cl.wait(eng, en, 32)
                    inst = eng.activation(out=bufQ[:, :], in_=bufP[:, :], func=AF.Exp, scale=ISQ)
                    s_ex.inc(inst)
                    chains[en].flush(eng)

                @blk.vector
                def _(eng):
                    en = "vector"
                    eng = CE(eng, chains[en])
                    bwait(eng, en)
                    s_cl.wait(eng, en, 64)
                    eng.reciprocal(bufP[:, :], bufR[:, :])
                    inst = eng.tensor_tensor(probsb[:, :], bufQ[:, :], bufP[:, :], ALU.mult)
                    s_prob.inc(inst)
                    chains[en].flush(eng)
            barrier_end(chains)

        ctx = ExitStack()
        A = ctx.enter_context
        with ctx:
            xbuf = A(nc.sbuf_tensor("a2_xbuf", [128, 2 * NCH * FT], F32))
            g0buf = A(nc.sbuf_tensor("a2_g0buf", [128, NCH * FT], BF16))
            x1buf = A(nc.sbuf_tensor("a2_x1buf", [128, 2 * NCH * FT], F32))
            x1bf = A(nc.sbuf_tensor("a2_x1bf", [128, 5 * FT], BF16))
            sqbf = A(nc.sbuf_tensor("a2_sqbf", [128, 5 * FT], BF16))
            xn1 = A(nc.sbuf_tensor("a2_xn1", [128, 2 * NCH * FT], BF16))
            vselw = A(nc.sbuf_tensor("a2_vselw", [16, NCH * 128], BF16))
            pbuf = A(nc.sbuf_tensor("a2_pbuf", [16, 2 * FT], BF16))
            tmpscr = A(nc.sbuf_tensor("a2_tmp", [128, FT], F32))
            t1scr = tmpscr
            qstage = A(nc.sbuf_tensor("a2_qstage", [128, 4 * FT], BF16))
            kstage = A(nc.sbuf_tensor("a2_kstage", [128, 4 * FT], BF16))
            vstage = A(nc.sbuf_tensor("a2_vstage", [128, 4 * D], BF16))
            s1stage = A(nc.sbuf_tensor("a2_s1stage", [64, 2 * FT], BF16))
            bd1s = A(nc.sbuf_tensor("a2_bd1s", [64, 1024], BF16))
            bd2s = A(nc.sbuf_tensor("a2_bd2s", [128, NCH * 64], BF16))
            w1os = A(nc.sbuf_tensor("a2_w1os", [64, 1024], BF16))
            b1fs = A(nc.sbuf_tensor("a2_b1fs", [128, NCH], F32))
            b1b2s = A(nc.sbuf_tensor("a2_b1b2s", [64, 1], F32))
            b1outs = A(nc.sbuf_tensor("a2_b1outs", [128, NCH], F32))
            hstage = A(nc.sbuf_tensor("a2_hstage", [128, 2 * FT], BF16))
            g1sb = A(nc.sbuf_tensor("a2_g1sb", [64, FT], BF16))
            sgscr = A(nc.sbuf_tensor("a2_sgscr", [128, 2 * FT], BF16))
            fscr = A(nc.sbuf_tensor("a2_fscr", [128, FT], F32))
            gstage = A(nc.sbuf_tensor("a2_gstage", [128, NCH * FT], BF16))
            mrow = A(nc.sbuf_tensor("a2_mrow", [1, FT], BF16))
            exrow = A(nc.sbuf_tensor("a2_exrow", [1, FT], F32))
            vrow = A(nc.sbuf_tensor("a2_vrow", [1, FT], F32))
            rrow = A(nc.sbuf_tensor("a2_rrow", [1, FT], BF16))
            srow = A(nc.sbuf_tensor("a2_srow", [1, FT], BF16))
            ones128 = A(nc.sbuf_tensor("a2_ones128", [128, 1], BF16))
            onesrow = A(nc.sbuf_tensor("a2_onesrow", [1, 128], F32))
            hbc = A(nc.sbuf_tensor("a2_hbc", [16, NCH * 128], BF16))
            w1q = A(nc.sbuf_tensor("a2_w1q", [128, NCH * D], BF16))
            w1k = A(nc.sbuf_tensor("a2_w1k", [128, NCH * D], BF16))
            w1v = A(nc.sbuf_tensor("a2_w1v", [128, NCH * D], BF16))
            w1i = A(nc.sbuf_tensor("a2_w1i", [128, NCH * 64], BF16))
            b1qs = A(nc.sbuf_tensor("a2_b1qs", [128, NCH], F32))
            b1ks = A(nc.sbuf_tensor("a2_b1ks", [128, NCH], F32))
            b1ins = A(nc.sbuf_tensor("a2_b1ins", [64, 1], F32))
            bvrow = A(nc.sbuf_tensor("a2_bvrow", [1, D], BF16))
            onesrowb = A(nc.sbuf_tensor("a2_onesrowb", [1, 128], BF16))
            bvbc = A(nc.sbuf_tensor("a2_bvbc", [128, D], BF16))

            psP = [A(nc.psum_tensor(f"a2_psP{k}", [128, 512], F32)) for k in range(2)]
            psLN1 = A(nc.psum_tensor("a2_psLN1", [128, 512], F32))
            psLN2 = A(nc.psum_tensor("a2_psLN2", [128, 512], F32))
            psF = [A(nc.psum_tensor(f"a2_psF{k}", [128, 512], F32)) for k in range(2)]
            psV = A(nc.psum_tensor("a2_psV", [128, 512], F32))
            psW1 = A(nc.psum_tensor("a2_psW1", [64, 512], F32))

            s_ld = SemGroup(nc, E, "a2_ld", 2, per=16)
            s_ldg = SemGroup(nc, E, "a2_ldg", 2, per=16)
            s_ldp = SemGroup(nc, E, "a2_ldp", 2, per=16)
            s_wld = Sem(nc, E, "a2_wld")
            s_init = pool.get()
            s_mmbv = pool.get()
            s_evbv = pool.get()
            s_mmP = pool.get()
            s_sa = pool.get()
            s_x1 = pool.get()
            s_cast = pool.get()
            s_mmstat = pool.get()
            s_rowd = pool.get()
            s_sqr = pool.get()
            s_rowg = pool.get()
            s_bc = pool.get()
            s_xn1 = pool.get()
            s_mmQ = pool.get()
            s_qev = pool.get()
            s_mmK = pool.get()
            s_kev = pool.get()
            s_mmV = pool.get()
            s_vev = pool.get()
            s_mmW1 = pool.get()
            s_s1ev = pool.get()
            s_mmvw = pool.get()
            s_vw = pool.get()
            s_x1st = SemGroup(nc, E, "a2_x1st", 2)
            s_qst = SemGroup(nc, E, "a2_qst", 2)
            s_kst = SemGroup(nc, E, "a2_kst", 2)
            s_vst = SemGroup(nc, E, "a2_vst", 2)
            s_gst = SemGroup(nc, E, "a2_gst", 2)
            s_mmE = pool.get()
            s_h = pool.get()
            s_mmG = pool.get()
            s_g1 = pool.get()
            s_mmW = pool.get()
            s_sg = pool.get()
            s_ep = pool.get()

            def xb(i, c):
                o = ((i % 2) * NCH + c) * FT
                return xbuf[:, o:o + FT]

            def g0s(i, c):
                return g0buf[:, c * FT:(c + 1) * FT]

            def x1s(i, c):
                o = ((i % 2) * NCH + c) * FT
                return x1buf[:, o:o + FT]

            def xn1s(i, c):
                o = ((i % 2) * NCH + c) * FT
                return xn1[:, o:o + FT]

            def wsl(w, c, o):
                return w[:, c * D + o * 128:c * D + o * 128 + 128]

            def row(buf, i):
                if buf.shape[1] == FT:
                    return buf[:, 0:FT]
                o = (i % 2) * FT
                return buf[:, o:o + FT]

            with nc.Block() as blk:
                @blk.gpsimd
                def _(eng):
                    en = "gpsimd"
                    eng = CE(eng, chains[en])
                    bwait(eng, en)
                    s_wld.inc(eng.dma_start(out=hbc[:, :], in_=dram["hselbc"].ap()), 16)
                    for nm, dst in (("w1qT", w1q), ("w1kT", w1k), ("w1vT", w1v)):
                        src = dram[nm].ap().rearrange("(c p) n -> p c n", p=128)
                        d2 = dst[:, :].rearrange("p (c n) -> p c n", c=NCH)
                        s_wld.inc(eng.dma_start(out=d2, in_=src), 16)
                    src = dram["w1in"].ap().rearrange("(c p) n -> p c n", p=128)
                    d2 = w1i[:, :].rearrange("p (c n) -> p c n", c=NCH)
                    s_wld.inc(eng.dma_start(out=d2, in_=src), 16)
                    s_wld.inc(eng.dma_start(out=b1qs[:, :], in_=dram["b1q"].ap()), 16)
                    s_wld.inc(eng.dma_start(out=b1ks[:, :], in_=dram["b1k"].ap()), 16)
                    s_wld.inc(eng.dma_start(out=b1ins[:, :], in_=dram["b1in"].ap()), 16)
                    s_wld.inc(eng.dma_start(out=bvrow[:, :], in_=dram["b1vrow"].ap()), 16)
                    s_wld.inc(eng.dma_start(out=bd1s[:, :], in_=dram["bd1"].ap()), 16)
                    src2 = dram["bd2"].ap().rearrange("(c p) n -> p c n", p=128)
                    d2b = bd2s[:, :].rearrange("p (c n) -> p c n", c=NCH)
                    s_wld.inc(eng.dma_start(out=d2b, in_=src2), 16)
                    s_wld.inc(eng.dma_start(out=w1os[:, :], in_=dram["w1out"].ap()), 16)
                    s_wld.inc(eng.dma_start(out=b1fs[:, :], in_=dram["b1f"].ap()), 16)
                    s_wld.inc(eng.dma_start(out=b1b2s[:, :], in_=dram["b1b2"].ap()), 16)
                    s_wld.inc(eng.dma_start(out=b1outs[:, :], in_=dram["b1out"].ap()), 16)
                    chains[en].flush(eng)

                @blk.vector
                def _(eng):
                    en = "vector"
                    eng = CE(eng, chains[en])
                    bwait(eng, en)
                    s_init.inc(eng.memset(ones128[:, :], 1.0))
                    s_init.inc(eng.memset(onesrow[:, :], 1.0))
                    s_init.inc(eng.memset(onesrowb[:, :], 1.0))
                    for h in range(2):
                        s_mmbv.wait(eng, en, h + 1)
                        inst = eng.tensor_copy(bvbc[:, h * 512:(h + 1) * 512], psP[h][:, :])
                        s_evbv.inc(inst)
                    for c in range(NCH):
                        s_mmvw.wait(eng, en, c + 1)
                        inst = eng.tensor_tensor(vselw[:, c * 128:(c + 1) * 128],
                                                 hbc[:, c * 128:(c + 1) * 128],
                                                 psP[c % 2][0:16, 0:128], ALU.mult)
                        s_vw.inc(inst)

                    def v_x1chain(j):
                        s_ld.wait_done(eng, en, j)
                        s_ldg.wait_done(eng, en, j)
                        s_x1st.wait_done(eng, en, j - 2)
                        s_cast.wait(eng, en, 16 * (j - 1))
                        for c in range(NCH):
                            s_mmP.wait(eng, en, 8 * j + c + 1)
                            inst = eng.tensor_tensor(t1scr[:, :], psP[c % 2][:, :], g0s(j, c),
                                                     ALU.mult)
                            s_sa.inc(inst)
                            inst = eng.tensor_tensor(x1s(j, c), t1scr[:, :], xb(j, c), ALU.add)
                        s_x1.inc(inst)

                    def v_rows(j):
                        s_mmstat.wait(eng, en, 8 * (j + 1))
                        s_bc.wait(eng, en, j)
                        eng.tensor_scalar(row(mrow, j), psLN1[0:1, :], 1.0 / D, None, ALU.mult)
                        eng.tensor_scalar(row(exrow, j), psLN2[0:1, :], 1.0 / D, None, ALU.mult)
                        eng.tensor_tensor(row(vrow, j), row(mrow, j), row(mrow, j), ALU.mult)
                        eng.tensor_tensor(row(vrow, j), row(exrow, j), row(vrow, j), ALU.subtract)
                        inst = eng.tensor_scalar(row(vrow, j), row(vrow, j), EPS, None, ALU.add)
                        s_rowd.inc(inst)
                        s_sqr.wait(eng, en, j + 1)
                        eng.reciprocal(row(vrow, j), srow[:, :])
                        inst = eng.tensor_copy(row(rrow, j), row(vrow, j))
                        s_rowg.inc(inst)

                    def v_xn1(j):
                        s_bc.wait(eng, en, j + 1)
                        s_mmW1.wait(eng, en, j - 1)
                        for c in range(NCH):
                            eng.tensor_tensor(tmpscr[:, :], x1s(j, c), psLN1[:, :], ALU.subtract)
                            inst = eng.tensor_tensor(xn1s(j, c), tmpscr[:, :],
                                                     psLN2[:, :], ALU.mult)
                        s_xn1.inc(inst)

                    def v_xn1_chunk(j, c):
                        if c == 0:
                            s_bc.wait(eng, en, j + 1)
                            s_mmW1.wait(eng, en, j - 1)
                        eng.tensor_tensor(tmpscr[:, :], x1s(j, c), psLN1[:, :], ALU.subtract)
                        inst = eng.tensor_tensor(xn1s(j, c), tmpscr[:, :],
                                                 psLN2[:, :], ALU.mult)
                        if c == NCH - 1:
                            s_xn1.inc(inst)

                    XNDIST = {3: [0], 4: [1], 5: [2], 6: [3], 7: [4, 5, 6, 7]}
                    v_x1chain(0)
                    v_rows(0)
                    v_xn1(0)
                    for i in range(NT):
                        if i + 1 < NT:
                            v_x1chain(i + 1)
                            v_rows(i + 1)
                        s_vst.wait_done(eng, en, i - 1)
                        for g in range(4):
                            for eh in range(2):
                                n = 2 * g + eh
                                s_mmV.wait(eng, en, 8 * i + n + 1)
                                inst = eng.tensor_tensor(
                                    vstage[:, g * D + eh * 512:g * D + eh * 512 + 512],
                                    psV[:, :], bvbc[:, eh * 512:(eh + 1) * 512], ALU.add)
                                s_vev.inc(inst)
                                if i + 1 < NT:
                                    for c in XNDIST.get(n, []):
                                        v_xn1_chunk(i + 1, c)
                        if i >= 1:
                            k = i - 1
                            s_gst.wait_done(eng, en, k - 1)
                            for c in range(NCH):
                                s_sg.wait(eng, en, 8 * k + c + 1)
                                eng.tensor_scalar(fscr[:, :], psP[0][:, :], b1outs[:, c:c + 1],
                                                  None, ALU.add)
                                inst = eng.tensor_tensor(gstage[:, c * FT:(c + 1) * FT],
                                                         fscr[:, :],
                                                         sgscr[:, (c % 2) * FT:(c % 2 + 1) * FT],
                                                         ALU.mult)
                                s_ep.inc(inst)
                        s_mmW1.wait(eng, en, i + 1)
                        s_mmE.wait(eng, en, 8 * (i - 1))
                        inst = eng.tensor_scalar(row(s1stage, i), psW1[:, :], b1ins[:, :],
                                                 None, ALU.add)
                        s_s1ev.inc(inst)
                    k = NT - 1
                    s_gst.wait_done(eng, en, k - 1)
                    for c in range(NCH):
                        s_sg.wait(eng, en, 8 * k + c + 1)
                        eng.tensor_scalar(fscr[:, :], psP[0][:, :], b1outs[:, c:c + 1],
                                          None, ALU.add)
                        inst = eng.tensor_tensor(gstage[:, c * FT:(c + 1) * FT], fscr[:, :],
                                                 sgscr[:, (c % 2) * FT:(c % 2 + 1) * FT],
                                                 ALU.mult)
                        s_ep.inc(inst)
                    chains[en].flush(eng)

                @blk.scalar
                def _(eng):
                    en = "scalar"
                    eng = CE(eng, chains[en])
                    bwait(eng, en)

                    def s_cast1(j, c):
                        if c == 0:
                            s_x1.wait(eng, en, j + 1)
                        s_mmstat.wait(eng, en, 8 * j + c - 4)
                        sl = ((8 * j + c) % 5) * FT
                        s_cast.inc(eng.copy(x1bf[:, sl:sl + FT], x1s(j, c)))
                        s_cast.inc(eng.activation(out=sqbf[:, sl:sl + FT],
                                                  in_=x1s(j, c), func=AF.Square))

                    def s_casts(j):
                        for c in range(NCH):
                            s_cast1(j, c)

                    def s_sqrt(j):
                        s_rowd.wait(eng, en, j + 1)
                        s_sqr.inc(eng.activation(out=srow[:, :], in_=row(vrow, j), func=AF.Sqrt))

                    def s_ffn_relu(k, cs):
                        for c in cs:
                            s_mmE.wait(eng, en, 8 * k + c + 1)
                            s_mmG.wait(eng, en, 8 * k + c - 1)
                            inst = eng.activation(out=hstage[:, (c % 2) * FT:(c % 2 + 1) * FT],
                                                  in_=psP[0][:, :], func=AF.Relu,
                                                  bias=b1fs[:, c:c + 1], scale=1.0)
                            s_h.inc(inst)

                    def s_ffn_g1(k):
                        s_mmG.wait(eng, en, 8 * (k + 1))
                        s_mmW.wait(eng, en, 8 * k)
                        inst = eng.activation(out=g1sb[:, :], in_=psP[1][0:64, :], func=AF.Identity,
                                              bias=b1b2s[:, :], scale=1.0)
                        s_g1.inc(inst)

                    def s_ffn(k):
                        s_ffn_relu(k, range(NCH))
                        s_ffn_g1(k)

                    def s_sgs(k):
                        for c in range(NCH):
                            s_mmW.wait(eng, en, 8 * k + c + 1)
                            s_ep.wait(eng, en, 8 * k + c - 1)
                            inst = eng.activation(out=sgscr[:, (c % 2) * FT:(c % 2 + 1) * FT],
                                                  in_=psP[0][:, :], func=AF.Sigmoid,
                                                  bias=b1outs[:, c:c + 1], scale=1.0)
                            s_sg.inc(inst)

                    s_casts(0)
                    s_sqrt(0)
                    for i in range(NT):
                        for o in range(NCH):
                            s_mmQ.wait(eng, en, 8 * i + o + 1)
                            s_qst.wait_done(eng, en, 2 * i + o // 4 - 1)
                            inst = eng.activation(out=qstage[:, (o % 4) * FT:(o % 4 + 1) * FT],
                                                  in_=psF[o % 2][:, :], func=AF.Identity,
                                                  bias=b1qs[:, o:o + 1], scale=1.0)
                            s_qev.inc(inst)
                        for o in range(NCH):
                            s_mmK.wait(eng, en, 8 * i + o + 1)
                            s_kst.wait_done(eng, en, 2 * i + o // 4 - 1)
                            inst = eng.activation(out=kstage[:, (o % 4) * FT:(o % 4 + 1) * FT],
                                                  in_=psF[o % 2][:, :], func=AF.Identity,
                                                  bias=b1ks[:, o:o + 1], scale=1.0)
                            s_kev.inc(inst)
                            if i + 1 < NT and o < 5:
                                s_cast1(i + 1, o)
                        if i + 1 < NT:
                            s_cast1(i + 1, 5)
                            s_cast1(i + 1, 6)
                            s_cast1(i + 1, 7)
                        if i >= 1:
                            s_ffn_relu(i - 1, range(0, 4))
                        if i + 1 < NT:
                            s_sqrt(i + 1)
                        if i >= 1:
                            s_ffn_relu(i - 1, range(4, NCH))
                            s_ffn_g1(i - 1)
                            s_sgs(i - 1)
                    s_ffn(NT - 1)
                    s_sgs(NT - 1)
                    chains[en].flush(eng)

                @blk.tensor
                def _(eng):
                    en = "tensor"
                    bwait(eng, en)
                    s_wld.wait(eng, en, 16 * 15)
                    s_init.wait(eng, en, 3)
                    for h in range(2):
                        inst = eng.matmul(psP[h][:, :], onesrowb[:, :],
                                          bvrow[:, h * 512:(h + 1) * 512], start=True, stop=True)
                        s_mmbv.inc(inst)
                    for c in range(NCH):
                        if c < 2:
                            s_evbv.wait(eng, en, c + 1)
                        else:
                            s_vw.wait(eng, en, c - 1)
                        inst = eng.matmul(psP[c % 2][0:16, 0:128], onesrowb[0:1, 0:16],
                                          vsumt[0:1, c * 128:(c + 1) * 128], start=True, stop=True)
                        s_mmvw.inc(inst)

                    def t_bcastP(j):
                        s_ldp.wait_done(eng, en, j)
                        s_ep.wait(eng, en, 8 * (j - 2))
                        s_g1.wait(eng, en, j - 2)
                        for c in range(NCH):
                            if j == 0:
                                s_vw.wait(eng, en, 8)
                            s_sa.wait(eng, en, 8 * j + c - 1)
                            inst = eng.matmul(psP[c % 2][:, :], vselw[:, c * 128:(c + 1) * 128],
                                              pbuf[:, (j % 2) * FT:(j % 2 + 1) * FT],
                                              start=True, stop=True)
                            s_mmP.inc(inst)

                    def t_stats(j):
                        s_xn1.wait(eng, en, j)
                        for c in range(NCH):
                            s_cast.wait(eng, en, 16 * j + 2 * (c + 1))
                            sl = ((8 * j + c) % 5) * FT
                            eng.matmul(psLN1[0:1, :], ones128[:, :], x1bf[:, sl:sl + FT],
                                       start=(c == 0), stop=(c == NCH - 1))
                            inst = eng.matmul(psLN2[0:1, :], ones128[:, :], sqbf[:, sl:sl + FT],
                                              start=(c == 0), stop=(c == NCH - 1))
                            s_mmstat.inc(inst)

                    def t_lnbcast(j):
                        s_rowg.wait(eng, en, j + 1)
                        eng.matmul(psLN1[:, :], onesrowb[:, :], row(mrow, j), start=True, stop=True)
                        inst = eng.matmul(psLN2[:, :], onesrowb[:, :], row(rrow, j),
                                          start=True, stop=True)
                        s_bc.inc(inst)

                    def t_ffn(k):
                        s_s1ev.wait(eng, en, k + 1)
                        for c in range(NCH):
                            if c == 0:
                                s_sa.wait(eng, en, min(8 * (k + 3), 8 * NT))
                                s_ep.wait(eng, en, 8 * k)
                            else:
                                s_h.wait(eng, en, 8 * k + c)
                            inst = eng.matmul(psP[0][:, :], bd1s[:, c * 128:(c + 1) * 128],
                                              s1stage[:, (k % 2) * FT:(k % 2 + 1) * FT],
                                              start=True, stop=True)
                            s_mmE.inc(inst)
                            if c >= 1:
                                cc = c - 1
                                s_h.wait(eng, en, 8 * k + cc + 1)
                                if cc == 0:
                                    s_g1.wait(eng, en, k)
                                inst = eng.matmul(psP[1][0:64, :], bd2s[:, cc * 64:(cc + 1) * 64],
                                                  hstage[:, (cc % 2) * FT:(cc % 2 + 1) * FT],
                                                  start=(cc == 0), stop=False)
                                s_mmG.inc(inst)
                        cc = NCH - 1
                        s_h.wait(eng, en, 8 * k + cc + 1)
                        inst = eng.matmul(psP[1][0:64, :], bd2s[:, cc * 64:(cc + 1) * 64],
                                          hstage[:, (cc % 2) * FT:(cc % 2 + 1) * FT],
                                          start=False, stop=True)
                        s_mmG.inc(inst)

                    def t_wout(k):
                        s_g1.wait(eng, en, k + 1)
                        for c in range(NCH):
                            if c >= 1:
                                s_ep.wait(eng, en, 8 * k + c)
                            inst = eng.matmul(psP[0][:, :], w1os[:, c * 128:(c + 1) * 128],
                                              g1sb[:, :], start=True, stop=True)
                            s_mmW.inc(inst)

                    t_bcastP(0)
                    t_stats(0)
                    t_lnbcast(0)
                    for i in range(NT):
                        s_xn1.wait(eng, en, i + 1)
                        for o in range(NCH):
                            if o < 2:
                                s_kev.wait(eng, en, 8 * (i - 1) + o + 7)
                            else:
                                s_qev.wait(eng, en, 8 * i + o - 1)
                            for c in range(NCH):
                                inst = eng.matmul(psF[o % 2][:, :], wsl(w1q, c, o),
                                                  xn1s(i, c),
                                                  start=(c == 0), stop=(c == NCH - 1))
                            s_mmQ.inc(inst)
                        if i + 1 < NT:
                            t_bcastP(i + 1)
                        for o in range(NCH):
                            if o == 0:
                                s_qev.wait(eng, en, 8 * i + 7)
                            elif o == 1:
                                s_qev.wait(eng, en, 8 * i + 8)
                            else:
                                s_kev.wait(eng, en, 8 * i + o - 1)
                            for c in range(NCH):
                                inst = eng.matmul(psF[o % 2][:, :], wsl(w1k, c, o),
                                                  xn1s(i, c),
                                                  start=(c == 0), stop=(c == NCH - 1))
                            s_mmK.inc(inst)
                        if i + 1 < NT:
                            t_stats(i + 1)
                        if i >= 1:
                            t_ffn(i - 1)
                        for g in range(4):
                            for eh in range(2):
                                n = 2 * g + eh
                                s_vev.wait(eng, en, 8 * i + n)
                                for c in range(NCH):
                                    base = ((i % 2) * NCH + c) * FT
                                    lhs = xn1[:, base + g * 128:base + (g + 1) * 128]
                                    rhs = w1v[:, c * D + eh * 512:c * D + eh * 512 + 512]
                                    inst = eng.matmul(psV[:, :], lhs, rhs,
                                                      start=(c == 0), stop=(c == NCH - 1))
                                s_mmV.inc(inst)
                                if n == 2 and i + 1 < NT:
                                    t_lnbcast(i + 1)
                        if i >= 1:
                            t_wout(i - 1)
                        s_s1ev.wait(eng, en, i)
                        for c in range(NCH):
                            inst = eng.matmul(psW1[:, :], w1i[:, c * 64:(c + 1) * 64],
                                              xn1s(i, c),
                                              start=(c == 0), stop=(c == NCH - 1))
                        s_mmW1.inc(inst)
                    t_ffn(NT - 1)
                    t_wout(NT - 1)

                @blk.sync
                def _(eng):
                    en = "sync"
                    bwait(eng, en)
                    for i in range(min(2, NT)):
                        s_ld.inc(chunked_load(eng, xbuf, (i % 2) * NCH * FT, xT, i * FT, FT), i)
                        s_ldp.inc(eng.dma_start(
                            out=pbuf[:, (i % 2) * FT:(i % 2 + 1) * FT],
                            in_=dram["probsH"].ap()[:, i * FT:(i + 1) * FT]), i)
                    s_ldg.inc(chunked_load(eng, g0buf, 0, g0T, 0, FT), 0)
                    for i in range(NT):
                        if i + 1 < NT:
                            s_x1.wait(eng, en, i + 1)
                            s_ldg.inc(chunked_load(eng, g0buf, 0, g0T, (i + 1) * FT, FT), i + 1)
                        s_x1.wait(eng, en, i + 1)
                        s_x1st.inc(chunked_store(eng, x1T, i * FT, FT, x1buf, (i % 2) * NCH * FT), i)
                        for h in range(2):
                            s_qev.wait(eng, en, 8 * i + 4 * (h + 1))
                            dst = qT.ap()[h * 512:(h + 1) * 512, i * FT:(i + 1) * FT] \
                                .rearrange("(c p) n -> p c n", p=128)
                            srcq = qstage[:, :].rearrange("p (c n) -> p c n", c=4)
                            s_qst.inc(eng.dma_start(out=dst, in_=srcq), 2 * i + h)
                            s_kev.wait(eng, en, 8 * i + 4 * (h + 1))
                            dstk = kT.ap()[h * 512:(h + 1) * 512, i * FT:(i + 1) * FT] \
                                .rearrange("(c p) n -> p c n", p=128)
                            srck = kstage[:, :].rearrange("p (c n) -> p c n", c=4)
                            s_kst.inc(eng.dma_start(out=dstk, in_=srck), 2 * i + h)
                        s_vev.wait(eng, en, 8 * (i + 1))
                        dst = v1.ap()[i * FT:(i + 1) * FT, :].rearrange("(g p) e -> p g e", p=128)
                        srcap = vstage[:, :].rearrange("p (g e) -> p g e", g=4)
                        s_vst.inc(eng.dma_start(out=dst, in_=srcap), i)
                        if i >= 1:
                            s_ep.wait(eng, en, 8 * i)
                            s_gst.inc(chunked_store(eng, gffT, (i - 1) * FT, FT, gstage, 0), i - 1)
                        if i + 2 < NT:
                            s_x1.wait(eng, en, i + 1)
                            s_ld.inc(chunked_load(eng, xbuf, (i % 2) * NCH * FT, xT,
                                                  (i + 2) * FT, FT), i + 2)
                            s_mmP.wait(eng, en, 8 * (i + 1))
                            s_ldp.inc(eng.dma_start(
                                out=pbuf[:, (i % 2) * FT:(i % 2 + 1) * FT],
                                in_=dram["probsH"].ap()[:, (i + 2) * FT:(i + 3) * FT]), i + 2)
                    s_ep.wait(eng, en, 8 * NT)
                    s_gst.inc(chunked_store(eng, gffT, (NT - 1) * FT, FT, gstage, 0), NT - 1)
                    s_x1st.wait_all(eng, en, NT)
                    s_qst.wait_all(eng, en, 2 * NT)
                    s_kst.wait_all(eng, en, 2 * NT)
                    s_vst.wait_all(eng, en, NT)
                    s_gst.wait_all(eng, en, NT)

            assert s_mmP.count == 8 * NT
            assert s_vev.count == 8 * NT
        return chains



def build_B(nc, E, dram, bwait, pool):
    x1T, qT, kT, v1, gffT, yT = (dram["x1T"], dram["qT"], dram["kT"], dram["v1"],
                                 dram["gffT"], dram["yT"])
    ctx = ExitStack()
    A = ctx.enter_context
    with ctx:
        qbuf = A(nc.sbuf_tensor("b_qbuf", [128, 2 * NCH * NQ], BF16))
        kbuf = A(nc.sbuf_tensor("b_kbuf", [128, 2 * NCH * KV], BF16))
        vring = A(nc.sbuf_tensor("b_vring", [128, SEG * D], BF16))
        pring = A(nc.sbuf_tensor("b_pring", [128, SEG * NQ], BF16))
        oacc = A(nc.sbuf_tensor("b_oacc", [128, 2 * NCH * NQ], F32))
        gffpan = A(nc.sbuf_tensor("b_gffpan", [128, 2 * NCH * NQ], BF16))
        x1pan = A(nc.sbuf_tensor("b_x1pan", [128, NCH * NQ], F32))
        outsb = A(nc.sbuf_tensor("b_outsb", [128, NCH * NQ], F32))
        rden = A(nc.sbuf_tensor("b_rden", [1, NQ], F32))
        maskbf = A(nc.sbuf_tensor("b_maskbf", [128, 4 * NQ], BF16))
        onescol = A(nc.sbuf_tensor("b_onescol", [128, 1], BF16))
        onesrow = A(nc.sbuf_tensor("b_onesrow", [1, 128], F32))

        psS = [A(nc.psum_tensor(f"b_psS{k}", [128, 512], F32)) for k in range(2)]
        psD = A(nc.psum_tensor("b_psD", [1, 512], F32))
        psO = [A(nc.psum_tensor(f"b_psO{k}", [128, 512], F32)) for k in range(4)]
        psBC = A(nc.psum_tensor("b_psBC", [128, 512], F32))

        s_k = SemGroup(nc, E, "b_k", 2)
        s_v = SemGroup(nc, E, "b_v", SEG)
        s_qld = SemGroup(nc, E, "b_qld", 2)
        s_x1ld = SemGroup(nc, E, "b_x1ld", 2)
        s_gld = SemGroup(nc, E, "b_gld", 2)
        s_wld = Sem(nc, E, "b_wld")
        s_yst = Sem(nc, E, "b_yst")
        s_init = pool.get()
        s_mmS = pool.get()
        s_exp = pool.get()
        s_msk = pool.get()
        s_tb = pool.get()
        s_swp = pool.get()
        s_oev = pool.get()
        s_mmD = pool.get()
        s_recD = pool.get()
        s_mmBc = pool.get()
        s_sat = pool.get()
        s_out = pool.get()
        chains = {"vector": Chain(pool.get(), "b_chv"), "scalar": Chain(pool.get(), "b_chs"),
                  "gpsimd": Chain(pool.get(), "b_chg")}

        def kslice(jg, c):
            o = (jg % 2) * NCH * KV + c * KV
            return kbuf[:, o:o + KV]

        def qslice(p, c):
            o = (p % 2) * NCH * NQ + c * NQ
            return qbuf[:, o:o + NQ]

        def vslice(jg, c):
            o = (jg % SEG) * D + c * 128
            return vring[:, o:o + 128]

        def pslice(jg):
            o = (jg % SEG) * NQ
            return pring[:, o:o + NQ]

        def oslice(p, c):
            o = ((p % 2) * NCH + c) * NQ
            return oacc[:, o:o + NQ]

        def gslice(p, c):
            o = ((p % 2) * NCH + c) * NQ
            return gffpan[:, o:o + NQ]

        def is_masked(p, j):
            return j >= tiles_in_panel(p) - 4

        nmask_before = {}
        cnt = 0
        for p in range(NP):
            for j in range(tiles_in_panel(p)):
                nmask_before[(p, j)] = cnt
                if is_masked(p, j):
                    cnt += 1

        def seg_tiles(p, s):
            J = tiles_in_panel(p)
            return list(range(s * SEG, min(J, (s + 1) * SEG)))

        with nc.Block() as blk:
            @blk.gpsimd
            def _(eng):
                en = "gpsimd"
                eng = CE(eng, chains[en])
                bwait(eng, en)
                s_wld.inc(eng.dma_start(out=maskbf[:, :], in_=dram["masks"].ap()), 16)
                for p in range(min(2, NP)):
                    s_qld.inc(chunked_load(eng, qbuf, (p % 2) * NCH * NQ, qT, p * NQ, NQ), p)
                    s_gld.inc(chunked_load(eng, gffpan, (p % 2) * NCH * NQ, gffT, p * NQ, NQ), p)
                s_x1ld.inc(chunked_load(eng, x1pan, 0, x1T, 0, NQ), 0)
                for p in range(NP):
                    s_out.wait(eng, en, p + 1)
                    s_yst.inc(chunked_store(eng, yT, p * NQ, NQ, outsb, 0), 16)
                    if p + 1 < NP:
                        s_x1ld.inc(chunked_load(eng, x1pan, 0, x1T, (p + 1) * NQ, NQ), p + 1)
                    if p + 2 < NP:
                        s_mmS.wait(eng, en, tiles_before_panel(p + 1))
                        s_qld.inc(chunked_load(eng, qbuf, ((p + 2) % 2) * NCH * NQ, qT,
                                               (p + 2) * NQ, NQ), p + 2)
                        s_gld.inc(chunked_load(eng, gffpan, ((p + 2) % 2) * NCH * NQ, gffT,
                                               (p + 2) * NQ, NQ), p + 2)
                s_yst.wait(eng, en, 16 * NP)

            @blk.sync
            def _(eng):
                en = "sync"
                bwait(eng, en)
                for p in range(NP):
                    jg0 = tiles_before_panel(p)
                    for j in range(tiles_in_panel(p)):
                        jg = jg0 + j
                        kv0 = j * KV
                        s_mmS.wait(eng, en, jg - 1)
                        srcap = kT.ap()[:, kv0:kv0 + KV].rearrange("(c p) n -> p c n", p=128)
                        dst = kbuf[:, (jg % 2) * NCH * KV:((jg % 2) + 1) * NCH * KV].rearrange(
                            "p (c n) -> p c n", c=NCH)
                        s_k.inc(eng.dma_start(out=dst, in_=srcap), jg)
                        s_tb.wait(eng, en, jg - (SEG - 1))
                        s_v.inc(eng.dma_start(out=vring[:, (jg % SEG) * D:((jg % SEG) + 1) * D],
                                              in_=v1.ap()[kv0:kv0 + KV, :]), jg)

            @blk.tensor
            def _(eng):
                en = "tensor"
                bwait(eng, en)
                s_init.wait(eng, en, 2)
                sg_global = 0
                for p in range(NP):
                    jg0 = tiles_before_panel(p)
                    J = tiles_in_panel(p)
                    nseg = segs_in_panel(p)
                    s_qld.wait_done(eng, en, p)

                    def emit_S(j):
                        jg = jg0 + j
                        s_k.wait_done(eng, en, jg)
                        s_exp.wait(eng, en, jg - 1)
                        for c in range(NCH):
                            inst = eng.matmul(psS[jg % 2][:, :], kslice(jg, c), qslice(p, c),
                                              start=(c == 0), stop=(c == NCH - 1))
                        s_mmS.inc(inst)

                    for s in range(nseg):
                        sg = sg_global
                        tiles = seg_tiles(p, s)
                        if s == 0:
                            emit_S(tiles[0])
                        for idx, j in enumerate(tiles):
                            jg = jg0 + j
                            if idx + 1 < len(tiles):
                                emit_S(tiles[idx + 1])
                            elif (s + 1) * SEG < J:
                                emit_S((s + 1) * SEG)
                            if is_masked(p, j):
                                s_msk.wait(eng, en, nmask_before[(p, j)] + 1)
                            else:
                                s_exp.wait(eng, en, jg + 1)
                            if j == 0:
                                s_recD.wait(eng, en, p)
                            inst = eng.matmul(psD[:, :], onescol[:, :], pslice(jg),
                                              start=(j == 0), stop=(j == J - 1))
                            if j == J - 1:
                                s_mmD.inc(inst)
                            s_v.wait_done(eng, en, jg)
                            if idx == 0:
                                s_oev.wait(eng, en, 8 * sg)
                            for c in range(4):
                                inst = eng.matmul(psO[c][:, :], vslice(jg, c), pslice(jg),
                                                  start=(idx == 0), stop=(idx == len(tiles) - 1))
                            if idx == len(tiles) - 1:
                                s_swp.inc(inst)
                        s_oev.wait(eng, en, 8 * sg + 4)
                        for idx, j in enumerate(tiles):
                            jg = jg0 + j
                            for c in range(4, 8):
                                inst = eng.matmul(psO[c - 4][:, :], vslice(jg, c), pslice(jg),
                                                  start=(idx == 0), stop=(idx == len(tiles) - 1))
                            s_tb.inc(inst)
                        sg_global += 1
                    s_recD.wait(eng, en, p + 1)
                    s_sat.wait(eng, en, p)
                    inst = eng.matmul(psBC[:, :], onesrow[:, :], rden[:, :], start=True, stop=True)
                    s_mmBc.inc(inst)

            @blk.scalar
            def _(eng):
                en = "scalar"
                eng = CE(eng, chains[en])
                bwait(eng, en)
                s_wld.wait(eng, en, 16)
                for p in range(NP):
                    jg0 = tiles_before_panel(p)
                    for j in range(tiles_in_panel(p)):
                        jg = jg0 + j
                        s_mmS.wait(eng, en, jg + 1)
                        s_tb.wait(eng, en, jg - (SEG - 1))
                        inst = eng.activation(out=pslice(jg), in_=psS[jg % 2][:, :],
                                              func=AF.Exp, scale=ISQ)
                        s_exp.inc(inst)

            @blk.vector
            def _(eng):
                en = "vector"
                eng = CE(eng, chains[en])
                bwait(eng, en)
                s_wld.wait(eng, en, 16)
                s_init.inc(eng.memset(onescol[:, :], 1.0))
                s_init.inc(eng.memset(onesrow[:, :], 1.0))
                sg_global = 0
                for p in range(NP):
                    jg0 = tiles_before_panel(p)
                    J = tiles_in_panel(p)
                    nseg = segs_in_panel(p)
                    for s in range(nseg):
                        sg = sg_global
                        tiles = seg_tiles(p, s)
                        for j in tiles:
                            if is_masked(p, j):
                                jg = jg0 + j
                                jd = j - (J - 4)
                                s_exp.wait(eng, en, jg + 1)
                                inst = eng.tensor_tensor(pslice(jg), pslice(jg),
                                                         maskbf[:, jd * NQ:(jd + 1) * NQ],
                                                         ALU.mult)
                                s_msk.inc(inst)
                        s_swp.wait(eng, en, sg + 1)
                        for c in range(4):
                            dst = oslice(p, c)
                            if s == 0:
                                inst = eng.tensor_copy(dst, psO[c][:, :])
                            else:
                                inst = eng.tensor_tensor(dst, dst, psO[c][:, :], ALU.add)
                            s_oev.inc(inst)
                        s_tb.wait(eng, en, jg0 + tiles[-1] + 1)
                        for c in range(4, 8):
                            dst = oslice(p, c)
                            if s == 0:
                                inst = eng.tensor_copy(dst, psO[c - 4][:, :])
                            else:
                                inst = eng.tensor_tensor(dst, dst, psO[c - 4][:, :], ALU.add)
                            s_oev.inc(inst)
                        sg_global += 1
                    s_mmD.wait(eng, en, p + 1)
                    inst = eng.reciprocal(rden[:, :], psD[0:1, 0:NQ])
                    s_recD.inc(inst)
                    s_mmBc.wait(eng, en, p + 1)
                    s_x1ld.wait_done(eng, en, p)
                    s_gld.wait_done(eng, en, p)
                    s_yst.wait(eng, en, 16 * p)
                    for c in range(NCH):
                        inst = eng.tensor_tensor(oslice(p, c), oslice(p, c), psBC[:, :], ALU.mult)
                        if c == NCH - 1:
                            s_sat.inc(inst)
                        eng.tensor_tensor(outsb[:, c * NQ:(c + 1) * NQ], oslice(p, c),
                                          gslice(p, c), ALU.mult)
                        inst = eng.tensor_tensor(outsb[:, c * NQ:(c + 1) * NQ],
                                                 outsb[:, c * NQ:(c + 1) * NQ],
                                                 x1pan[:, c * NQ:(c + 1) * NQ], ALU.add)
                    s_out.inc(inst)
                chains[en].flush(eng)

        assert s_exp.count == TOTAL_KV
        assert s_out.count == NP
        return chains



def kernel(**inputs):
    from concourse.bass_utils import run_bass_kernel_spmd
    C = prep_common(inputs)
    xTs = prep_percore(inputs)
    nc, input_names = make_program("FUSED")
    in_maps = []
    for c in range(8):
        m = {}
        for n in input_names:
            m[n] = xTs[c] if n == "xT" else C[n]
        in_maps.append(m)
    res = run_bass_kernel_spmd(nc, in_maps, core_ids=list(range(8)))
    out = np.stack([np.asarray(res.results[c]["yT"]).T for c in range(8)])
    return np.ascontiguousarray(out.astype(np.float32))


# version 12
# speedup vs baseline: 1.2331x; 1.0960x over previous
import numpy as np
import concourse.bass as bass
import concourse.mybir as mybir
from contextlib import ExitStack

F32 = mybir.dt.float32
BF16 = mybir.dt.bfloat16
AF = mybir.ActivationFunctionType
ALU = mybir.AluOpType

T = 8192
D = 1024
NCH = 8
FT = 512
NT = T // FT
NQ = 512
NP = T // NQ
KV = 128
SEG = 16
EPS = 1e-5
ISQ = 1.0 / 32.0


def tiles_in_panel(p):
    return 4 * p + 4


def tiles_before_panel(p):
    return 2 * p * p + 2 * p


def segs_in_panel(p):
    return (tiles_in_panel(p) + SEG - 1) // SEG


def segs_before_panel(p):
    return sum(segs_in_panel(q) for q in range(p))


TOTAL_KV = tiles_before_panel(NP)


LASTINC = {}
KEEPALIVE = []


class Sem:
    def __init__(self, nc, E, name):
        self.h = E(nc.semaphore(name))
        self.name = name
        self.count = 0
        self.hw = {}

    def inc(self, inst, n=1):
        inst.then_inc(self.h, n)
        self.count += n
        KEEPALIVE.append(inst)
        LASTINC[id(inst)] = (self, self.count)

    def wait(self, eng, ename, val):
        if val <= 0:
            return
        if self.hw.get(ename, 0) >= val:
            return
        self.hw[ename] = val
        eng.wait_ge(self.h, val)


class PhaseSem:

    def __init__(self, s):
        self.s = s
        self.base = s.count
        self.name = s.name

    def inc(self, inst, n=1):
        self.s.inc(inst, n)

    @property
    def count(self):
        return self.s.count - self.base

    def wait(self, eng, ename, val):
        if val <= 0:
            return
        self.s.wait(eng, ename, self.base + val)


class SemPool:
    def __init__(self, nc, E):
        self.nc = nc
        self.E = E
        self.pool = []
        self.idx = 0

    def phase_reset(self):
        self.idx = 0

    def get(self, name=None):
        if self.idx == len(self.pool):
            self.pool.append(Sem(self.nc, self.E, f"pool{self.idx}"))
        s = self.pool[self.idx]
        self.idx += 1
        return PhaseSem(s)


class Chain:

    def __init__(self, sem, name):
        self.sem = sem
        self.name = name
        self.last = None
        self.pending = None

    def before(self, eng):
        if self.pending is not None:
            inst = self.pending
            li = LASTINC.get(id(inst))
            if li is None:
                self.sem.inc(inst, 1)
                li = (self.sem, self.sem.count)
            self.last = li
            self.pending = None
        if self.last is not None:
            sem, val = self.last
            sem.wait(eng, self.name, val)

    def after(self, inst):
        self.pending = inst

    def flush(self, eng):
        self.before(eng)


class SemGroup:

    def __init__(self, nc, E, name, n, per=16):
        self.sems = [Sem(nc, E, f"{name}{k}") for k in range(n)]
        self.n = n
        self.per = per

    def inc(self, inst, idx):
        self.sems[idx % self.n].inc(inst, self.per)

    def wait_done(self, eng, en, idx):
        if idx < 0:
            return
        self.sems[idx % self.n].wait(eng, en, self.per * (idx // self.n + 1))

    def wait_all(self, eng, en, total):
        for idx in range(max(0, total - self.n), total):
            self.wait_done(eng, en, idx)


COMPUTE_OPS = {"tensor_tensor", "tensor_scalar", "tensor_copy", "tensor_reduce",
               "memset", "reciprocal", "activation", "copy", "add", "mul", "iota",
               "reduce_sum", "reduce_max", "select", "bn_stats"}


class CE:

    def __init__(self, eng, chain):
        self._e = eng
        self._c = chain

    def __getattr__(self, a):
        f = getattr(self._e, a)
        if a in COMPUTE_OPS:
            def g(*args, **kw):
                self._c.before(self._e)
                inst = f(*args, **kw)
                KEEPALIVE.append(inst)
                self._c.after(inst)
                return inst
            return g
        return f



def prep_common(I):
    C = {}
    C["w0rT"] = np.ascontiguousarray(I["L0_ffn_W_r"].T)
    C["w0kT"] = np.ascontiguousarray(I["L0_ffn_W_k"].T)
    C["w0vT"] = np.ascontiguousarray(I["L0_ffn_W_v"].T)
    C["wq0"] = np.ascontiguousarray(I["L0_attn_Wq"])
    C["wk0"] = np.ascontiguousarray(I["L0_attn_Wk"])
    C["wv0T"] = np.ascontiguousarray(I["L0_attn_Wv"].T)
    C["mur"] = np.ascontiguousarray(np.asarray(I["L0_ffn_mu_r"]).reshape(NCH, 128).T)
    C["muk"] = np.ascontiguousarray(np.asarray(I["L0_ffn_mu_k"]).reshape(NCH, 128).T)
    C["w1qT"] = np.ascontiguousarray(I["L1_attn_Wq"].T)
    C["w1kT"] = np.ascontiguousarray(I["L1_attn_Wk"].T)
    C["w1vT"] = np.ascontiguousarray(I["L1_attn_Wv"].T)
    C["b1q"] = np.ascontiguousarray(np.asarray(I["L1_attn_bq"]).reshape(NCH, 128).T)
    C["b1k"] = np.ascontiguousarray(np.asarray(I["L1_attn_bk"]).reshape(NCH, 128).T)
    C["b1vrow"] = np.ascontiguousarray(np.asarray(I["L1_attn_bv"]).reshape(1, D))
    C["w1in"] = np.ascontiguousarray(I["L1_ffn_Win"])
    C["b1in"] = np.ascontiguousarray(np.asarray(I["L1_ffn_bin_"]).reshape(64, 1))
    W1 = np.asarray(I["L1_ffn_W1"])
    bd1 = np.zeros((64, 1024), np.float32)
    for m in range(64):
        bd1[m, 16 * m:16 * m + 16] = W1[m]
    C["bd1"] = bd1
    C["b1f"] = np.ascontiguousarray(
        np.asarray(I["L1_ffn_b1"]).reshape(64 * 16).reshape(NCH, 128).T)
    W2 = np.asarray(I["L1_ffn_W2"])
    bd2 = np.zeros((1024, 64), np.float32)
    for m in range(64):
        bd2[16 * m:16 * m + 16, m] = W2[m]
    C["bd2"] = bd2
    C["b1b2"] = np.ascontiguousarray(np.asarray(I["L1_ffn_b2"]).reshape(64, 1))
    C["w1out"] = np.ascontiguousarray(I["L1_ffn_Wout"])
    C["b1out"] = np.ascontiguousarray(np.asarray(I["L1_ffn_bout"]).reshape(NCH, 128).T)
    hselred = np.zeros((128, NCH * 16), np.float32)
    for c in range(NCH):
        for p in range(128):
            hselred[p, c * 16 + 2 * c + p // 64] = 1.0
    C["hselred"] = hselred
    hselbc = np.zeros((16, NCH * 128), np.float32)
    for c in range(NCH):
        for p in range(128):
            hselbc[2 * c + p // 64, c * 128 + p] = 1.0
    C["hselbc"] = hselbc
    tk = np.arange(128)[:, None]
    tq = np.arange(512)[None, :]
    masks = np.concatenate([(tq >= tk + 128 * j).astype(np.float32) for j in range(4)],
                           axis=1)
    C["masks"] = np.ascontiguousarray(masks)
    return {k: np.asarray(v, np.float32) for k, v in C.items()}


def prep_percore(I):
    x = np.asarray(I["L0_attn_x"])
    return [np.ascontiguousarray(x[c].T) for c in range(8)]


PHASE_SETS = {"A1": ("A1",), "A2": ("A2",), "B": ("B",), "FUSED": ("A1", "A2", "B")}

DRAM_SPECS = {
    "xT": ((D, T), F32, None, ("A1", "A2")),
    "w0rT": ((D, D), F32, None, ("A1",)),
    "w0kT": ((D, D), F32, None, ("A1",)),
    "w0vT": ((D, D), F32, None, ("A1",)),
    "wq0": ((64, 64), F32, None, ("A1",)),
    "wk0": ((64, 64), F32, None, ("A1",)),
    "wv0T": ((64, 64), F32, None, ("A1",)),
    "mur": ((128, NCH), F32, None, ("A1",)),
    "muk": ((128, NCH), F32, None, ("A1",)),
    "hselred": ((128, NCH * 16), F32, None, ("A1",)),
    "g0T": ((D, T), BF16, "A1", ("A2",)),
    "s0T": ((16, T), F32, "A1", ("A2",)),
    "vsumH": ((128, NCH), F32, "A1", ("A2",)),
    "vsumT": ((1, NCH * 128), F32, "A1", ("A2",)),
    "hselbc": ((16, NCH * 128), F32, None, ("A2",)),
    "w1qT": ((D, D), F32, None, ("A2",)),
    "w1kT": ((D, D), F32, None, ("A2",)),
    "w1vT": ((D, D), F32, None, ("A2",)),
    "b1q": ((128, NCH), F32, None, ("A2",)),
    "b1k": ((128, NCH), F32, None, ("A2",)),
    "b1vrow": ((1, D), F32, None, ("A2",)),
    "w1in": ((D, 64), F32, None, ("A2",)),
    "b1in": ((64, 1), F32, None, ("A2",)),
    "x1T": ((D, T), F32, "A2", ("B",)),
    "qT": ((D, T), BF16, "A2", ("B",)),
    "kT": ((D, T), BF16, "A2", ("B",)),
    "v1": ((T, D), BF16, "A2", ("B",)),
    "gffT": ((D, T), BF16, "A2", ("B",)),
    "masks": ((128, 4 * 512), F32, None, ("B",)),
    "bd1": ((64, 1024), F32, None, ("A2",)),
    "b1f": ((128, NCH), F32, None, ("A2",)),
    "bd2": ((1024, 64), F32, None, ("A2",)),
    "b1b2": ((64, 1), F32, None, ("A2",)),
    "w1out": ((64, 1024), F32, None, ("A2",)),
    "b1out": ((128, NCH), F32, None, ("A2",)),
    "yT": ((D, T), F32, "B", ()),
}


def chunked_load(eng, dst_sb, dst_off, src_dram, col0, ncol, nch=NCH):
    src = src_dram.ap()[:, col0:col0 + ncol].rearrange("(c p) n -> p c n", p=128)
    dst = dst_sb[:, dst_off:dst_off + nch * ncol].rearrange("p (c n) -> p c n", c=nch)
    return eng.dma_start(out=dst, in_=src)


def chunked_store(eng, dst_dram, col0, ncol, src_sb, src_off, nch=NCH):
    dst = dst_dram.ap()[:, col0:col0 + ncol].rearrange("(c p) n -> p c n", p=128)
    src = src_sb[:, src_off:src_off + nch * ncol].rearrange("p (c n) -> p c n", c=nch)
    return eng.dma_start(out=dst, in_=src)


def make_program(mode):
    phases = PHASE_SETS[mode]
    nc = bass.Bass("TRN2", target_bir_lowering=False)
    dram = {}
    input_names = []
    for name, (shape, dt, prod, cons) in DRAM_SPECS.items():
        used_c = any(p in phases for p in cons)
        prod_here = prod in phases
        if prod is None:
            if used_c:
                dram[name] = nc.dram_tensor(name, list(shape), dt, kind="ExternalInput")
                input_names.append(name)
        elif name == "yT":
            if prod_here:
                dram[name] = nc.dram_tensor(name, list(shape), dt, kind="ExternalOutput")
        elif prod_here and used_c:
            dram[name] = nc.dram_tensor(name, list(shape), dt)
        elif prod_here:
            dram[name] = nc.dram_tensor(name, list(shape), dt, kind="ExternalOutput")
        elif used_c:
            dram[name] = nc.dram_tensor(name, list(shape), dt, kind="ExternalInput")
            input_names.append(name)
    if "A2" in phases:
        dram["cc_in"] = nc.dram_tensor("cc_in", [16, T], F32)
        dram["cc_out"] = nc.dram_tensor("cc_out", [16, T], F32)
        dram["probsH"] = nc.dram_tensor("probsH", [16, T], BF16)

    with ExitStack() as octx:
        E = octx.enter_context
        bar = Sem(nc, E, "bar")
        pool = SemPool(nc, E)
        state = {"gen": 0}

        def barrier_end(chains=None):
            with nc.Block() as blk:
                for nm, engf in (("sync", blk.sync), ("tensor", blk.tensor),
                                 ("scalar", blk.scalar), ("vector", blk.vector),
                                 ("gpsimd", blk.gpsimd)):
                    @engf
                    def _(eng, nm=nm):
                        if chains and nm in chains and chains[nm].last is not None:
                            sem, val = chains[nm].last
                            sem.wait(eng, nm, val)
                        bar.inc(eng.nop(), 1)
            state["gen"] += 5

        def bwait(eng, ename):
            bar.wait(eng, ename, state["gen"])

        for ph in phases:
            pool.phase_reset()
            if ph == "A1":
                chains = build_A1(nc, E, dram, bwait, pool)
            elif ph == "A2":
                chains = build_A2(nc, E, dram, bwait, barrier_end, pool)
            else:
                chains = build_B(nc, E, dram, bwait, pool)
            barrier_end(chains)
    return nc, input_names



def build_A1(nc, E, dram, bwait, pool):
    xT, g0T, s0T, vsumH = dram["xT"], dram["g0T"], dram["s0T"], dram["vsumH"]

    ctx = ExitStack()
    A = ctx.enter_context
    with ctx:
        xbuf = A(nc.sbuf_tensor("a1_xbuf", [128, 2 * NCH * FT], F32))
        xbf = A(nc.sbuf_tensor("a1_xbf", [128, NCH * FT], BF16))
        sqbf = A(nc.sbuf_tensor("a1_sqbf", [128, NCH * FT], BF16))
        xn = A(nc.sbuf_tensor("a1_xn", [128, 2 * NCH * 513], BF16))
        xr = A(nc.sbuf_tensor("a1_xr", [128, 2 * NCH * FT], BF16))
        xk = A(nc.sbuf_tensor("a1_xk", [128, 2 * NCH * FT], BF16))
        dscr = A(nc.sbuf_tensor("a1_d", [128, FT], BF16))
        escr = A(nc.sbuf_tensor("a1_e", [128, FT], BF16))
        tmpscr = A(nc.sbuf_tensor("a1_tmp", [128, FT], F32))
        r_sb = A(nc.sbuf_tensor("a1_r", [128, NCH * FT], BF16))
        relu_sb = A(nc.sbuf_tensor("a1_relu", [128, 2 * FT], BF16))
        kact = A(nc.sbuf_tensor("a1_kact", [128, NCH * FT], BF16))
        kvprod = A(nc.sbuf_tensor("a1_kvp", [128, 2 * FT], BF16))
        g0sb = A(nc.sbuf_tensor("a1_g0", [128, 2 * NCH * FT], BF16))
        qksb = A(nc.sbuf_tensor("a1_qk", [128, 2 * FT], BF16))
        vp = A(nc.sbuf_tensor("a1_vp", [128, NCH], F32))
        vsum = A(nc.sbuf_tensor("a1_vsum", [128, NCH], F32))
        s0row = A(nc.sbuf_tensor("a1_s0row", [16, 2 * FT], F32))
        mrow = A(nc.sbuf_tensor("a1_mrow", [1, FT], BF16))
        exrow = A(nc.sbuf_tensor("a1_exrow", [1, FT], F32))
        vrow = A(nc.sbuf_tensor("a1_vrow", [1, FT], F32))
        rrow = A(nc.sbuf_tensor("a1_rrow", [1, FT], BF16))
        srow = A(nc.sbuf_tensor("a1_srow", [1, FT], F32))
        bmrow = A(nc.sbuf_tensor("a1_bmrow", [128, 2 * FT], BF16))
        brrow = A(nc.sbuf_tensor("a1_brrow", [128, 2 * FT], BF16))
        ones128 = A(nc.sbuf_tensor("a1_ones128", [128, 1], BF16))
        onesrow = A(nc.sbuf_tensor("a1_onesrow", [1, 128], BF16))
        w0r = A(nc.sbuf_tensor("a1_w0r", [128, NCH * D], BF16))
        w0k = A(nc.sbuf_tensor("a1_w0k", [128, NCH * D], BF16))
        w0v = A(nc.sbuf_tensor("a1_w0v", [128, NCH * D], BF16))
        wq0b = A(nc.sbuf_tensor("a1_wq0b", [64, 64], BF16))
        wk0b = A(nc.sbuf_tensor("a1_wk0b", [64, 64], BF16))
        wv0b = A(nc.sbuf_tensor("a1_wv0b", [64, 64], BF16))
        mscr = A(nc.sbuf_tensor("a1_mscr", [64, 64], BF16))
        Mbd = A(nc.sbuf_tensor("a1_Mbd", [128, 128], BF16))
        Vbd = A(nc.sbuf_tensor("a1_Vbd", [128, 128], BF16))
        murs = A(nc.sbuf_tensor("a1_murs", [128, NCH], F32))
        muks = A(nc.sbuf_tensor("a1_muks", [128, NCH], F32))
        hred = A(nc.sbuf_tensor("a1_hred", [128, NCH * 16], BF16))

        psLN1 = A(nc.psum_tensor("a1_psLN1", [128, 512], F32))
        psLN2 = A(nc.psum_tensor("a1_psLN2", [128, 512], F32))
        psF = [A(nc.psum_tensor(f"a1_psF{k}", [128, 512], F32)) for k in range(2)]
        psQ = [A(nc.psum_tensor(f"a1_psQ{k}", [128, 512], F32)) for k in range(2)]
        psS0 = A(nc.psum_tensor("a1_psS0", [16, 512], F32))
        psM = A(nc.psum_tensor("a1_psM", [64, 64], F32))

        s_ld = SemGroup(nc, E, "a1_ld", 2)
        s_wld = Sem(nc, E, "a1_wld")
        s_init = pool.get()
        s_mmM = pool.get()
        s_evM = pool.get()
        s_cast = pool.get()
        s_mmstat = pool.get()
        s_rowd = pool.get()
        s_sqr = pool.get()
        s_rowg = pool.get()
        s_bc = pool.get()
        s_bcevS = pool.get()
        s_xnLN = pool.get()
        s_mix = pool.get()
        s_mmR = pool.get()
        s_sig = pool.get()
        s_mmK = pool.get()
        s_rel = pool.get()
        s_kact = pool.get()
        s_mmV = pool.get()
        s_kvev = pool.get()
        s_silu = pool.get()
        s_rk = pool.get()
        s_mmU = pool.get()
        s_qk = pool.get()
        s_mmV0 = pool.get()
        s_mms0 = pool.get()
        s_vev = pool.get()
        s_vsadd = pool.get()
        s_xnfree = pool.get()
        s_s0ev = pool.get()
        s_g0st = SemGroup(nc, E, "a1_g0st", 2)
        s_s0st = SemGroup(nc, E, "a1_s0st", 2)
        s_fin = Sem(nc, E, "a1_fin")
        chains = {"vector": Chain(pool.get(), "a1_chv"), "scalar": Chain(pool.get(), "a1_chs"),
                  "gpsimd": Chain(pool.get(), "a1_chg")}

        def xb(i, c):
            o = ((i % 2) * NCH + c) * FT
            return xbuf[:, o:o + FT]

        def xns(i, c, a, b):
            o = ((i % 2) * NCH + c) * 513
            return xn[:, o + a:o + b]

        def xrs(i, c):
            o = ((i % 2) * NCH + c) * FT
            return xr[:, o:o + FT]

        def xks(i, c):
            o = ((i % 2) * NCH + c) * FT
            return xk[:, o:o + FT]

        def wsl(w, c, o):
            return w[:, c * D + o * 128:c * D + o * 128 + 128]

        def row(buf, i):
            if buf.shape[1] == FT:
                return buf[:, 0:FT]
            o = (i % 2) * FT
            return buf[:, o:o + FT]

        with nc.Block() as blk:
            @blk.gpsimd
            def _(eng):
                en = "gpsimd"
                eng = CE(eng, chains[en])
                bwait(eng, en)
                for nm, dst in (("wq0", wq0b), ("wk0", wk0b), ("wv0T", wv0b)):
                    s_wld.inc(eng.dma_start(out=dst[:, :], in_=dram[nm].ap()), 16)
                for nm, dst in (("mur", murs), ("muk", muks)):
                    s_wld.inc(eng.dma_start(out=dst[:, :], in_=dram[nm].ap()), 16)
                s_wld.inc(eng.dma_start(out=hred[:, :], in_=dram["hselred"].ap()), 16)
                for nm, dst in (("w0rT", w0r), ("w0kT", w0k), ("w0vT", w0v)):
                    src = dram[nm].ap().rearrange("(c p) n -> p c n", p=128)
                    d2 = dst[:, :].rearrange("p (c n) -> p c n", c=NCH)
                    s_wld.inc(eng.dma_start(out=d2, in_=src), 16)
                chains[en].flush(eng)

            @blk.vector
            def _(eng):
                en = "vector"
                eng = CE(eng, chains[en])
                bwait(eng, en)
                s_init.inc(eng.memset(Mbd[:, :], 0.0))
                s_init.inc(eng.memset(Vbd[:, :], 0.0))
                s_init.inc(eng.memset(ones128[:, :], 1.0))
                s_init.inc(eng.memset(onesrow[:, :], 1.0))
                s_init.inc(eng.memset(vp[:, :], 0.0))
                s_init.inc(eng.memset(vsum[:, :], 0.0))
                for c in range(NCH):
                    s_init.inc(eng.memset(xn[:, c * 513:c * 513 + 1], 0.0))
                s_evM.wait(eng, en, 1)
                s_init.inc(eng.tensor_copy(Mbd[0:64, 0:64], mscr[:, :]))
                s_init.inc(eng.tensor_copy(Mbd[64:128, 64:128], mscr[:, :]))
                s_wld.wait(eng, en, 144)
                s_init.inc(eng.tensor_copy(Vbd[0:64, 0:64], wv0b[:, :]))
                s_init.inc(eng.tensor_copy(Vbd[64:128, 64:128], wv0b[:, :]))

                def v_rows(i):
                    s_mmstat.wait(eng, en, 8 * (i + 1))
                    s_bc.wait(eng, en, i)
                    eng.tensor_scalar(row(mrow, i), psLN1[0:1, :], 1.0 / D, None, ALU.mult)
                    eng.tensor_scalar(row(exrow, i), psLN2[0:1, :], 1.0 / D, None, ALU.mult)
                    eng.tensor_tensor(row(vrow, i), row(mrow, i), row(mrow, i), ALU.mult)
                    eng.tensor_tensor(row(vrow, i), row(exrow, i), row(vrow, i), ALU.subtract)
                    inst = eng.tensor_scalar(row(vrow, i), row(vrow, i), EPS, None, ALU.add)
                    s_rowd.inc(inst)
                    s_sqr.wait(eng, en, i + 1)
                    eng.reciprocal(row(vrow, i), srow[:, :])
                    inst = eng.tensor_copy(row(rrow, i), row(vrow, i))
                    s_rowg.inc(inst)

                def v_xn1c(i, c):
                    if c == 0:
                        s_bcevS.wait(eng, en, i + 1)
                        s_xnfree.wait(eng, en, i - 1)
                    sl = (i % 2) * FT
                    eng.tensor_tensor(tmpscr[:, :], xb(i, c), bmrow[:, sl:sl + FT], ALU.subtract)
                    inst = eng.tensor_tensor(xns(i, c, 1, 513), tmpscr[:, :],
                                             brrow[:, sl:sl + FT], ALU.mult)
                    if c == NCH - 1:
                        s_xnLN.inc(inst)
                    if i + 1 < NT:
                        eng.tensor_copy(xns(i + 1, c, 0, 1), xns(i, c, 512, 513))

                def v_xn(i):
                    for c in range(NCH):
                        v_xn1c(i, c)

                def v_mix1c(i, c):
                    if c == 0:
                        s_rk.wait(eng, en, i - 1)
                    eng.tensor_tensor(dscr[:, :], xns(i, c, 1, 513), xns(i, c, 0, 512), ALU.subtract)
                    eng.tensor_scalar(escr[:, :], dscr[:, :], murs[:, c:c + 1], None, ALU.mult)
                    eng.tensor_tensor(xrs(i, c), xns(i, c, 0, 512), escr[:, :], ALU.add)
                    eng.tensor_scalar(escr[:, :], dscr[:, :], muks[:, c:c + 1], None, ALU.mult)
                    inst = eng.tensor_tensor(xks(i, c), xns(i, c, 0, 512), escr[:, :], ALU.add)
                    if c == NCH - 1:
                        s_mix.inc(inst)

                def v_mix(i):
                    for c in range(NCH):
                        v_mix1c(i, c)

                def v_vsum(i):
                    s_vev.wait(eng, en, 8 * (i + 1))
                    inst = eng.tensor_tensor(vsum[:, :], vsum[:, :], vp[:, :], ALU.add)
                    s_vsadd.inc(inst)

                v_rows(0)
                v_xn(0)
                v_mix(0)
                for i in range(NT):
                    if i + 1 < NT:
                        v_rows(i + 1)
                    if i >= 1:
                        v_vsum(i - 1)
                    for o in range(NCH):
                        s_rel.wait(eng, en, 8 * i + o + 1)
                        inst = eng.tensor_tensor(kact[:, o * FT:(o + 1) * FT], psF[o % 2][:, :],
                                                 relu_sb[:, (o % 2) * FT:(o % 2 + 1) * FT], ALU.mult)
                        s_kact.inc(inst)
                        if i + 1 < NT and o >= 1:
                            v_xn1c(i + 1, o - 1)
                    if i + 1 < NT:
                        v_xn1c(i + 1, NCH - 1)
                    for o in range(NCH):
                        s_mmV.wait(eng, en, 8 * i + o + 1)
                        s_silu.wait(eng, en, 8 * i + o - 1)
                        inst = eng.tensor_tensor(kvprod[:, (o % 2) * FT:(o % 2 + 1) * FT],
                                                 psF[o % 2][:, :], r_sb[:, o * FT:(o + 1) * FT], ALU.mult)
                        s_kvev.inc(inst)
                        s_mmU.wait(eng, en, 8 * i + o + 1)
                        s_mms0.wait(eng, en, 8 * i + o - 1)
                        inst = eng.tensor_tensor(qksb[:, (o % 2) * FT:(o % 2 + 1) * FT],
                                                 xns(i, o, 1, 513), psQ[o % 2][:, :], ALU.mult)
                        s_qk.inc(inst)
                        if i + 1 < NT:
                            v_mix1c(i + 1, o)
                v_vsum(NT - 1)
                chains[en].flush(eng)

            @blk.scalar
            def _(eng):
                en = "scalar"
                eng = CE(eng, chains[en])
                bwait(eng, en)
                s_mmM.wait(eng, en, 1)
                s_evM.inc(eng.copy(mscr[:, :], psM[:, :]))

                def s_cast1(j, c):
                    if c == 0:
                        s_ld.wait_done(eng, en, j)
                    s_mmstat.wait(eng, en, 8 * j + c - 7)
                    sl = c * FT
                    s_cast.inc(eng.copy(xbf[:, sl:sl + FT], xb(j, c)))
                    s_cast.inc(eng.activation(out=sqbf[:, sl:sl + FT], in_=xb(j, c),
                                              func=AF.Square))

                def s_casts(j):
                    for c in range(NCH):
                        s_cast1(j, c)

                def s_sqrt(j):
                    s_rowd.wait(eng, en, j + 1)
                    s_sqr.inc(eng.activation(out=srow[:, :], in_=row(vrow, j), func=AF.Sqrt))

                def s_bcev(j):
                    s_bc.wait(eng, en, j + 1)
                    sl = (j % 2) * FT
                    eng.copy(bmrow[:, sl:sl + FT], psLN1[:, :])
                    inst = eng.copy(brrow[:, sl:sl + FT], psLN2[:, :])
                    s_bcevS.inc(inst)

                s_casts(0)
                s_sqrt(0)
                s_bcev(0)
                if NT > 1:
                    s_casts(1)

                def s_vev1(k, c):
                    s_mmV0.wait(eng, en, 8 * k + c + 1)
                    s_vsadd.wait(eng, en, k)
                    inst = eng.activation(out=relu_sb[:, 0:FT], in_=psQ[c % 2][:, :],
                                          func=AF.Copy, accum_out=vp[:, c:c + 1])
                    s_vev.inc(inst)

                def s_s0(k):
                    s_mms0.wait(eng, en, 8 * (k + 1))
                    s_s0st.wait_done(eng, en, k - 2)
                    inst = eng.copy(row(s0row, k), psS0[:, :])
                    s_s0ev.inc(inst)

                for i in range(NT):
                    for o in range(NCH):
                        s_mmR.wait(eng, en, 8 * i + o + 1)
                        s_kvev.wait(eng, en, 8 * (i - 1) + o + 1)
                        inst = eng.activation(out=r_sb[:, o * FT:(o + 1) * FT], in_=psF[o % 2][:, :],
                                              func=AF.Sigmoid)
                        s_sig.inc(inst)
                        if i >= 1:
                            s_vev1(i - 1, o)
                        if o == 2 and i + 1 < NT:
                            s_sqrt(i + 1)
                    if i >= 1:
                        s_s0(i - 1)
                    if i + 1 < NT:
                        s_bcev(i + 1)
                    for o in range(NCH):
                        s_mmK.wait(eng, en, 8 * i + o + 1)
                        s_kact.wait(eng, en, 8 * i + o - 1)
                        inst = eng.activation(out=relu_sb[:, (o % 2) * FT:(o % 2 + 1) * FT],
                                              in_=psF[o % 2][:, :], func=AF.Relu)
                        s_rel.inc(inst)
                    for o in range(NCH):
                        s_kvev.wait(eng, en, 8 * i + o + 1)
                        s_g0st.wait_done(eng, en, i - 2)
                        dst = g0sb[:, ((i % 2) * NCH + o) * FT:((i % 2) * NCH + o + 1) * FT]
                        inst = eng.activation(out=dst, in_=kvprod[:, (o % 2) * FT:(o % 2 + 1) * FT],
                                              func=AF.Silu)
                        s_silu.inc(inst)
                        if i + 2 < NT:
                            s_cast1(i + 2, o)
                for c in range(NCH):
                    s_vev1(NT - 1, c)
                s_s0(NT - 1)
                chains[en].flush(eng)

            @blk.tensor
            def _(eng):
                en = "tensor"
                bwait(eng, en)
                s_wld.wait(eng, en, 144)
                inst = eng.matmul(psM[:, :], wk0b[:, :], wq0b[:, :], start=True, stop=True)
                s_mmM.inc(inst)
                s_init.wait(eng, en, 18)

                def t_stats(j):
                    s_bcevS.wait(eng, en, j)
                    for c in range(NCH):
                        s_cast.wait(eng, en, 16 * j + 2 * (c + 1))
                        sl = c * FT
                        eng.matmul(psLN1[0:1, :], ones128[:, :], xbf[:, sl:sl + FT],
                                   start=(c == 0), stop=(c == NCH - 1))
                        inst = eng.matmul(psLN2[0:1, :], ones128[:, :], sqbf[:, sl:sl + FT],
                                          start=(c == 0), stop=(c == NCH - 1))
                        s_mmstat.inc(inst)

                def t_bcast(j):
                    s_rowg.wait(eng, en, j + 1)
                    eng.matmul(psLN1[:, :], onesrow[:, :], row(mrow, j), start=True, stop=True)
                    inst = eng.matmul(psLN2[:, :], onesrow[:, :], row(rrow, j), start=True, stop=True)
                    s_bc.inc(inst)

                t_stats(0)
                t_bcast(0)

                def emit_v(k, c):
                    if c < 2:
                        s_qk.wait(eng, en, 8 * k + c + 7)
                    else:
                        s_vev.wait(eng, en, 8 * k + c - 1)
                    inst = eng.matmul(psQ[c % 2][:, :], Vbd[:, :], xns(k, c, 1, 513),
                                      start=True, stop=True)
                    s_mmV0.inc(inst)
                    if c == NCH - 1:
                        s_xnfree.inc(eng.nop())

                for i in range(NT):
                    if i + 1 < NT:
                        t_stats(i + 1)
                    s_mix.wait(eng, en, i + 1)
                    for o in range(NCH):
                        if o < 2:
                            s_kvev.wait(eng, en, 8 * (i - 1) + o + 7)
                        else:
                            s_sig.wait(eng, en, 8 * i + o - 1)
                        for c in range(NCH):
                            inst = eng.matmul(psF[o % 2][:, :], wsl(w0r, c, o), xrs(i, c),
                                              start=(c == 0), stop=(c == NCH - 1))
                        s_mmR.inc(inst)
                        if i >= 1:
                            emit_v(i - 1, o)
                    if i + 1 < NT:
                        t_bcast(i + 1)
                    for o in range(NCH):
                        if o == 0:
                            s_sig.wait(eng, en, 8 * i + 7)
                        elif o == 1:
                            s_sig.wait(eng, en, 8 * i + 8)
                        else:
                            s_kact.wait(eng, en, 8 * i + o - 1)
                        for c in range(NCH):
                            inst = eng.matmul(psF[o % 2][:, :], wsl(w0k, c, o), xks(i, c),
                                              start=(c == 0), stop=(c == NCH - 1))
                        s_mmK.inc(inst)
                        if o == NCH - 1:
                            s_rk.inc(eng.nop())
                    def emit_s0red(c):
                        s_qk.wait(eng, en, 8 * i + c + 1)
                        if c == 0:
                            s_s0ev.wait(eng, en, i)
                        inst = eng.matmul(psS0[:, :], hred[:, c * 16:(c + 1) * 16],
                                          qksb[:, (c % 2) * FT:(c % 2 + 1) * FT],
                                          start=(c == 0), stop=(c == NCH - 1))
                        s_mms0.inc(inst)

                    s_kact.wait(eng, en, 8 * (i + 1))
                    for o in range(NCH):
                        if o >= 2:
                            s_kvev.wait(eng, en, 8 * i + o - 1)
                        for c in range(NCH):
                            inst = eng.matmul(psF[o % 2][:, :], wsl(w0v, c, o),
                                              kact[:, c * FT:(c + 1) * FT],
                                              start=(c == 0), stop=(c == NCH - 1))
                        s_mmV.inc(inst)
                        if o < 2:
                            s_vev.wait(eng, en, 8 * i + o - 1)
                        else:
                            s_qk.wait(eng, en, 8 * i + o - 1)
                        inst = eng.matmul(psQ[o % 2][:, :], Mbd[:, :], xns(i, o, 1, 513),
                                          start=True, stop=True)
                        s_mmU.inc(inst)
                        if o >= 2:
                            emit_s0red(o - 2)
                    emit_s0red(NCH - 2)
                    emit_s0red(NCH - 1)
                for c in range(NCH):
                    emit_v(NT - 1, c)

            @blk.sync
            def _(eng):
                en = "sync"
                bwait(eng, en)
                for i in range(min(2, NT)):
                    s_ld.inc(chunked_load(eng, xbuf, (i % 2) * NCH * FT, xT, i * FT, FT), i)
                for i in range(NT):
                    if i + 2 < NT:
                        s_xnLN.wait(eng, en, i + 1)
                        s_ld.inc(chunked_load(eng, xbuf, (i % 2) * NCH * FT, xT, (i + 2) * FT, FT), i + 2)
                    s_silu.wait(eng, en, 8 * (i + 1))
                    s_g0st.inc(chunked_store(eng, g0T, i * FT, FT, g0sb, (i % 2) * NCH * FT), i)
                    s_s0ev.wait(eng, en, i + 1)
                    s_s0st.inc(eng.dma_start(out=s0T.ap()[:, i * FT:(i + 1) * FT],
                                             in_=row(s0row, i)), i)
                s_vsadd.wait(eng, en, NT)
                s_fin.inc(eng.dma_start(out=vsumH.ap(), in_=vsum[:, :]), 16)
                s_fin.inc(eng.dma_start(
                    out=dram["vsumT"].ap().rearrange("o (c p) -> (o p) c", p=128),
                    in_=vsum[:, :], allow_slow_non_contiguous=True), 16)
                s_fin.wait(eng, en, 32)
                s_g0st.wait_all(eng, en, NT)
                s_s0st.wait_all(eng, en, NT)

        assert s_mix.count == NT and s_mmstat.count == 8 * NT
        assert s_sig.count == 8 * NT and s_kact.count == 8 * NT
        return chains



def build_A2(nc, E, dram, bwait, barrier_end, pool):
    xT, g0T, s0T, vsumH = dram["xT"], dram["g0T"], dram["s0T"], dram["vsumH"]
    x1T, qT, kT, v1, gffT = dram["x1T"], dram["qT"], dram["kT"], dram["v1"], dram["gffT"]
    cc_in, cc_out = dram["cc_in"], dram["cc_out"]

    pctx = ExitStack()
    P = pctx.enter_context
    with pctx:
        vsumt = P(nc.sbuf_tensor("a2_vsumt", [1, NCH * 128], BF16))
        s_prob = pool.get()
        chains = {"vector": Chain(pool.get(), "a2_chv"), "scalar": Chain(pool.get(), "a2_chs"),
                  "gpsimd": Chain(pool.get(), "a2_chg")}

        cctx = ExitStack()
        C = cctx.enter_context
        with cctx:
            bufP = C(nc.sbuf_tensor("a2_bufP", [16, T], F32))
            bufQ = C(nc.sbuf_tensor("a2_bufQ", [16, T], F32))
            bufR = C(nc.sbuf_tensor("a2_bufR", [16, T], F32))
            probsb = C(nc.sbuf_tensor("a2_probsb", [16, T], BF16))
            s_cl = Sem(nc, E, "a2_cl")
            s_ex = pool.get()
            s_cc = Sem(nc, E, "a2_cc")
            with nc.Block() as blk:
                @blk.gpsimd
                def _(eng):
                    en = "gpsimd"
                    eng = CE(eng, chains[en])
                    bwait(eng, en)
                    s_cl.inc(eng.dma_start(out=vsumt[:, :], in_=dram["vsumT"].ap()), 16)
                    s_cl.inc(eng.dma_start(out=bufP[:, :], in_=s0T.ap()), 16)
                    s_ex.wait(eng, en, 1)
                    s_cl.inc(eng.dma_start(out=cc_in.ap(), in_=bufQ[:, :]), 16)
                    s_cl.wait(eng, en, 48)
                    inst = eng.collective_compute(
                        "AllReduce", ALU.add, replica_groups=[list(range(8))],
                        ins=[cc_in.ap().opt()], outs=[cc_out.ap().opt()])
                    s_cc.inc(inst, 1)
                    s_cc.wait(eng, en, 1)
                    s_cl.inc(eng.dma_start(out=bufR[:, :], in_=cc_out.ap()), 16)
                    s_prob.wait(eng, en, 1)
                    s_cl.inc(eng.dma_start(out=dram["probsH"].ap(), in_=probsb[:, :]), 16)
                    s_cl.wait(eng, en, 80)
                    chains[en].flush(eng)

                @blk.scalar
                def _(eng):
                    en = "scalar"
                    eng = CE(eng, chains[en])
                    bwait(eng, en)
                    s_cl.wait(eng, en, 32)
                    inst = eng.activation(out=bufQ[:, :], in_=bufP[:, :], func=AF.Exp, scale=ISQ)
                    s_ex.inc(inst)
                    chains[en].flush(eng)

                @blk.vector
                def _(eng):
                    en = "vector"
                    eng = CE(eng, chains[en])
                    bwait(eng, en)
                    s_cl.wait(eng, en, 64)
                    eng.reciprocal(bufP[:, :], bufR[:, :])
                    inst = eng.tensor_tensor(probsb[:, :], bufQ[:, :], bufP[:, :], ALU.mult)
                    s_prob.inc(inst)
                    chains[en].flush(eng)
            barrier_end(chains)

        ctx = ExitStack()
        A = ctx.enter_context
        with ctx:
            xbuf = A(nc.sbuf_tensor("a2_xbuf", [128, 2 * NCH * FT], F32))
            g0buf = A(nc.sbuf_tensor("a2_g0buf", [128, NCH * FT], BF16))
            x1buf = A(nc.sbuf_tensor("a2_x1buf", [128, 2 * NCH * FT], F32))
            x1bf = A(nc.sbuf_tensor("a2_x1bf", [128, 5 * FT], BF16))
            sqbf = A(nc.sbuf_tensor("a2_sqbf", [128, 5 * FT], BF16))
            xn1 = A(nc.sbuf_tensor("a2_xn1", [128, 2 * NCH * FT], BF16))
            vselw = A(nc.sbuf_tensor("a2_vselw", [16, NCH * 128], BF16))
            pbuf = A(nc.sbuf_tensor("a2_pbuf", [16, 2 * FT], BF16))
            tmpscr = A(nc.sbuf_tensor("a2_tmp", [128, FT], F32))
            t1scr = tmpscr
            qstage = A(nc.sbuf_tensor("a2_qstage", [128, 4 * FT], BF16))
            kstage = A(nc.sbuf_tensor("a2_kstage", [128, 4 * FT], BF16))
            vstage = A(nc.sbuf_tensor("a2_vstage", [128, 4 * D], BF16))
            s1stage = A(nc.sbuf_tensor("a2_s1stage", [64, 2 * FT], BF16))
            bd1s = A(nc.sbuf_tensor("a2_bd1s", [64, 1024], BF16))
            bd2s = A(nc.sbuf_tensor("a2_bd2s", [128, NCH * 64], BF16))
            w1os = A(nc.sbuf_tensor("a2_w1os", [64, 1024], BF16))
            b1fs = A(nc.sbuf_tensor("a2_b1fs", [128, NCH], F32))
            b1b2s = A(nc.sbuf_tensor("a2_b1b2s", [64, 1], F32))
            b1outs = A(nc.sbuf_tensor("a2_b1outs", [128, NCH], F32))
            hstage = A(nc.sbuf_tensor("a2_hstage", [128, 2 * FT], BF16))
            g1sb = A(nc.sbuf_tensor("a2_g1sb", [64, FT], BF16))
            sgscr = A(nc.sbuf_tensor("a2_sgscr", [128, 2 * FT], BF16))
            fscr = A(nc.sbuf_tensor("a2_fscr", [128, FT], F32))
            gstage = A(nc.sbuf_tensor("a2_gstage", [128, NCH * FT], BF16))
            mrow = A(nc.sbuf_tensor("a2_mrow", [1, FT], BF16))
            exrow = A(nc.sbuf_tensor("a2_exrow", [1, FT], F32))
            vrow = A(nc.sbuf_tensor("a2_vrow", [1, FT], F32))
            rrow = A(nc.sbuf_tensor("a2_rrow", [1, FT], BF16))
            srow = A(nc.sbuf_tensor("a2_srow", [1, FT], BF16))
            ones128 = A(nc.sbuf_tensor("a2_ones128", [128, 1], BF16))
            onesrow = A(nc.sbuf_tensor("a2_onesrow", [1, 128], F32))
            hbc = A(nc.sbuf_tensor("a2_hbc", [16, NCH * 128], BF16))
            w1q = A(nc.sbuf_tensor("a2_w1q", [128, NCH * D], BF16))
            w1k = A(nc.sbuf_tensor("a2_w1k", [128, NCH * D], BF16))
            w1v = A(nc.sbuf_tensor("a2_w1v", [128, NCH * D], BF16))
            w1i = A(nc.sbuf_tensor("a2_w1i", [128, NCH * 64], BF16))
            b1qs = A(nc.sbuf_tensor("a2_b1qs", [128, NCH], F32))
            b1ks = A(nc.sbuf_tensor("a2_b1ks", [128, NCH], F32))
            b1ins = A(nc.sbuf_tensor("a2_b1ins", [64, 1], F32))
            bvrow = A(nc.sbuf_tensor("a2_bvrow", [1, D], BF16))
            onesrowb = A(nc.sbuf_tensor("a2_onesrowb", [1, 128], BF16))
            bvbc = A(nc.sbuf_tensor("a2_bvbc", [128, D], BF16))

            psP = [A(nc.psum_tensor(f"a2_psP{k}", [128, 512], F32)) for k in range(2)]
            psLN1 = A(nc.psum_tensor("a2_psLN1", [128, 512], F32))
            psLN2 = A(nc.psum_tensor("a2_psLN2", [128, 512], F32))
            psF = [A(nc.psum_tensor(f"a2_psF{k}", [128, 512], F32)) for k in range(2)]
            psV = A(nc.psum_tensor("a2_psV", [128, 512], F32))
            psW1 = A(nc.psum_tensor("a2_psW1", [64, 512], F32))

            s_ld = SemGroup(nc, E, "a2_ld", 2, per=16)
            s_ldg = SemGroup(nc, E, "a2_ldg", 2, per=16)
            s_ldp = SemGroup(nc, E, "a2_ldp", 2, per=16)
            s_wld = Sem(nc, E, "a2_wld")
            s_init = pool.get()
            s_mmbv = pool.get()
            s_evbv = pool.get()
            s_mmP = pool.get()
            s_sa = pool.get()
            s_x1 = pool.get()
            s_cast = pool.get()
            s_mmstat = pool.get()
            s_rowd = pool.get()
            s_sqr = pool.get()
            s_rowg = pool.get()
            s_bc = pool.get()
            s_xn1 = pool.get()
            s_mmQ = pool.get()
            s_qev = pool.get()
            s_mmK = pool.get()
            s_kev = pool.get()
            s_mmV = pool.get()
            s_vev = pool.get()
            s_mmW1 = pool.get()
            s_s1ev = pool.get()
            s_mmvw = pool.get()
            s_vw = pool.get()
            s_x1st = SemGroup(nc, E, "a2_x1st", 2)
            s_qst = SemGroup(nc, E, "a2_qst", 2)
            s_kst = SemGroup(nc, E, "a2_kst", 2)
            s_vst = SemGroup(nc, E, "a2_vst", 2)
            s_gst = SemGroup(nc, E, "a2_gst", 2)
            s_mmE = pool.get()
            s_h = pool.get()
            s_mmG = pool.get()
            s_g1 = pool.get()
            s_mmW = pool.get()
            s_sg = pool.get()
            s_ep = pool.get()

            def xb(i, c):
                o = ((i % 2) * NCH + c) * FT
                return xbuf[:, o:o + FT]

            def g0s(i, c):
                return g0buf[:, c * FT:(c + 1) * FT]

            def x1s(i, c):
                o = ((i % 2) * NCH + c) * FT
                return x1buf[:, o:o + FT]

            def xn1s(i, c):
                o = ((i % 2) * NCH + c) * FT
                return xn1[:, o:o + FT]

            def wsl(w, c, o):
                return w[:, c * D + o * 128:c * D + o * 128 + 128]

            def row(buf, i):
                if buf.shape[1] == FT:
                    return buf[:, 0:FT]
                o = (i % 2) * FT
                return buf[:, o:o + FT]

            with nc.Block() as blk:
                @blk.gpsimd
                def _(eng):
                    en = "gpsimd"
                    eng = CE(eng, chains[en])
                    bwait(eng, en)
                    s_wld.inc(eng.dma_start(out=hbc[:, :], in_=dram["hselbc"].ap()), 16)
                    for nm, dst in (("w1qT", w1q), ("w1kT", w1k), ("w1vT", w1v)):
                        src = dram[nm].ap().rearrange("(c p) n -> p c n", p=128)
                        d2 = dst[:, :].rearrange("p (c n) -> p c n", c=NCH)
                        s_wld.inc(eng.dma_start(out=d2, in_=src), 16)
                    src = dram["w1in"].ap().rearrange("(c p) n -> p c n", p=128)
                    d2 = w1i[:, :].rearrange("p (c n) -> p c n", c=NCH)
                    s_wld.inc(eng.dma_start(out=d2, in_=src), 16)
                    s_wld.inc(eng.dma_start(out=b1qs[:, :], in_=dram["b1q"].ap()), 16)
                    s_wld.inc(eng.dma_start(out=b1ks[:, :], in_=dram["b1k"].ap()), 16)
                    s_wld.inc(eng.dma_start(out=b1ins[:, :], in_=dram["b1in"].ap()), 16)
                    s_wld.inc(eng.dma_start(out=bvrow[:, :], in_=dram["b1vrow"].ap()), 16)
                    s_wld.inc(eng.dma_start(out=bd1s[:, :], in_=dram["bd1"].ap()), 16)
                    src2 = dram["bd2"].ap().rearrange("(c p) n -> p c n", p=128)
                    d2b = bd2s[:, :].rearrange("p (c n) -> p c n", c=NCH)
                    s_wld.inc(eng.dma_start(out=d2b, in_=src2), 16)
                    s_wld.inc(eng.dma_start(out=w1os[:, :], in_=dram["w1out"].ap()), 16)
                    s_wld.inc(eng.dma_start(out=b1fs[:, :], in_=dram["b1f"].ap()), 16)
                    s_wld.inc(eng.dma_start(out=b1b2s[:, :], in_=dram["b1b2"].ap()), 16)
                    s_wld.inc(eng.dma_start(out=b1outs[:, :], in_=dram["b1out"].ap()), 16)
                    chains[en].flush(eng)

                @blk.vector
                def _(eng):
                    en = "vector"
                    eng = CE(eng, chains[en])
                    bwait(eng, en)
                    s_init.inc(eng.memset(ones128[:, :], 1.0))
                    s_init.inc(eng.memset(onesrow[:, :], 1.0))
                    s_init.inc(eng.memset(onesrowb[:, :], 1.0))
                    for h in range(2):
                        s_mmbv.wait(eng, en, h + 1)
                        inst = eng.tensor_copy(bvbc[:, h * 512:(h + 1) * 512], psP[h][:, :])
                        s_evbv.inc(inst)
                    for c in range(NCH):
                        s_mmvw.wait(eng, en, c + 1)
                        inst = eng.tensor_tensor(vselw[:, c * 128:(c + 1) * 128],
                                                 hbc[:, c * 128:(c + 1) * 128],
                                                 psP[c % 2][0:16, 0:128], ALU.mult)
                        s_vw.inc(inst)

                    def v_x1chain(j):
                        s_ld.wait_done(eng, en, j)
                        s_ldg.wait_done(eng, en, j)
                        s_x1st.wait_done(eng, en, j - 2)
                        s_cast.wait(eng, en, 16 * (j - 1))
                        for c in range(NCH):
                            s_mmP.wait(eng, en, 8 * j + c + 1)
                            inst = eng.tensor_tensor(t1scr[:, :], psP[c % 2][:, :], g0s(j, c),
                                                     ALU.mult)
                            s_sa.inc(inst)
                            inst = eng.tensor_tensor(x1s(j, c), t1scr[:, :], xb(j, c), ALU.add)
                        s_x1.inc(inst)

                    def v_rows(j):
                        s_mmstat.wait(eng, en, 8 * (j + 1))
                        s_bc.wait(eng, en, j)
                        eng.tensor_scalar(row(mrow, j), psLN1[0:1, :], 1.0 / D, None, ALU.mult)
                        eng.tensor_scalar(row(exrow, j), psLN2[0:1, :], 1.0 / D, None, ALU.mult)
                        eng.tensor_tensor(row(vrow, j), row(mrow, j), row(mrow, j), ALU.mult)
                        eng.tensor_tensor(row(vrow, j), row(exrow, j), row(vrow, j), ALU.subtract)
                        inst = eng.tensor_scalar(row(vrow, j), row(vrow, j), EPS, None, ALU.add)
                        s_rowd.inc(inst)
                        s_sqr.wait(eng, en, j + 1)
                        eng.reciprocal(row(vrow, j), srow[:, :])
                        inst = eng.tensor_copy(row(rrow, j), row(vrow, j))
                        s_rowg.inc(inst)

                    def v_xn1(j):
                        s_bc.wait(eng, en, j + 1)
                        s_mmW1.wait(eng, en, j - 1)
                        for c in range(NCH):
                            eng.tensor_tensor(tmpscr[:, :], x1s(j, c), psLN1[:, :], ALU.subtract)
                            inst = eng.tensor_tensor(xn1s(j, c), tmpscr[:, :],
                                                     psLN2[:, :], ALU.mult)
                        s_xn1.inc(inst)

                    def v_xn1_chunk(j, c):
                        if c == 0:
                            s_bc.wait(eng, en, j + 1)
                            s_mmW1.wait(eng, en, j - 1)
                        eng.tensor_tensor(tmpscr[:, :], x1s(j, c), psLN1[:, :], ALU.subtract)
                        inst = eng.tensor_tensor(xn1s(j, c), tmpscr[:, :],
                                                 psLN2[:, :], ALU.mult)
                        if c == NCH - 1:
                            s_xn1.inc(inst)

                    XNDIST = {3: [0], 4: [1], 5: [2], 6: [3], 7: [4, 5, 6, 7]}
                    v_x1chain(0)
                    v_rows(0)
                    v_xn1(0)
                    for i in range(NT):
                        if i + 1 < NT:
                            v_x1chain(i + 1)
                            v_rows(i + 1)
                        s_vst.wait_done(eng, en, i - 1)
                        for g in range(4):
                            for eh in range(2):
                                n = 2 * g + eh
                                s_mmV.wait(eng, en, 8 * i + n + 1)
                                inst = eng.tensor_tensor(
                                    vstage[:, g * D + eh * 512:g * D + eh * 512 + 512],
                                    psV[:, :], bvbc[:, eh * 512:(eh + 1) * 512], ALU.add)
                                s_vev.inc(inst)
                                if i + 1 < NT:
                                    for c in XNDIST.get(n, []):
                                        v_xn1_chunk(i + 1, c)
                        if i >= 1:
                            k = i - 1
                            s_gst.wait_done(eng, en, k - 1)
                            for c in range(NCH):
                                s_sg.wait(eng, en, 8 * k + c + 1)
                                eng.tensor_scalar(fscr[:, :], psP[0][:, :], b1outs[:, c:c + 1],
                                                  None, ALU.add)
                                inst = eng.tensor_tensor(gstage[:, c * FT:(c + 1) * FT],
                                                         fscr[:, :],
                                                         sgscr[:, (c % 2) * FT:(c % 2 + 1) * FT],
                                                         ALU.mult)
                                s_ep.inc(inst)
                        s_mmW1.wait(eng, en, i + 1)
                        s_mmE.wait(eng, en, 8 * (i - 1))
                        inst = eng.tensor_scalar(row(s1stage, i), psW1[:, :], b1ins[:, :],
                                                 None, ALU.add)
                        s_s1ev.inc(inst)
                    k = NT - 1
                    s_gst.wait_done(eng, en, k - 1)
                    for c in range(NCH):
                        s_sg.wait(eng, en, 8 * k + c + 1)
                        eng.tensor_scalar(fscr[:, :], psP[0][:, :], b1outs[:, c:c + 1],
                                          None, ALU.add)
                        inst = eng.tensor_tensor(gstage[:, c * FT:(c + 1) * FT], fscr[:, :],
                                                 sgscr[:, (c % 2) * FT:(c % 2 + 1) * FT],
                                                 ALU.mult)
                        s_ep.inc(inst)
                    chains[en].flush(eng)

                @blk.scalar
                def _(eng):
                    en = "scalar"
                    eng = CE(eng, chains[en])
                    bwait(eng, en)

                    def s_cast1(j, c):
                        if c == 0:
                            s_x1.wait(eng, en, j + 1)
                        s_mmstat.wait(eng, en, 8 * j + c - 4)
                        sl = ((8 * j + c) % 5) * FT
                        s_cast.inc(eng.copy(x1bf[:, sl:sl + FT], x1s(j, c)))
                        s_cast.inc(eng.activation(out=sqbf[:, sl:sl + FT],
                                                  in_=x1s(j, c), func=AF.Square))

                    def s_casts(j):
                        for c in range(NCH):
                            s_cast1(j, c)

                    def s_sqrt(j):
                        s_rowd.wait(eng, en, j + 1)
                        s_sqr.inc(eng.activation(out=srow[:, :], in_=row(vrow, j), func=AF.Sqrt))

                    def s_ffn_relu(k, cs):
                        for c in cs:
                            s_mmE.wait(eng, en, 8 * k + c + 1)
                            s_mmG.wait(eng, en, 8 * k + c - 1)
                            inst = eng.activation(out=hstage[:, (c % 2) * FT:(c % 2 + 1) * FT],
                                                  in_=psP[0][:, :], func=AF.Relu,
                                                  bias=b1fs[:, c:c + 1], scale=1.0)
                            s_h.inc(inst)

                    def s_ffn_g1(k):
                        s_mmG.wait(eng, en, 8 * (k + 1))
                        s_mmW.wait(eng, en, 8 * k)
                        inst = eng.activation(out=g1sb[:, :], in_=psP[1][0:64, :], func=AF.Identity,
                                              bias=b1b2s[:, :], scale=1.0)
                        s_g1.inc(inst)

                    def s_ffn(k):
                        s_ffn_relu(k, range(NCH))
                        s_ffn_g1(k)

                    def s_sgs(k):
                        for c in range(NCH):
                            s_mmW.wait(eng, en, 8 * k + c + 1)
                            s_ep.wait(eng, en, 8 * k + c - 1)
                            inst = eng.activation(out=sgscr[:, (c % 2) * FT:(c % 2 + 1) * FT],
                                                  in_=psP[0][:, :], func=AF.Sigmoid,
                                                  bias=b1outs[:, c:c + 1], scale=1.0)
                            s_sg.inc(inst)

                    s_casts(0)
                    s_sqrt(0)
                    for i in range(NT):
                        for o in range(NCH):
                            s_mmQ.wait(eng, en, 8 * i + o + 1)
                            s_qst.wait_done(eng, en, 2 * i + o // 4 - 1)
                            inst = eng.activation(out=qstage[:, (o % 4) * FT:(o % 4 + 1) * FT],
                                                  in_=psF[o % 2][:, :], func=AF.Identity,
                                                  bias=b1qs[:, o:o + 1], scale=1.0)
                            s_qev.inc(inst)
                        for o in range(NCH):
                            s_mmK.wait(eng, en, 8 * i + o + 1)
                            s_kst.wait_done(eng, en, 2 * i + o // 4 - 1)
                            inst = eng.activation(out=kstage[:, (o % 4) * FT:(o % 4 + 1) * FT],
                                                  in_=psF[o % 2][:, :], func=AF.Identity,
                                                  bias=b1ks[:, o:o + 1], scale=1.0)
                            s_kev.inc(inst)
                            if i + 1 < NT and o < 5:
                                s_cast1(i + 1, o)
                        if i + 1 < NT:
                            s_cast1(i + 1, 5)
                            s_cast1(i + 1, 6)
                            s_cast1(i + 1, 7)
                        if i >= 1:
                            s_ffn_relu(i - 1, range(0, 4))
                        if i + 1 < NT:
                            s_sqrt(i + 1)
                        if i >= 1:
                            s_ffn_relu(i - 1, range(4, NCH))
                            s_ffn_g1(i - 1)
                            s_sgs(i - 1)
                    s_ffn(NT - 1)
                    s_sgs(NT - 1)
                    chains[en].flush(eng)

                @blk.tensor
                def _(eng):
                    en = "tensor"
                    bwait(eng, en)
                    s_wld.wait(eng, en, 16 * 15)
                    s_init.wait(eng, en, 3)
                    for h in range(2):
                        inst = eng.matmul(psP[h][:, :], onesrowb[:, :],
                                          bvrow[:, h * 512:(h + 1) * 512], start=True, stop=True)
                        s_mmbv.inc(inst)
                    for c in range(NCH):
                        if c < 2:
                            s_evbv.wait(eng, en, c + 1)
                        else:
                            s_vw.wait(eng, en, c - 1)
                        inst = eng.matmul(psP[c % 2][0:16, 0:128], onesrowb[0:1, 0:16],
                                          vsumt[0:1, c * 128:(c + 1) * 128], start=True, stop=True)
                        s_mmvw.inc(inst)

                    def t_bcastP(j):
                        s_ldp.wait_done(eng, en, j)
                        s_ep.wait(eng, en, 8 * (j - 2))
                        s_g1.wait(eng, en, j - 2)
                        for c in range(NCH):
                            if j == 0:
                                s_vw.wait(eng, en, 8)
                            s_sa.wait(eng, en, 8 * j + c - 1)
                            inst = eng.matmul(psP[c % 2][:, :], vselw[:, c * 128:(c + 1) * 128],
                                              pbuf[:, (j % 2) * FT:(j % 2 + 1) * FT],
                                              start=True, stop=True)
                            s_mmP.inc(inst)

                    def t_stats(j):
                        s_xn1.wait(eng, en, j)
                        for c in range(NCH):
                            s_cast.wait(eng, en, 16 * j + 2 * (c + 1))
                            sl = ((8 * j + c) % 5) * FT
                            eng.matmul(psLN1[0:1, :], ones128[:, :], x1bf[:, sl:sl + FT],
                                       start=(c == 0), stop=(c == NCH - 1))
                            inst = eng.matmul(psLN2[0:1, :], ones128[:, :], sqbf[:, sl:sl + FT],
                                              start=(c == 0), stop=(c == NCH - 1))
                            s_mmstat.inc(inst)

                    def t_lnbcast(j):
                        s_rowg.wait(eng, en, j + 1)
                        eng.matmul(psLN1[:, :], onesrowb[:, :], row(mrow, j), start=True, stop=True)
                        inst = eng.matmul(psLN2[:, :], onesrowb[:, :], row(rrow, j),
                                          start=True, stop=True)
                        s_bc.inc(inst)

                    def t_ffn(k):
                        s_s1ev.wait(eng, en, k + 1)
                        for c in range(NCH):
                            if c == 0:
                                s_sa.wait(eng, en, min(8 * (k + 3), 8 * NT))
                                s_ep.wait(eng, en, 8 * k)
                            else:
                                s_h.wait(eng, en, 8 * k + c)
                            inst = eng.matmul(psP[0][:, :], bd1s[:, c * 128:(c + 1) * 128],
                                              s1stage[:, (k % 2) * FT:(k % 2 + 1) * FT],
                                              start=True, stop=True)
                            s_mmE.inc(inst)
                            if c >= 1:
                                cc = c - 1
                                s_h.wait(eng, en, 8 * k + cc + 1)
                                if cc == 0:
                                    s_g1.wait(eng, en, k)
                                inst = eng.matmul(psP[1][0:64, :], bd2s[:, cc * 64:(cc + 1) * 64],
                                                  hstage[:, (cc % 2) * FT:(cc % 2 + 1) * FT],
                                                  start=(cc == 0), stop=False)
                                s_mmG.inc(inst)
                        cc = NCH - 1
                        s_h.wait(eng, en, 8 * k + cc + 1)
                        inst = eng.matmul(psP[1][0:64, :], bd2s[:, cc * 64:(cc + 1) * 64],
                                          hstage[:, (cc % 2) * FT:(cc % 2 + 1) * FT],
                                          start=False, stop=True)
                        s_mmG.inc(inst)

                    def t_wout(k):
                        s_g1.wait(eng, en, k + 1)
                        for c in range(NCH):
                            if c >= 1:
                                s_ep.wait(eng, en, 8 * k + c)
                            inst = eng.matmul(psP[0][:, :], w1os[:, c * 128:(c + 1) * 128],
                                              g1sb[:, :], start=True, stop=True)
                            s_mmW.inc(inst)

                    t_bcastP(0)
                    t_stats(0)
                    t_lnbcast(0)
                    for i in range(NT):
                        s_xn1.wait(eng, en, i + 1)
                        for o in range(NCH):
                            if o < 2:
                                s_kev.wait(eng, en, 8 * (i - 1) + o + 7)
                            else:
                                s_qev.wait(eng, en, 8 * i + o - 1)
                            for c in range(NCH):
                                inst = eng.matmul(psF[o % 2][:, :], wsl(w1q, c, o),
                                                  xn1s(i, c),
                                                  start=(c == 0), stop=(c == NCH - 1))
                            s_mmQ.inc(inst)
                        if i + 1 < NT:
                            t_bcastP(i + 1)
                        for o in range(NCH):
                            if o == 0:
                                s_qev.wait(eng, en, 8 * i + 7)
                            elif o == 1:
                                s_qev.wait(eng, en, 8 * i + 8)
                            else:
                                s_kev.wait(eng, en, 8 * i + o - 1)
                            for c in range(NCH):
                                inst = eng.matmul(psF[o % 2][:, :], wsl(w1k, c, o),
                                                  xn1s(i, c),
                                                  start=(c == 0), stop=(c == NCH - 1))
                            s_mmK.inc(inst)
                        if i + 1 < NT:
                            t_stats(i + 1)
                        if i >= 1:
                            t_ffn(i - 1)
                        for g in range(4):
                            for eh in range(2):
                                n = 2 * g + eh
                                s_vev.wait(eng, en, 8 * i + n)
                                for c in range(NCH):
                                    base = ((i % 2) * NCH + c) * FT
                                    lhs = xn1[:, base + g * 128:base + (g + 1) * 128]
                                    rhs = w1v[:, c * D + eh * 512:c * D + eh * 512 + 512]
                                    inst = eng.matmul(psV[:, :], lhs, rhs,
                                                      start=(c == 0), stop=(c == NCH - 1))
                                s_mmV.inc(inst)
                                if n == 2 and i + 1 < NT:
                                    t_lnbcast(i + 1)
                        if i >= 1:
                            t_wout(i - 1)
                        s_s1ev.wait(eng, en, i)
                        for c in range(NCH):
                            inst = eng.matmul(psW1[:, :], w1i[:, c * 64:(c + 1) * 64],
                                              xn1s(i, c),
                                              start=(c == 0), stop=(c == NCH - 1))
                        s_mmW1.inc(inst)
                    t_ffn(NT - 1)
                    t_wout(NT - 1)

                @blk.sync
                def _(eng):
                    en = "sync"
                    bwait(eng, en)
                    for i in range(min(2, NT)):
                        s_ld.inc(chunked_load(eng, xbuf, (i % 2) * NCH * FT, xT, i * FT, FT), i)
                        s_ldp.inc(eng.dma_start(
                            out=pbuf[:, (i % 2) * FT:(i % 2 + 1) * FT],
                            in_=dram["probsH"].ap()[:, i * FT:(i + 1) * FT]), i)
                    s_ldg.inc(chunked_load(eng, g0buf, 0, g0T, 0, FT), 0)
                    for i in range(NT):
                        if i + 1 < NT:
                            s_x1.wait(eng, en, i + 1)
                            s_ldg.inc(chunked_load(eng, g0buf, 0, g0T, (i + 1) * FT, FT), i + 1)
                        s_x1.wait(eng, en, i + 1)
                        s_x1st.inc(chunked_store(eng, x1T, i * FT, FT, x1buf, (i % 2) * NCH * FT), i)
                        for h in range(2):
                            s_qev.wait(eng, en, 8 * i + 4 * (h + 1))
                            dst = qT.ap()[h * 512:(h + 1) * 512, i * FT:(i + 1) * FT] \
                                .rearrange("(c p) n -> p c n", p=128)
                            srcq = qstage[:, :].rearrange("p (c n) -> p c n", c=4)
                            s_qst.inc(eng.dma_start(out=dst, in_=srcq), 2 * i + h)
                            s_kev.wait(eng, en, 8 * i + 4 * (h + 1))
                            dstk = kT.ap()[h * 512:(h + 1) * 512, i * FT:(i + 1) * FT] \
                                .rearrange("(c p) n -> p c n", p=128)
                            srck = kstage[:, :].rearrange("p (c n) -> p c n", c=4)
                            s_kst.inc(eng.dma_start(out=dstk, in_=srck), 2 * i + h)
                        s_vev.wait(eng, en, 8 * (i + 1))
                        dst = v1.ap()[i * FT:(i + 1) * FT, :].rearrange("(g p) e -> p g e", p=128)
                        srcap = vstage[:, :].rearrange("p (g e) -> p g e", g=4)
                        s_vst.inc(eng.dma_start(out=dst, in_=srcap), i)
                        if i >= 1:
                            s_ep.wait(eng, en, 8 * i)
                            s_gst.inc(chunked_store(eng, gffT, (i - 1) * FT, FT, gstage, 0), i - 1)
                        if i + 2 < NT:
                            s_x1.wait(eng, en, i + 1)
                            s_ld.inc(chunked_load(eng, xbuf, (i % 2) * NCH * FT, xT,
                                                  (i + 2) * FT, FT), i + 2)
                            s_mmP.wait(eng, en, 8 * (i + 1))
                            s_ldp.inc(eng.dma_start(
                                out=pbuf[:, (i % 2) * FT:(i % 2 + 1) * FT],
                                in_=dram["probsH"].ap()[:, (i + 2) * FT:(i + 3) * FT]), i + 2)
                    s_ep.wait(eng, en, 8 * NT)
                    s_gst.inc(chunked_store(eng, gffT, (NT - 1) * FT, FT, gstage, 0), NT - 1)
                    s_x1st.wait_all(eng, en, NT)
                    s_qst.wait_all(eng, en, 2 * NT)
                    s_kst.wait_all(eng, en, 2 * NT)
                    s_vst.wait_all(eng, en, NT)
                    s_gst.wait_all(eng, en, NT)

            assert s_mmP.count == 8 * NT
            assert s_vev.count == 8 * NT
        return chains



def build_B(nc, E, dram, bwait, pool):
    x1T, qT, kT, v1, gffT, yT = (dram["x1T"], dram["qT"], dram["kT"], dram["v1"],
                                 dram["gffT"], dram["yT"])
    ctx = ExitStack()
    A = ctx.enter_context
    with ctx:
        qbuf = A(nc.sbuf_tensor("b_qbuf", [128, 2 * NCH * NQ], BF16))
        kbuf = A(nc.sbuf_tensor("b_kbuf", [128, 2 * NCH * KV], BF16))
        vring = A(nc.sbuf_tensor("b_vring", [128, SEG * D], BF16))
        pring = A(nc.sbuf_tensor("b_pring", [128, SEG * NQ], BF16))
        oacc = A(nc.sbuf_tensor("b_oacc", [128, 2 * NCH * NQ], F32))
        gffpan = A(nc.sbuf_tensor("b_gffpan", [128, 2 * NCH * NQ], BF16))
        x1pan = A(nc.sbuf_tensor("b_x1pan", [128, NCH * NQ], F32))
        outsb = A(nc.sbuf_tensor("b_outsb", [128, NCH * NQ], F32))
        rden = A(nc.sbuf_tensor("b_rden", [1, NQ], F32))
        maskbf = A(nc.sbuf_tensor("b_maskbf", [128, 4 * NQ], BF16))
        onescol = A(nc.sbuf_tensor("b_onescol", [128, 1], BF16))
        onesrow = A(nc.sbuf_tensor("b_onesrow", [1, 128], F32))

        psS = [A(nc.psum_tensor(f"b_psS{k}", [128, 512], F32)) for k in range(2)]
        psD = A(nc.psum_tensor("b_psD", [1, 512], F32))
        psO = [A(nc.psum_tensor(f"b_psO{k}", [128, 512], F32)) for k in range(4)]
        psBC = A(nc.psum_tensor("b_psBC", [128, 512], F32))

        s_k = SemGroup(nc, E, "b_k", 2)
        s_v = SemGroup(nc, E, "b_v", SEG)
        s_qld = SemGroup(nc, E, "b_qld", 2)
        s_x1ld = SemGroup(nc, E, "b_x1ld", 2)
        s_gld = SemGroup(nc, E, "b_gld", 2)
        s_wld = Sem(nc, E, "b_wld")
        s_yst = Sem(nc, E, "b_yst")
        s_init = pool.get()
        s_mmS = pool.get()
        s_exp = pool.get()
        s_msk = pool.get()
        s_tb = pool.get()
        s_swp = pool.get()
        s_oev = pool.get()
        s_mmD = pool.get()
        s_recD = pool.get()
        s_mmBc = pool.get()
        s_sat = pool.get()
        s_out = pool.get()
        chains = {"vector": Chain(pool.get(), "b_chv"), "scalar": Chain(pool.get(), "b_chs"),
                  "gpsimd": Chain(pool.get(), "b_chg")}

        def kslice(jg, c):
            o = (jg % 2) * NCH * KV + c * KV
            return kbuf[:, o:o + KV]

        def qslice(p, c):
            o = (p % 2) * NCH * NQ + c * NQ
            return qbuf[:, o:o + NQ]

        def vslice(jg, c):
            o = (jg % SEG) * D + c * 128
            return vring[:, o:o + 128]

        def pslice(jg):
            o = (jg % SEG) * NQ
            return pring[:, o:o + NQ]

        def oslice(p, c):
            o = ((p % 2) * NCH + c) * NQ
            return oacc[:, o:o + NQ]

        def gslice(p, c):
            o = ((p % 2) * NCH + c) * NQ
            return gffpan[:, o:o + NQ]

        def is_masked(p, j):
            return j >= tiles_in_panel(p) - 4

        nmask_before = {}
        cnt = 0
        for p in range(NP):
            for j in range(tiles_in_panel(p)):
                nmask_before[(p, j)] = cnt
                if is_masked(p, j):
                    cnt += 1

        def seg_tiles(p, s):
            J = tiles_in_panel(p)
            return list(range(s * SEG, min(J, (s + 1) * SEG)))

        with nc.Block() as blk:
            @blk.gpsimd
            def _(eng):
                en = "gpsimd"
                eng = CE(eng, chains[en])
                bwait(eng, en)
                s_wld.inc(eng.dma_start(out=maskbf[:, :], in_=dram["masks"].ap()), 16)
                for p in range(min(2, NP)):
                    s_qld.inc(chunked_load(eng, qbuf, (p % 2) * NCH * NQ, qT, p * NQ, NQ), p)
                    s_gld.inc(chunked_load(eng, gffpan, (p % 2) * NCH * NQ, gffT, p * NQ, NQ), p)
                s_x1ld.inc(chunked_load(eng, x1pan, 0, x1T, 0, NQ), 0)
                for p in range(NP):
                    s_out.wait(eng, en, p + 1)
                    s_yst.inc(chunked_store(eng, yT, p * NQ, NQ, outsb, 0), 16)
                    if p + 1 < NP:
                        s_x1ld.inc(chunked_load(eng, x1pan, 0, x1T, (p + 1) * NQ, NQ), p + 1)
                    if p + 2 < NP:
                        s_mmS.wait(eng, en, tiles_before_panel(p + 1))
                        s_qld.inc(chunked_load(eng, qbuf, ((p + 2) % 2) * NCH * NQ, qT,
                                               (p + 2) * NQ, NQ), p + 2)
                        s_gld.inc(chunked_load(eng, gffpan, ((p + 2) % 2) * NCH * NQ, gffT,
                                               (p + 2) * NQ, NQ), p + 2)
                s_yst.wait(eng, en, 16 * NP)

            @blk.sync
            def _(eng):
                en = "sync"
                bwait(eng, en)
                for p in range(NP):
                    jg0 = tiles_before_panel(p)
                    for j in range(tiles_in_panel(p)):
                        jg = jg0 + j
                        kv0 = j * KV
                        s_mmS.wait(eng, en, jg - 1)
                        srcap = kT.ap()[:, kv0:kv0 + KV].rearrange("(c p) n -> p c n", p=128)
                        dst = kbuf[:, (jg % 2) * NCH * KV:((jg % 2) + 1) * NCH * KV].rearrange(
                            "p (c n) -> p c n", c=NCH)
                        s_k.inc(eng.dma_start(out=dst, in_=srcap), jg)
                        s_tb.wait(eng, en, jg - (SEG - 1))
                        s_v.inc(eng.dma_start(out=vring[:, (jg % SEG) * D:((jg % SEG) + 1) * D],
                                              in_=v1.ap()[kv0:kv0 + KV, :]), jg)

            @blk.tensor
            def _(eng):
                en = "tensor"
                bwait(eng, en)
                s_init.wait(eng, en, 2)
                sg_global = 0
                for p in range(NP):
                    jg0 = tiles_before_panel(p)
                    J = tiles_in_panel(p)
                    nseg = segs_in_panel(p)
                    s_qld.wait_done(eng, en, p)

                    def emit_S(j):
                        jg = jg0 + j
                        s_k.wait_done(eng, en, jg)
                        s_exp.wait(eng, en, jg - 1)
                        for c in range(NCH):
                            inst = eng.matmul(psS[jg % 2][:, :], kslice(jg, c), qslice(p, c),
                                              start=(c == 0), stop=(c == NCH - 1))
                        s_mmS.inc(inst)

                    for s in range(nseg):
                        sg = sg_global
                        tiles = seg_tiles(p, s)
                        if s == 0:
                            emit_S(tiles[0])
                        for idx, j in enumerate(tiles):
                            jg = jg0 + j
                            if idx + 1 < len(tiles):
                                emit_S(tiles[idx + 1])
                            elif (s + 1) * SEG < J:
                                emit_S((s + 1) * SEG)
                            if is_masked(p, j):
                                s_msk.wait(eng, en, nmask_before[(p, j)] + 1)
                            else:
                                s_exp.wait(eng, en, jg + 1)
                            if j == 0:
                                s_recD.wait(eng, en, p)
                            inst = eng.matmul(psD[:, :], onescol[:, :], pslice(jg),
                                              start=(j == 0), stop=(j == J - 1))
                            if j == J - 1:
                                s_mmD.inc(inst)
                            s_v.wait_done(eng, en, jg)
                            if idx == 0:
                                s_oev.wait(eng, en, 8 * sg)
                            for c in range(4):
                                inst = eng.matmul(psO[c][:, :], vslice(jg, c), pslice(jg),
                                                  start=(idx == 0), stop=(idx == len(tiles) - 1))
                            if idx == len(tiles) - 1:
                                s_swp.inc(inst)
                        s_oev.wait(eng, en, 8 * sg + 4)
                        for idx, j in enumerate(tiles):
                            jg = jg0 + j
                            for c in range(4, 8):
                                inst = eng.matmul(psO[c - 4][:, :], vslice(jg, c), pslice(jg),
                                                  start=(idx == 0), stop=(idx == len(tiles) - 1))
                            s_tb.inc(inst)
                        sg_global += 1
                    s_recD.wait(eng, en, p + 1)
                    s_sat.wait(eng, en, p)
                    inst = eng.matmul(psBC[:, :], onesrow[:, :], rden[:, :], start=True, stop=True)
                    s_mmBc.inc(inst)

            @blk.scalar
            def _(eng):
                en = "scalar"
                eng = CE(eng, chains[en])
                bwait(eng, en)
                s_wld.wait(eng, en, 16)
                for p in range(NP):
                    jg0 = tiles_before_panel(p)
                    for j in range(tiles_in_panel(p)):
                        jg = jg0 + j
                        s_mmS.wait(eng, en, jg + 1)
                        s_tb.wait(eng, en, jg - (SEG - 1))
                        inst = eng.activation(out=pslice(jg), in_=psS[jg % 2][:, :],
                                              func=AF.Exp, scale=ISQ)
                        s_exp.inc(inst)

            @blk.vector
            def _(eng):
                en = "vector"
                eng = CE(eng, chains[en])
                bwait(eng, en)
                s_wld.wait(eng, en, 16)
                s_init.inc(eng.memset(onescol[:, :], 1.0))
                s_init.inc(eng.memset(onesrow[:, :], 1.0))
                sg_global = 0
                for p in range(NP):
                    jg0 = tiles_before_panel(p)
                    J = tiles_in_panel(p)
                    nseg = segs_in_panel(p)
                    for s in range(nseg):
                        sg = sg_global
                        tiles = seg_tiles(p, s)
                        for j in tiles:
                            if is_masked(p, j):
                                jg = jg0 + j
                                jd = j - (J - 4)
                                s_exp.wait(eng, en, jg + 1)
                                inst = eng.tensor_tensor(pslice(jg), pslice(jg),
                                                         maskbf[:, jd * NQ:(jd + 1) * NQ],
                                                         ALU.mult)
                                s_msk.inc(inst)
                        s_swp.wait(eng, en, sg + 1)
                        for c in range(4):
                            dst = oslice(p, c)
                            if s == 0:
                                inst = eng.tensor_copy(dst, psO[c][:, :])
                            else:
                                inst = eng.tensor_tensor(dst, dst, psO[c][:, :], ALU.add)
                            s_oev.inc(inst)
                        s_tb.wait(eng, en, jg0 + tiles[-1] + 1)
                        for c in range(4, 8):
                            dst = oslice(p, c)
                            if s == 0:
                                inst = eng.tensor_copy(dst, psO[c - 4][:, :])
                            else:
                                inst = eng.tensor_tensor(dst, dst, psO[c - 4][:, :], ALU.add)
                            s_oev.inc(inst)
                        sg_global += 1
                    s_mmD.wait(eng, en, p + 1)
                    inst = eng.reciprocal(rden[:, :], psD[0:1, 0:NQ])
                    s_recD.inc(inst)
                    s_mmBc.wait(eng, en, p + 1)
                    s_x1ld.wait_done(eng, en, p)
                    s_gld.wait_done(eng, en, p)
                    s_yst.wait(eng, en, 16 * p)
                    for c in range(NCH):
                        inst = eng.tensor_tensor(oslice(p, c), oslice(p, c), psBC[:, :], ALU.mult)
                        if c == NCH - 1:
                            s_sat.inc(inst)
                        eng.tensor_tensor(outsb[:, c * NQ:(c + 1) * NQ], oslice(p, c),
                                          gslice(p, c), ALU.mult)
                        inst = eng.tensor_tensor(outsb[:, c * NQ:(c + 1) * NQ],
                                                 outsb[:, c * NQ:(c + 1) * NQ],
                                                 x1pan[:, c * NQ:(c + 1) * NQ], ALU.add)
                    s_out.inc(inst)
                chains[en].flush(eng)

        assert s_exp.count == TOTAL_KV
        assert s_out.count == NP
        return chains



def kernel(**inputs):
    from concourse.bass_utils import run_bass_kernel_spmd
    C = prep_common(inputs)
    xTs = prep_percore(inputs)
    nc, input_names = make_program("FUSED")
    in_maps = []
    for c in range(8):
        m = {}
        for n in input_names:
            m[n] = xTs[c] if n == "xT" else C[n]
        in_maps.append(m)
    res = run_bass_kernel_spmd(nc, in_maps, core_ids=list(range(8)))
    out = np.stack([np.asarray(res.results[c]["yT"]).T for c in range(8)])
    return np.ascontiguousarray(out.astype(np.float32))
